# version 20
# baseline (speedup 1.0000x reference)
import numpy as np
import ml_dtypes
import concourse.bass as bass
import concourse.mybir as mybir
from concourse.bass_utils import run_bass_kernel_spmd

F32 = mybir.dt.float32
BF16 = mybir.dt.bfloat16
I32 = mybir.dt.int32
AF = mybir.ActivationFunctionType
OP = mybir.AluOpType
AX = mybir.AxisListType

TOK = 2048
D = 1024
C = 128
HID = 4096
NT = 16
ND = 8
NH = 32
EPS = 1e-5
NEG = -1e30
POS = 1e30

ENGS = ("sync", "gpsimd", "vector", "scalar", "tensor")


class Emitter:
    def __init__(self):
        self.q = {e: [] for e in ENGS}
        self.cnt = {}
        self.sems = {}

    def add_sem(self, name, handle):
        self.sems[name] = handle
        self.cnt[name] = 0

    def on(self, eng, fn, wait=(), inc=None, amt=1):
        tok = None
        if inc is not None:
            self.cnt[inc] += amt
            tok = (inc, self.cnt[inc])
        wl = tuple(t for t in wait if t is not None)
        self.q[eng].append((fn, wl, tok, amt))
        return tok

    def barrier(self):
        return [(s, self.cnt[s]) for s in self.cnt]

    def run(self, eng_name, eng_obj):
        mark = {}
        for fn, waits, tok, amt in self.q[eng_name]:
            seen = {}
            for (s, v) in waits:
                if v > 0 and v > mark.get(s, 0) and seen.get(s, -1) < v:
                    seen[s] = v
            for s, v in seen.items():
                eng_obj.wait_ge(self.sems[s], v)
                mark[s] = v
            if fn is None:
                continue
            inst = fn(eng_obj)
            if tok is not None:
                inst.then_inc(self.sems[tok[0]], amt)


class Bank:

    def __init__(self, ap):
        self.ap = ap
        self.rd = []
        self.wr = []
        self.mode = "r"

    def w(self):
        if self.mode == "r":
            out = list(self.rd)
            self.rd = []
            self.wr = []
            self.mode = "w"
            return out
        return []

    def note_w(self, tok):
        self.wr.append(tok)

    def r(self):
        if self.mode == "w":
            self.mode = "r"
        return list(self.wr)

    def note_r(self, tok):
        self.rd.append(tok)


class Reg:

    def __init__(self, ap=None):
        self.ap = ap
        self.lw = None
        self.rd = []

    def w(self):
        out = [self.lw] + self.rd
        self.rd = []
        return out

    def note_w(self, tok):
        self.lw = tok

    def r(self):
        return [self.lw]

    def note_r(self, tok):
        self.rd.append(tok)


def build_nc(tap=None):
    nc = bass.Bass("TRN2", target_bir_lowering=False)

    def din(name, shp, dt=F32):
        return nc.dram_tensor(name, shp, dt, kind="ExternalInput").ap()

    x_d = din("x", [TOK, D])
    xT_d = din("xT", [D, TOK])
    Wq_d = din("Wq", [D, C]); Wk_d = din("Wk", [D, C])
    Wv_d = din("Wv", [D, C]); Wg_d = din("Wg", [D, C])
    Wab_d = din("Wab", [D, 2])
    Wo_d = din("Wo", [C, D])
    bq_d = din("bq", [C, 1]); bk_d = din("bk", [C, 1])
    bv_d = din("bv", [C, 1]); bg_d = din("bg", [C, 1])
    bab_d = din("bab", [2, 1])
    bo_b_d = din("bo_b", [128, D])
    b2_b_d = din("b2_b", [128, D])
    gam_b_d = din("gam_b", [128, D])
    bet_b_d = din("bet_b", [128, D])
    b1T_d = din("b1T", [128, NH])
    W1T_d = din("W1T", [D, HID])
    W2T_d = din("W2T", [HID, D])
    idf_d = din("idf", [128, 128])
    idb_d = din("idb", [128, 128], BF16)
    maskW_d = din("maskW", [128, 128])
    maskU_d = din("maskU", [128, 128])
    smask_d = din("smask", [128, 6])
    out_d = nc.dram_tensor("out", [TOK, D], F32, kind="ExternalOutput").ap()
    dbg_d = nc.dram_tensor("dbg", [128, 4096], F32, kind="ExternalOutput").ap()

    cc_in = nc.dram_tensor("cc_in", [128, 256], F32).ap()
    cc_out = nc.dram_tensor("cc_out", [512, 256], F32).ap()
    Hb = nc.dram_tensor("Hb", [HID, TOK], BF16).ap()

    from contextlib import ExitStack
    es = ExitStack()
    SB = lambda n, s, d=F32: es.enter_context(nc.sbuf_tensor(n, s, d))
    PS = lambda n, s, d=F32: es.enter_context(nc.psum_tensor(n, s, d))

    with es:
        FA = SB("FA", [128, 12864], F32)

        def fa(off, ln):
            return FA[:, off:off + ln]
        CO = fa(0, 768)
        RW = fa(768, 2048)
        ONES = fa(2816, 128)
        IDF = fa(2944, 128)
        MKW = fa(3072, 128)
        MKU = fa(3200, 128)
        SMK = fa(3328, 6)
        BCOL = fa(3336, 8)
        B1T = fa(3352, 32)
        S1 = fa(3392, 2048)
        WMS = S1
        F1 = fa(5440, 2048)
        RCV = fa(5440, 768)
        CCP = fa(6208, 256)
        ZI = fa(6464, 256)
        GPT = fa(6720, 128)
        HPT = fa(6848, 128)
        F2 = fa(7488, 2048)
        OUT2 = fa(3392, 4096)
        TR1_ = fa(9536, 128)
        TR2_ = fa(9664, 128)
        YTL = fa(9792, 2048)
        W1S = YTL
        W2S = fa(11840, 1024)
        OSTG = fa(11840, 1024)

        BA = SB("BA", [128, 79104], BF16)

        def ba(off, ln):
            return BA[:, off:off + ln]
        R1 = ba(0, NT * D)
        R2 = ba(16384, ND * TOK)
        AT1o = 32768
        qT = ba(AT1o + 0, 2048)
        kT = ba(AT1o + 2048, 2048)
        vT = ba(AT1o + 4096, 2048)
        gT = ba(AT1o + 6144, 2048)
        UCB = ba(AT1o + 8192, 4096)
        PTBs = ba(AT1o + 12288, 2048)
        ULT = ba(AT1o + 14336, 2048)
        GTB = ba(AT1o + 16384, 2048)
        HCB = ba(AT1o + 18432, 2048)
        HTB = ba(AT1o + 20480, 2048)
        KNB = ba(AT1o + 22528, 2048)
        VNB = ba(AT1o + 24576, 2048)
        KPB = ba(AT1o + 26624, 2048)
        GUP = ba(AT1o + 28672, 2048)
        QTT = ba(AT1o + 30720, 2048)
        W2Q = ba(AT1o, NH * D)
        BF2o = 65536
        WQ = ba(BF2o + 0, 1024)
        WK = ba(BF2o + 1024, 1024)
        WV = ba(BF2o + 2048, 1024)
        WG = ba(BF2o + 3072, 1024)
        WO = ba(BF2o + 4096, 1024)
        WAB = ba(BF2o + 5120, 16)
        GB = ba(BF2o + 0, 2048)
        XQN = ba(BF2o + 2048, 2048)
        HSL = ba(BF2o + 0, 2 * TOK)
        W1Q = ba(BF2o + 4096, 2048)
        BF3o = 71680
        WCH = ba(BF3o + 0, 128)
        XP = ba(BF3o + 128, 256)
        YS = ba(BF3o + 384, 896)
        TPP = ba(BF3o + 1280, 512)
        MPT = ba(BF3o + 1792, 256)
        UFT = ba(BF3o + 2048, 256)
        ZBt = ba(BF3o + 2304, 256)
        ZTt = ba(BF3o + 2560, 256)
        GA = ba(BF3o + 2816, 256)
        GAT = ba(BF3o + 3072, 256)
        HA = ba(BF3o + 3328, 256)
        IDB = ba(BF3o + 3584, 128)
        ONEB = ba(BF3o + 3712, 128)
        GMB = ba(BF3o + 3840, 1024)
        BTB = ba(BF3o + 4864, 1024)
        BOB = ba(BF3o + 5888, 1024)
        B2B = BOB
        SQS = ba(BF2o + 4096, 1024)

        IA = SB("IA", [128, 512], I32)
        ZI32 = IA[:, 0:512]
        ZH32 = IA[:, 0:512]

        pb = [Bank(PS(f"pp{i}", [128, 512], F32)) for i in range(6)]
        btf = Bank(PS("ptf", [128, 512], F32))
        btb = Bank(PS("ptb", [128, 1024], BF16))

        em = Emitter()
        for s in ("dS", "dW0", "dW1", "dX", "dM0", "dM1", "dR", "dG", "dA", "v", "s", "t", "g", "cc", "out"):
            em.add_sem(s, es.enter_context(nc.semaphore(s)))

        def col(i, w=16):
            return CO[:, i * 16: i * 16 + w]

        def do_tap(name, region, width, waits):
            if tap == name:
                em.on("gpsimd", (lambda region=region, width=width: (lambda g: g.dma_start(
                    out=dbg_d[:, 0:width], in_=region)))(),
                    wait=list(waits), inc="dG", amt=16)

        (SUM1, SQ1, MU1, R1C, MR1, V1A, V1B,
         SUM2, SQ2, MU2, R2C, NM2,
         SUM3, SQ3, MU3, RHO, NMR, SXC, ISX,
         S2C, IS2, SUM5, SQ5, MU5, R5C, NM5,
         LCOL, BCOLc, GCOL, BGC, LLB, EKP, GLR,
         WM1, WM2, SW1C, ISW1, SW2C, ISW2,
         TMPA, TMPB, TMPC, WMT) = range(43)
        NCS = 43

        x_r = x_d.rearrange("(t p) d -> p t d", p=128)
        tok_x = em.on("gpsimd", lambda g: g.dma_start(
            out=R1[:, :].rearrange("p (t d) -> p t d", t=NT), in_=x_r), inc="dG", amt=16)
        xT_r = xT_d.rearrange("(k p) t -> p k t", p=128)
        tok_xT = em.on("gpsimd", lambda g: g.dma_start(
            out=R2[:, :].rearrange("p (k t) -> p k t", k=ND), in_=xT_r), inc="dG", amt=16)

        def ldw(dst, src, m):
            src_r = src.rearrange("(k p) c -> p k c", p=128)
            return lambda g: g.dma_start(
                out=dst[:, :].rearrange("p (k c) -> p k c", k=ND), in_=src_r)
        tok_wq = em.on("gpsimd", ldw(WQ, Wq_d, C), inc="dG", amt=16)
        tok_wk = em.on("gpsimd", ldw(WK, Wk_d, C), inc="dG", amt=16)
        tok_wv = em.on("gpsimd", ldw(WV, Wv_d, C), inc="dG", amt=16)
        tok_wg = em.on("gpsimd", ldw(WG, Wg_d, C), inc="dG", amt=16)
        tok_wab = em.on("gpsimd", ldw(WAB, Wab_d, 2), inc="dG", amt=16)
        tok_wo = em.on("gpsimd", lambda g: g.dma_start(out=WO[:, :], in_=Wo_d), inc="dG", amt=16)

        csts = []
        for dst, src in [(IDF, idf_d), (MKW, maskW_d), (MKU, maskU_d),
                         (B1T, b1T_d)]:
            csts.append(em.on("sync", (lambda dst=dst, src=src: (
                lambda sy: sy.dma_start(out=dst[:, :], in_=src)))(), inc="dS", amt=16))
        CIDF, CMKW, CMKU, CB1T = csts
        CBOB = em.on("gpsimd", lambda g: g.dma_start(out=BOB[:, :], in_=bo_b_d), inc="dG", amt=16)
        CGMB = em.on("gpsimd", lambda g: g.dma_start(out=GMB[:, :], in_=gam_b_d), inc="dG", amt=16)
        CBTB = em.on("gpsimd", lambda g: g.dma_start(out=BTB[:, :], in_=bet_b_d), inc="dG", amt=16)
        tok_idb = em.on("sync", lambda sy: sy.dma_start(out=IDB[:, :], in_=idb_d), inc="dS", amt=16)
        tok_smk = em.on("sync", lambda sy: sy.dma_start(out=SMK[:, :], in_=smask_d), inc="dS", amt=16)
        for i, src in enumerate([bq_d, bk_d, bv_d, bg_d]):
            csts.append(em.on("sync", (lambda i=i, src=src: (
                lambda sy: sy.dma_start(out=BCOL[:, i:i + 1], in_=src)))(), inc="dS", amt=16))
        tok_bab = em.on("sync", lambda sy: sy.dma_start(out=BCOL[0:2, 4:5], in_=bab_d), inc="dS", amt=16)
        csts.append(tok_bab)
        tok_cst = csts[-1]

        tok_ones = em.on("vector", lambda v: v.memset(ONES[:, :], 1.0), inc="v")
        tok_epsc = em.on("vector", lambda v: v.memset(BCOL[:, 5:6], EPS), wait=[tok_bab], inc="v")
        EPSC = BCOL[:, 5:6]
        tok_oneb = em.on("vector", lambda v: v.memset(ONEB[:, 0:1], 1.0), inc="v")
        tok_negb = em.on("vector", lambda v: v.memset(ONEB[:, 1:2], -1.0), inc="v")

        rS1 = Reg()
        st_toks = []
        for t in range(NT):
            ts_ = em.on("vector", (lambda t=t: (lambda v: v.reduce_sum(
                col(SUM1)[:, t:t + 1], R1[:, t * D:(t + 1) * D], axis=AX.X)))(),
                wait=[tok_x], inc="v")
            tq = em.on("scalar", (lambda t=t: (lambda s: s.activation(
                S1[:, 0:D], R1[:, t * D:(t + 1) * D], AF.Square,
                accum_out=col(SQ1)[:, t:t + 1])))(), wait=[tok_x] + rS1.w(), inc="s")
            rS1.note_w(tq)
            st_toks += [ts_, tq]
        t_mu1 = em.on("vector", lambda v: v.tensor_scalar(
            col(MU1), col(SUM1), 1.0 / D, None, OP.mult), wait=st_toks, inc="v")
        t_e2 = em.on("vector", lambda v: v.tensor_scalar(
            col(V1A), col(SQ1), 1.0 / D, None, OP.mult), wait=st_toks, inc="v")
        t_m2 = em.on("vector", lambda v: v.tensor_mul(col(V1B), col(MU1), col(MU1)),
                     wait=[t_mu1], inc="v")
        t_v1 = em.on("vector", lambda v: v.tensor_sub(col(V1A), col(V1A), col(V1B)),
                     wait=[t_e2, t_m2], inc="v")
        t_sd = em.on("scalar", lambda s: s.activation(col(V1B), col(V1A), AF.Sqrt, bias=EPSC),
                     wait=[t_v1, tok_epsc], inc="s")
        t_r1 = em.on("vector", lambda v: v.reciprocal(col(R1C), col(V1B)), wait=[t_sd], inc="v")
        t_mr1 = em.on("vector", lambda v: v.tensor_mul(col(MR1), col(MU1), col(R1C)),
                      wait=[t_r1], inc="v")

        rROW0 = Reg()
        rRW16 = Reg()

        def bcast_from_col(col_ap, dstreg, dst_ap, pre_toks):
            wb = btf.w()
            tt = em.on("tensor", (lambda col_ap=col_ap: (lambda t: t.transpose(
                btf.ap[0:16, 0:128], col_ap, IDF[:, :])))(),
                wait=wb + list(pre_toks) + [CIDF], inc="t")
            btf.note_w(tt)
            tv = em.on("vector", lambda v: v.tensor_copy(RW[32:48, 384:512], btf.ap[0:16, 0:128]),
                       wait=btf.r() + rRW16.w(), inc="v")
            btf.note_r(tv)
            rRW16.note_w(tv)
            td = em.on("gpsimd", lambda g: g.dma_start(out=RW[0:1, 0:2048], in_=RW[32:48, 384:512]),
                       wait=rRW16.r() + rROW0.w(), inc="dG", amt=16)
            rRW16.note_r(td)
            rROW0.note_w(td)
            return bcast_from_row(RW[0:1, 0:2048], rROW0, dstreg, dst_ap, [td])

        def bcast_from_row(row_ap, rowreg, dstreg, dst_ap, pre_toks):
            last = None
            wdst = dstreg.w()
            for nchk in range(4):
                wb = pb[5].w()
                tm = em.on("tensor", (lambda nchk=nchk, row_ap=row_ap: (lambda t: t.matmul(
                    pb[5].ap[:, 0:512], ONES[0:1, :], row_ap[0:1, nchk * 512:(nchk + 1) * 512],
                    start=True, stop=True)))(),
                    wait=wb + list(pre_toks) + [tok_ones], inc="t")
                pb[5].note_w(tm)
                te = em.on("vector", (lambda nchk=nchk, dst_ap=dst_ap: (lambda v: v.tensor_copy(
                    dst_ap[:, nchk * 512:(nchk + 1) * 512], pb[5].ap[:, 0:512])))(),
                    wait=pb[5].r() + (wdst if nchk == 0 else []), inc="v")
                pb[5].note_r(te)
                if rowreg is not None:
                    rowreg.note_r(te)
                last = te
            dstreg.note_w(last)
            return last

        rF1 = Reg()
        rF2 = Reg()
        tok_r1b = bcast_from_col(col(R1C), rF1, F1, [t_r1])
        tok_mr1b = bcast_from_col(col(MR1), rF2, F2, [t_mr1])

        def colsum(Wt, slot, M, wtok):
            wb = btf.w()
            mm = []
            for k in range(ND):
                tm = em.on("tensor", (lambda k=k, Wt=Wt, M=M: (lambda t: t.matmul(
                    btf.ap[0:M, 2:3], Wt[:, k * M:(k + 1) * M], ONEB[:, 1:2],
                    start=(k == 0), stop=(k == ND - 1))))(),
                    wait=(wb if k == 0 else []) + [wtok, tok_negb] + (mm[-1:] if mm else []),
                    inc="t")
                mm.append(tm)
            btf.note_w(mm[-1])
            te = em.on("vector", (lambda M=M, slot=slot: (lambda v: v.tensor_copy(
                col(slot)[0:M, 0:1], btf.ap[0:M, 2:3])))(), wait=btf.r(), inc="v")
            btf.note_r(te)
            return te
        tok_csq = colsum(WQ, NCS + 0, C, tok_wq)
        tok_csk = colsum(WK, NCS + 1, C, tok_wk)
        tok_csv = colsum(WV, NCS + 2, C, tok_wv)
        tok_csg = colsum(WG, NCS + 3, C, tok_wg)
        tok_csab = colsum(WAB, NCS + 4, 2, tok_wab)

        rYTL = Reg()

        def project(Wt, M, cstok, cs_slot, bias_idx, act, dst_ap, wtok, dstreg, xw=()):
            wy = rYTL.w()
            ev = []
            pair_mms = {}
            for pair in range(2):
                banks = [pb[2 * pair], pb[2 * pair + 1]]
                wbs = [b.w() for b in banks]
                mms = [[], []]
                for k in range(ND):
                    for j in range(2):
                        nchk = 2 * pair + j
                        tm = em.on("tensor", (lambda k=k, nchk=nchk, bank=banks[j], Wt=Wt, M=M: (lambda t: t.matmul(
                            bank.ap[0:M, 0:512], Wt[:, k * M:(k + 1) * M],
                            R2[:, k * TOK + nchk * 512: k * TOK + (nchk + 1) * 512],
                            start=(k == 0), stop=(k == ND - 1))))(),
                            wait=(wbs[j] if k == 0 else []) + [wtok, tok_xT],
                            inc="t")
                        mms[j].append(tm)
                for j in range(2):
                    banks[j].note_w(mms[j][-1])
                    pair_mms[2 * pair + j] = mms[j]
            for nchk in range(4):
                bank = pb[nchk]
                te1 = em.on("vector", (lambda nchk=nchk, bank=bank, M=M: (lambda v: v.tensor_mul(
                    YTL[0:M, nchk * 512:(nchk + 1) * 512], bank.ap[0:M, 0:512],
                    F1[0:M, nchk * 512:(nchk + 1) * 512])))(),
                    wait=bank.r() + [tok_r1b] + (wy if nchk == 0 else []), inc="v")
                bank.note_r(te1)
                rF1.note_r(te1)
                te2 = em.on("vector", (lambda nchk=nchk, M=M, cs_slot=cs_slot: (lambda v: v.scalar_tensor_tensor(
                    YTL[0:M, nchk * 512:(nchk + 1) * 512],
                    F2[0:M, nchk * 512:(nchk + 1) * 512], col(cs_slot)[0:M, 0:1],
                    YTL[0:M, nchk * 512:(nchk + 1) * 512], OP.mult, OP.add)))(),
                    wait=[te1, tok_mr1b, cstok], inc="v")
                rF2.note_r(te2)
                ev.append(te2)
            tse = em.on("scalar", (lambda M=M, bias_idx=bias_idx, act=act, dst_ap=dst_ap: (lambda s: s.activation(
                dst_ap, YTL[0:M, 0:TOK], act, bias=BCOL[0:M, bias_idx:bias_idx + 1])))(),
                wait=ev + list(xw) + [tok_cst], inc="s")
            rYTL.note_w(tse)
            rYTL.note_r(tse)
            if dstreg is not None:
                dstreg.note_w(tse)
            return tse

        tok_qT = project(WQ, C, tok_csq, NCS + 0, 0, AF.Identity, qT[:, 0:TOK], tok_wq, None)
        tok_kTr = project(WK, C, tok_csk, NCS + 1, 1, AF.Identity, S1[:, 0:TOK], tok_wk, rS1)
        tok_vT = project(WV, C, tok_csv, NCS + 2, 2, AF.Identity, vT[:, 0:TOK], tok_wv, None)
        tok_gT = project(WG, C, tok_csg, NCS + 3, 3, AF.Sigmoid, gT[:, 0:TOK], tok_wg, None)
        tok_ab = project(WAB, 2, tok_csab, NCS + 4, 4, AF.Sigmoid, RW[0:2, 0:TOK],
                         tok_wab, None, xw=rROW0.w())
        rROW0.note_w(tok_ab)

        t_k2 = em.on("scalar", lambda s: s.activation(F2[:, 0:TOK], S1[:, 0:TOK], AF.Square),
                     wait=rF2.w() + rS1.r() + [tok_kTr, tok_ab], inc="s")
        rF2.note_w(t_k2)
        rS1.note_r(t_k2)
        kn_last = None
        for nchk in range(4):
            wb = btf.w()
            tm = em.on("tensor", (lambda nchk=nchk: (lambda t: t.matmul(
                btf.ap[0:1, 0:512], ONES[:, 0:1], F2[:, nchk * 512:(nchk + 1) * 512],
                start=True, stop=True)))(), wait=wb + rF2.r() + [tok_ones], inc="t")
            btf.note_w(tm)
            rF2.note_r(tm)
            te = em.on("vector", (lambda nchk=nchk: (lambda v: v.tensor_copy(
                YTL[0:1, nchk * 512:(nchk + 1) * 512], btf.ap[0:1, 0:512])))(),
                wait=btf.r() + (rYTL.w() if nchk == 0 else []), inc="v")
            btf.note_r(te)
            kn_last = te
        rYTL.note_w(kn_last)
        t_ks = em.on("scalar", lambda s: s.activation(YTL[0:1, 0:TOK], YTL[0:1, 0:TOK], AF.Sqrt),
                     wait=rYTL.w(), inc="s")
        rYTL.note_w(t_ks)
        t_kc = em.on("vector", lambda v: v.tensor_scalar(
            YTL[0:1, 0:TOK], YTL[0:1, 0:TOK], 1e-12, None, OP.max), wait=rYTL.w(), inc="v")
        rYTL.note_w(t_kc)
        t_kr = em.on("vector", lambda v: v.reciprocal(YTL[0:1, 0:TOK], YTL[0:1, 0:TOK]),
                     wait=rYTL.w(), inc="v")
        rYTL.note_w(t_kr)
        tok_rkb = bcast_from_row(YTL[0:1, 0:TOK], rYTL, rF2, F2, rF2.w() + [t_kr])
        tok_kT = em.on("vector", lambda v: v.tensor_mul(kT[:, 0:TOK], S1[:, 0:TOK], F2[:, 0:TOK]),
                       wait=[tok_rkb] + rS1.r(), inc="v")
        rS1.note_r(tok_kT)
        rF2.note_r(tok_kT)
        w1_r = W1T_d.rearrange("(k p) h -> p k h", p=128)
        w2_r = W2T_d.rearrange("(k p) dd -> p k dd", p=128)
        w1_slices = [w1_r[:, k, hh * 1024:(hh + 1) * 1024] for k in range(ND) for hh in range(4)]
        w2_slices = [w2_r[:, i, :] for i in range(NH)]
        rWMb = [Reg(), Reg()]
        wmall = []
        wm_parts = {WM1: [], WM2: []}
        wm_sub = {WM1: (SUM1, SQ1), WM2: (MU1, R1C)}
        wm_work = [(a, WM1) for a in w1_slices] + [(a, WM2) for a in w2_slices]
        wm_j = [0]

        def emit_wmean(n):
            for _ in range(n):
                j = wm_j[0]
                if j >= len(wm_work):
                    return
                wm_j[0] += 1
                src_ap, slot = wm_work[j]
                bb_ = j % 2
                td = em.on("sync" if bb_ == 0 else "scalar",
                    (lambda src_ap=src_ap, bb_=bb_: (lambda sy: sy.dma_start(
                    out=F2[:, bb_ * 1024:(bb_ + 1) * 1024], in_=src_ap)))(),
                    wait=rWMb[bb_].w() + ([tok_kT] if j <= 1 else []),
                    inc="dM0" if bb_ == 0 else "dM1", amt=16)
                rWMb[bb_].note_w(td)
                i2 = len(wm_parts[slot])
                tslot = wm_sub[slot][0] if i2 < 16 else wm_sub[slot][1]
                ta = em.on("scalar", (lambda bb_=bb_, tslot=tslot, i2=i2: (lambda s: s.activation(
                    F2[:, bb_ * 1024:(bb_ + 1) * 1024], F2[:, bb_ * 1024:(bb_ + 1) * 1024],
                    AF.Abs, accum_out=col(tslot)[:, i2 % 16:i2 % 16 + 1])))(),
                    wait=rWMb[bb_].w(), inc="s")
                rWMb[bb_].note_w(ta)
                wm_parts[slot].append(ta)
                if len(wm_parts[slot]) == 32:
                    tm_ = em.on("vector", (lambda slot=slot: (lambda v: v.tensor_add(
                        col(slot), col(wm_sub[slot][0]), col(wm_sub[slot][1]))))(),
                        wait=wm_parts[slot], inc="v")
                    wm_parts[slot] = [tm_]
                    wmall.append(tm_)
        emit_wmean(len(wm_work))
        wm1_toks = [wmall[0]]
        wm2_toks = [wmall[1]]

        t_la = em.on("scalar", lambda s: s.activation(RW[0:1, 0:TOK], RW[0:1, 0:TOK], AF.Ln),
                     wait=[tok_ab] + rROW0.w(), inc="s")
        rROW0.note_w(t_la)
        rL16 = [Reg(), Reg()]
        rB16 = Reg()
        t_l16 = em.on("gpsimd", lambda g: g.dma_start(out=RW[32:48, 0:128], in_=RW[0:1, 0:TOK]),
                      wait=rROW0.r() + rL16[0].w(), inc="dG", amt=16)
        rROW0.note_r(t_l16)
        rL16[0].note_w(t_l16)
        t_b16 = em.on("gpsimd", lambda g: g.dma_start(out=RW[32:48, 256:384], in_=RW[1:2, 0:TOK]),
                      wait=[tok_ab] + rB16.w(), inc="dG", amt=16)
        rB16.note_w(t_b16)
        srco, dsto = 0, 1
        for lvl in range(7):
            sft = 1 << lvl
            w = rL16[srco].r() + rL16[dsto].w()
            t_c1 = em.on("vector", (lambda srco=srco, dsto=dsto, sft=sft: (lambda v: v.tensor_copy(
                RW[32:48, dsto * 128:dsto * 128 + sft], RW[32:48, srco * 128:srco * 128 + sft])))(),
                wait=w, inc="v")
            t_c2 = em.on("vector", (lambda srco=srco, dsto=dsto, sft=sft: (lambda v: v.tensor_add(
                RW[32:48, dsto * 128 + sft:dsto * 128 + 128],
                RW[32:48, srco * 128 + sft:srco * 128 + 128],
                RW[32:48, srco * 128:srco * 128 + 128 - sft])))(), wait=w, inc="v")
            rL16[srco].note_r(t_c2)
            rL16[dsto].note_w(t_c2)
            rL16[dsto].note_w(t_c1)
            rL16[dsto].note_w(t_c2)
            srco, dsto = dsto, srco
        l16o = srco * 128
        tok_l16 = rL16[srco].lw

        wb = btf.w()
        t_lc = em.on("tensor", (lambda l16o=l16o: (lambda t: t.transpose(
            btf.ap[:, 0:16], RW[32:48, l16o:l16o + 128], IDF[32:48, 32:48])))(),
            wait=wb + rL16[srco].r() + [CIDF], inc="t")
        btf.note_w(t_lc)
        rL16[srco].note_r(t_lc)
        t_lcv = em.on("vector", lambda v: v.tensor_copy(col(LCOL), btf.ap[:, 0:16]),
                      wait=btf.r(), inc="v")
        btf.note_r(t_lcv)
        wb = btf.w()
        t_bc_ = em.on("tensor", lambda t: t.transpose(
            btf.ap[:, 16:32], RW[32:48, 256:384], IDF[32:48, 32:48]),
            wait=wb + rB16.r() + [CIDF], inc="t")
        btf.note_w(t_bc_)
        rB16.note_r(t_bc_)
        t_bcv = em.on("vector", lambda v: v.tensor_copy(col(BCOLc), btf.ap[:, 16:32]),
                      wait=btf.r(), inc="v")
        btf.note_r(t_bcv)
        t_gc = em.on("scalar", lambda s: s.activation(col(GCOL), col(LCOL), AF.Exp),
                     wait=[t_lcv], inc="s")
        t_bg = em.on("vector", lambda v: v.tensor_mul(col(BGC), col(BCOLc), col(GCOL)),
                     wait=[t_bcv, t_gc], inc="v")
        rRW4 = Reg()
        t_ll = em.on("gpsimd", (lambda l16o=l16o: (lambda g: g.dma_start(
            out=col(V1A)[0:1, 0:16], in_=RW[32:48, l16o + 127:l16o + 128])))(),
            wait=rL16[srco].r() + rRW4.w(), inc="dG", amt=16)
        rL16[srco].note_r(t_ll)
        rRW4.note_w(t_ll)
        wb = btf.w()
        t_llb = em.on("tensor", lambda t: t.matmul(
            btf.ap[:, 32:48], ONES[0:1, :], col(V1A)[0:1, 0:16], start=True, stop=True),
            wait=wb + rRW4.r() + [tok_ones], inc="t")
        btf.note_w(t_llb)
        rRW4.note_r(t_llb)
        t_llbv = em.on("vector", lambda v: v.tensor_copy(col(LLB), btf.ap[:, 32:48]),
                       wait=btf.r(), inc="v")
        btf.note_r(t_llbv)
        t_ek1 = em.on("vector", lambda v: v.tensor_sub(col(TMPC), col(LLB), col(LCOL)),
                      wait=[t_llbv, t_lcv], inc="v")
        t_ekp = em.on("scalar", lambda s: s.activation(col(EKP), col(TMPC), AF.Exp),
                      wait=[t_ek1], inc="s")
        t_glr = em.on("scalar", lambda s: s.activation(col(GLR), col(LLB), AF.Exp),
                      wait=[t_llbv], inc="s")

        t_lrow = em.on("gpsimd", (lambda l16o=l16o: (lambda g: g.dma_start(
            out=RW[0:1, 0:TOK], in_=RW[32:48, l16o:l16o + 128])))(),
            wait=rL16[srco].r() + rROW0.w(), inc="dG", amt=16)
        rL16[srco].note_r(t_lrow)
        rROW0.note_w(t_lrow)
        tok_ljb = bcast_from_row(RW[0:1, 0:TOK], rROW0, rS1, S1, rS1.w() + [t_lrow])
        tok_gb = em.on("scalar", lambda s: s.activation(GB[:, 0:TOK], S1[:, 0:TOK], AF.Exp),
                       wait=rS1.r(), inc="s")
        rS1.note_r(tok_gb)

        rWCH = Reg(); rXP = [Reg(), Reg()]; rYS = [Reg() for _ in range(7)]
        rTPP = [Reg(), Reg()]; rTR1 = Reg(); rTR2 = Reg()
        tgu_toks = [None] * NT
        tqt_toks = [None] * NT
        hcb_toks = [None] * NT
        htb_toks = [None] * NT
        gtb_toks = [None] * NT
        knb_toks = [None] * NT
        ucb_toks = [None] * NT
        ptb_toks = [None] * NT
        ult_toks = [None] * NT
        ga_tok = []
        gacc_cur = 0
        for c in range(NT):
            cC = slice(c * C, (c + 1) * C)
            wb = btb.w()
            tt1 = em.on("tensor", (lambda c=c: (lambda t: t.transpose(
                btb.ap[:, 0:128], kT[:, c * C:(c + 1) * C], IDB[:, :])))(),
                wait=wb + [tok_kT, tok_idb], inc="t")
            tt2 = em.on("tensor", (lambda c=c: (lambda t: t.transpose(
                btb.ap[:, 128:256], vT[:, c * C:(c + 1) * C], IDB[:, :])))(),
                wait=[tok_vT], inc="t")
            btb.note_w(tt1); btb.note_w(tt2)
            rb = btb.r()
            tv1 = em.on("vector", (lambda c=c: (lambda v: v.tensor_copy(
                KNB[:, c * C:(c + 1) * C], btb.ap[:, 0:128])))(), wait=rb, inc="v")
            tv2 = em.on("vector", (lambda c=c: (lambda v: v.tensor_copy(
                VNB[:, c * C:(c + 1) * C], btb.ap[:, 128:256])))(), wait=rb, inc="v")
            btb.note_r(tv1); btb.note_r(tv2)
            knb_toks[c] = tv1
            tkp = em.on("vector", (lambda c=c: (lambda v: v.tensor_scalar(
                KPB[:, c * C:(c + 1) * C], KNB[:, c * C:(c + 1) * C],
                col(EKP)[:, c:c + 1], None, OP.mult)))(), wait=[tv1, t_ekp], inc="v")
            wb = pb[2].w()
            tkk = em.on("tensor", (lambda c=c: (lambda t: t.matmul(
                pb[2].ap[:, 0:128], kT[:, c * C:(c + 1) * C], kT[:, c * C:(c + 1) * C],
                start=True, stop=True)))(), wait=wb + [tok_kT], inc="t")
            pb[2].note_w(tkk)
            td1 = em.on("vector", (lambda c=c: (lambda v: v.tensor_scalar(
                TR1_[:, :], S1[:, c * C:(c + 1) * C], col(LCOL)[:, c:c + 1], None,
                OP.subtract)))(), wait=rTR1.w() + [tok_ljb, t_lcv], inc="v")
            rS1.note_r(td1)
            rTR1.note_w(td1)
            td2 = em.on("vector", lambda v: v.tensor_add(TR2_[:, :], TR1_[:, :], MKW[:, :]),
                        wait=rTR2.w() + rTR1.r() + [CMKW], inc="v")
            rTR1.note_r(td2)
            rTR2.note_w(td2)
            tgs = em.on("scalar", lambda s: s.activation(XP[:, 0:128], TR2_[:, :], AF.Exp, scale=-1.0),
                        wait=rTR2.r() + rXP[0].w(), inc="s")
            rTR2.note_r(tgs)
            rXP[0].note_w(tgs)
            tw = em.on("vector", (lambda c=c: (lambda v: v.scalar_tensor_tensor(
                WCH[:, :], pb[2].ap[:, 0:128], col(BCOLc)[:, c:c + 1], XP[:, 0:128],
                OP.mult, OP.mult)))(), wait=pb[2].r() + rXP[0].r() + rWCH.w() + [t_bcv], inc="v")
            pb[2].note_r(tw)
            rXP[0].note_r(tw)
            rWCH.note_w(tw)
            tdu = em.on("vector", lambda v: v.tensor_add(TR2_[:, :], TR1_[:, :], MKU[:, :]),
                        wait=rTR2.w() + rTR1.r() + [CMKU], inc="v")
            rTR1.note_r(tdu)
            rTR2.note_w(tdu)
            tgu = em.on("scalar", (lambda c=c: (lambda s: s.activation(
                GUP[:, c * C:(c + 1) * C], TR2_[:, :], AF.Exp)))(), wait=rTR2.r(), inc="s")
            rTR2.note_r(tgu)
            tgu_toks[c] = tgu
            tqt = em.on("vector", (lambda c=c: (lambda v: v.tensor_mul(
                QTT[:, c * C:(c + 1) * C], qT[:, c * C:(c + 1) * C],
                GB[:, c * C:(c + 1) * C])))(), wait=[tok_qT, tok_gb], inc="v")
            tqt_toks[c] = tqt
            wb = btb.w()
            ty0 = em.on("tensor", lambda t: t.transpose(btb.ap[:, 256:384], WCH[:, :], IDB[:, :]),
                        wait=wb + rWCH.r() + [tok_idb], inc="t")
            btb.note_w(ty0)
            rWCH.note_r(ty0)
            ty0v = em.on("vector", lambda v: v.tensor_copy(YS[:, 0:128], btb.ap[:, 256:384]),
                         wait=btb.r() + rYS[0].w(), inc="v")
            btb.note_r(ty0v)
            rYS[0].note_w(ty0v)
            w0 = rTPP[0].w()
            tr1 = em.on("vector", (lambda c=c: (lambda v: v.tensor_scalar(
                TPP[:, 0:128], VNB[:, c * C:(c + 1) * C], col(BCOLc)[:, c:c + 1],
                None, OP.mult)))(), wait=w0 + [tv2, t_bcv], inc="v")
            tr2 = em.on("vector", (lambda c=c: (lambda v: v.tensor_scalar(
                TPP[:, 128:256], KNB[:, c * C:(c + 1) * C], col(BGC)[:, c:c + 1],
                None, OP.mult)))(), wait=w0 + [tv1, t_bg], inc="v")
            rTPP[0].note_w(tr1)
            rTPP[0].note_w(tr2)
            if c == 0:
                do_tap("wch0", WCH[:, :], 128, [tw])
                do_tap("ys00", YS[:, 0:128], 128, [ty0v])
                do_tap("tpp0", TPP[:, 0:256], 256, [tr1, tr2])
                do_tap("kk0", pb[2].ap[:, 0:128], 128, [tkk]) if False else None
            xcur_reg, xcur_ap = rWCH, WCH[:, :]
            ycur_reg, ycur_ap = rYS[0], YS[:, 0:128]
            for kk in range(1, 7):
                if kk <= 5:
                    xo = kk % 2
                    wb = pb[3].w()
                    tmx = em.on("tensor", (lambda a=ycur_ap, b=xcur_ap: (lambda t: t.matmul(
                        pb[3].ap[:, 0:128], a, b, start=True, stop=True)))(),
                        wait=wb + ycur_reg.r() + xcur_reg.r(), inc="t")
                    pb[3].note_w(tmx)
                    ycur_reg.note_r(tmx)
                    xcur_reg.note_r(tmx)
                    tvx = em.on("vector", (lambda xo=xo: (lambda v: v.tensor_copy(
                        XP[:, xo * 128:(xo + 1) * 128], pb[3].ap[:, 0:128])))(),
                        wait=pb[3].r() + rXP[xo].w(), inc="v")
                    pb[3].note_r(tvx)
                    rXP[xo].note_w(tvx)
                wb = pb[4].w()
                tmy = em.on("tensor", (lambda a=xcur_ap, b=ycur_ap: (lambda t: t.matmul(
                    pb[4].ap[:, 0:128], a, b, start=True, stop=True)))(),
                    wait=wb + ycur_reg.r() + xcur_reg.r(), inc="t")
                pb[4].note_w(tmy)
                ycur_reg.note_r(tmy)
                xcur_reg.note_r(tmy)
                tvy = em.on("vector", (lambda kk=kk: (lambda v: v.tensor_copy(
                    YS[:, kk * 128:(kk + 1) * 128], pb[4].ap[:, 0:128])))(),
                    wait=pb[4].r() + rYS[kk].w(), inc="v")
                pb[4].note_r(tvy)
                rYS[kk].note_w(tvy)
                if kk <= 5:
                    xcur_reg, xcur_ap = rXP[xo], XP[:, xo * 128:(xo + 1) * 128]
                ycur_reg, ycur_ap = rYS[kk], YS[:, kk * 128:(kk + 1) * 128]
            if c == 0:
                do_tap("ys10", YS[:, 128:256], 128, [])
                do_tap("ysall", YS[:, 0:896], 896, [])
            for i, kk in enumerate(range(6, -1, -1)):
                cur, nxt = i % 2, 1 - (i % 2)
                wb = pb[3].w()
                tm = em.on("tensor", (lambda cur=cur, kk=kk: (lambda t: t.matmul(
                    pb[3].ap[:, 0:256], YS[:, kk * 128:(kk + 1) * 128],
                    TPP[:, cur * 256:(cur + 1) * 256], start=True, stop=True)))(),
                    wait=wb + rYS[kk].r() + rTPP[cur].r(), inc="t")
                pb[3].note_w(tm)
                rYS[kk].note_r(tm)
                rTPP[cur].note_r(tm)
                if kk > 0:
                    tv = em.on("vector", (lambda cur=cur, nxt=nxt: (lambda v: v.tensor_add(
                        TPP[:, nxt * 256:(nxt + 1) * 256], TPP[:, cur * 256:(cur + 1) * 256],
                        pb[3].ap[:, 0:256])))(),
                        wait=pb[3].r() + rTPP[nxt].w() + rTPP[cur].r(), inc="v")
                    rTPP[cur].note_r(tv)
                    rTPP[nxt].note_w(tv)
                else:
                    tv = em.on("vector", (lambda c=c, cur=cur: (lambda v: v.tensor_sub(
                        UCB[:, c * 256:(c + 1) * 256], TPP[:, cur * 256:(cur + 1) * 256],
                        pb[3].ap[:, 0:256])))(), wait=pb[3].r() + rTPP[cur].r(), inc="v")
                    rTPP[cur].note_r(tv)
                pb[3].note_r(tv)
            tok_ucb = tv
            ucb_toks[c] = tok_ucb
            wb = btb.w()
            tpt = em.on("tensor", (lambda c=c: (lambda t: t.transpose(
                btb.ap[:, 384:512], UCB[:, c * 256 + 128:c * 256 + 256], IDB[:, :])))(),
                wait=wb + [tok_ucb], inc="t")
            tut = em.on("tensor", (lambda c=c: (lambda t: t.transpose(
                btb.ap[:, 512:640], UCB[:, c * 256:c * 256 + 128], IDB[:, :])))(),
                wait=[tok_ucb], inc="t")
            btb.note_w(tpt); btb.note_w(tut)
            rb = btb.r()
            tptv = em.on("vector", (lambda c=c: (lambda v: v.tensor_copy(
                PTBs[:, c * C:(c + 1) * C], btb.ap[:, 384:512])))(), wait=rb, inc="v")
            tutv = em.on("vector", (lambda c=c: (lambda v: v.tensor_copy(
                ULT[:, c * C:(c + 1) * C], btb.ap[:, 512:640])))(), wait=rb, inc="v")
            btb.note_r(tptv); btb.note_r(tutv)
            ptb_toks[c] = tptv
            ult_toks[c] = tutv
            wb = pb[2].w()
            thm = em.on("tensor", (lambda c=c: (lambda t: t.matmul(
                pb[2].ap[:, 0:128], KPB[:, c * C:(c + 1) * C],
                UCB[:, c * 256:c * 256 + 128], start=True, stop=True)))(),
                wait=wb + [tkp, tok_ucb], inc="t")
            thtm = em.on("tensor", (lambda c=c: (lambda t: t.matmul(
                pb[2].ap[:, 128:256], UCB[:, c * 256:c * 256 + 128],
                KPB[:, c * C:(c + 1) * C], start=True, stop=True)))(), inc="t")
            tgm = em.on("tensor", (lambda c=c: (lambda t: t.matmul(
                pb[2].ap[:, 256:384], UCB[:, c * 256 + 128:c * 256 + 256],
                KPB[:, c * C:(c + 1) * C], start=True, stop=True)))(), inc="t")
            pb[2].note_w(thm); pb[2].note_w(thtm); pb[2].note_w(tgm)
            rb = pb[2].r()
            thv = em.on("vector", (lambda c=c: (lambda v: v.tensor_copy(
                HCB[:, c * C:(c + 1) * C], pb[2].ap[:, 0:128])))(), wait=rb, inc="v")
            thtv = em.on("vector", (lambda c=c: (lambda v: v.tensor_copy(
                HTB[:, c * C:(c + 1) * C], pb[2].ap[:, 128:256])))(), inc="v")
            tgv = em.on("vector", (lambda c=c: (lambda v: v.scalar_tensor_tensor(
                GTB[:, c * C:(c + 1) * C], IDF[:, :], col(GLR)[:, c:c + 1],
                pb[2].ap[:, 256:384], OP.mult, OP.subtract)))(),
                wait=[t_glr, CIDF], inc="v")
            pb[2].note_r(thv); pb[2].note_r(thtv); pb[2].note_r(tgv)
            hcb_toks[c] = thv
            htb_toks[c] = thtv
            gtb_toks[c] = tgv
            if c == 0:
                wb = btb.w()
                tga = em.on("tensor", lambda t: t.transpose(btb.ap[:, 640:768], GTB[:, 0:C], IDB[:, :]),
                            wait=wb + [tgv], inc="t")
                btb.note_w(tga)
                tgav = em.on("vector", lambda v: v.tensor_copy(GA[:, 0:C], btb.ap[:, 640:768]),
                             wait=btb.r(), inc="v")
                btb.note_r(tgav)
                tgatv = em.on("vector", lambda v: v.tensor_copy(GAT[:, 0:C], GTB[:, 0:C]),
                              wait=[tgv], inc="v")
                thav = em.on("vector", lambda v: v.tensor_copy(HA[:, 0:C], HCB[:, 0:C]),
                             wait=[thv], inc="v")
                ga_tok = [tgav, tgatv, thav]
                gacc_cur = 0
            else:
                prv, nxt = gacc_cur, 1 - gacc_cur
                wb = pb[2].w()
                tm1 = em.on("tensor", (lambda c=c, prv=prv: (lambda t: t.matmul(
                    pb[2].ap[:, 384:512], GTB[:, c * C:(c + 1) * C],
                    GA[:, prv * C:(prv + 1) * C], start=True, stop=True)))(),
                    wait=wb + ga_tok + [tgv], inc="t")
                pb[2].note_w(tm1)
                tv1_ = em.on("vector", (lambda nxt=nxt: (lambda v: v.tensor_copy(
                    GA[:, nxt * C:(nxt + 1) * C], pb[2].ap[:, 384:512])))(),
                    wait=pb[2].r(), inc="v")
                pb[2].note_r(tv1_)
                wb = pb[2].w()
                tm2 = em.on("tensor", (lambda c=c, prv=prv: (lambda t: t.matmul(
                    pb[2].ap[:, 384:512], GA[:, prv * C:(prv + 1) * C],
                    GTB[:, c * C:(c + 1) * C], start=True, stop=True)))(),
                    wait=wb + ga_tok + [tgv], inc="t")
                pb[2].note_w(tm2)
                tv2_ = em.on("vector", (lambda nxt=nxt: (lambda v: v.tensor_copy(
                    GAT[:, nxt * C:(nxt + 1) * C], pb[2].ap[:, 384:512])))(),
                    wait=pb[2].r(), inc="v")
                pb[2].note_r(tv2_)
                wb = pb[2].w()
                tm3 = em.on("tensor", (lambda c=c, prv=prv: (lambda t: t.matmul(
                    pb[2].ap[:, 384:512], GTB[:, c * C:(c + 1) * C],
                    HA[:, prv * C:(prv + 1) * C], start=True, stop=True)))(),
                    wait=wb + ga_tok + [tgv], inc="t")
                pb[2].note_w(tm3)
                tv3_ = em.on("vector", (lambda c=c, nxt=nxt: (lambda v: v.tensor_add(
                    HA[:, nxt * C:(nxt + 1) * C], HCB[:, c * C:(c + 1) * C],
                    pb[2].ap[:, 384:512])))(), wait=pb[2].r() + [thv], inc="v")
                pb[2].note_r(tv3_)
                ga_tok = [tv1_, tv2_, tv3_]
                gacc_cur = nxt

        def wmean_finish(wmslot, swslot, iswslot, nelem, wtok):
            t1 = em.on("vector", (lambda wmslot=wmslot: (lambda v: v.reduce_sum(
                col(WMT)[:, 0:1], col(wmslot), axis=AX.X)))(), wait=[wtok], inc="v")
            wb = btf.w()
            tm = em.on("tensor", lambda t: t.matmul(
                btf.ap[0:1, 0:1], ONES[:, 0:1], col(WMT)[:, 0:1], start=True, stop=True),
                wait=wb + [t1, tok_ones], inc="t")
            btf.note_w(tm)
            t2 = em.on("vector", (lambda nelem=nelem, wmslot=wmslot: (lambda v: v.tensor_scalar(
                col(wmslot)[0:1, 1:2], btf.ap[0:1, 0:1], 1.0 / nelem, EPS, OP.mult, OP.max)))(),
                wait=btf.r(), inc="v")
            btf.note_r(t2)
            wb = btf.w()
            tb = em.on("tensor", (lambda wmslot=wmslot: (lambda t: t.matmul(
                btf.ap[:, 1:2], ONES[0:1, :], col(wmslot)[0:1, 1:2], start=True, stop=True)))(),
                wait=wb + [t2], inc="t")
            btf.note_w(tb)
            t3 = em.on("vector", (lambda iswslot=iswslot: (lambda v: v.tensor_copy(
                col(iswslot)[:, 0:1], btf.ap[:, 1:2])))(), wait=btf.r(), inc="v")
            btf.note_r(t3)
            t4 = em.on("vector", (lambda swslot=swslot, iswslot=iswslot: (lambda v: v.reciprocal(
                col(swslot)[:, 0:1], col(iswslot)[:, 0:1])))(), wait=[t3], inc="v")
            return t3, t4
        tok_isw1, tok_sw1 = wmean_finish(WM1, SW1C, ISW1, D * HID, wm1_toks[-1])
        tok_isw2, tok_sw2 = wmean_finish(WM2, SW2C, ISW2, D * HID, wm2_toks[-1])

        tp1 = em.on("vector", (lambda gc=gacc_cur: (lambda v: v.tensor_copy(
            CCP[:, 0:128], GAT[:, gc * C:(gc + 1) * C])))(), wait=ga_tok, inc="v")
        tp2 = em.on("vector", (lambda gc=gacc_cur: (lambda v: v.tensor_copy(
            CCP[:, 128:256], HA[:, gc * C:(gc + 1) * C])))(), wait=ga_tok, inc="v")
        tcc1 = em.on("gpsimd", lambda g: g.dma_start(out=cc_in, in_=CCP[:, :]),
                     wait=[tp1, tp2], inc="dG", amt=16)
        tcc2 = em.on("gpsimd", lambda g: g.collective_compute(
            "AllGather", OP.bypass, replica_groups=[[0, 1, 2, 3], [4, 5, 6, 7]],
            ins=[cc_in], outs=[cc_out]), wait=[tcc1], inc="cc")
        tcc3 = em.on("gpsimd", lambda g: g.dma_start(
            out=RCV[:, :].rearrange("p (s w) -> p s w", s=3),
            in_=cc_out.rearrange("(s p) w -> p s w", p=128)[:, 0:3, :]),
            wait=[tcc2], inc="dG", amt=16)

        tz0 = em.on("vector", lambda v: v.memset(ZI[:, 0:C], 0.0), inc="v")
        zi_cur = 0
        tz_tok = [tz0, tcc3, tok_smk]
        rGPT = Reg(); rHPT = Reg()
        for p in range(3):
            tb1 = em.on("vector", (lambda p=p: (lambda v: v.tensor_scalar(
                GPT[:, :], RCV[:, p * 256:p * 256 + 128], SMK[:, p:p + 1], None,
                OP.mult)))(), wait=tz_tok + rGPT.w(), inc="v")
            rGPT.note_w(tb1)
            tb2 = em.on("vector", (lambda p=p: (lambda v: v.scalar_tensor_tensor(
                GPT[:, :], IDF[:, :], SMK[:, 3 + p:4 + p], GPT[:, :],
                OP.mult, OP.add)))(), wait=rGPT.w() + [CIDF], inc="v")
            rGPT.note_w(tb2)
            tb3 = em.on("vector", (lambda p=p: (lambda v: v.tensor_scalar(
                HPT[:, :], RCV[:, p * 256 + 128:p * 256 + 256], SMK[:, p:p + 1],
                None, OP.mult)))(), wait=tz_tok + rHPT.w(), inc="v")
            rHPT.note_w(tb3)
            nxt = 1 - zi_cur
            wb = btf.w()
            tzm = em.on("tensor", (lambda zc=zi_cur: (lambda t: t.matmul(
                btf.ap[:, 128:256], GPT[:, :], ZI[:, zc * C:(zc + 1) * C],
                start=True, stop=True)))(), wait=wb + rGPT.r() + rHPT.r() + tz_tok, inc="t")
            btf.note_w(tzm)
            rGPT.note_r(tzm)
            tza = em.on("vector", (lambda nxt=nxt: (lambda v: v.tensor_add(
                ZI[:, nxt * C:(nxt + 1) * C], HPT[:, :], btf.ap[:, 128:256])))(),
                wait=btf.r() + rHPT.r(), inc="v")
            btf.note_r(tza)
            rHPT.note_r(tza)
            tz_tok = [tza]
            zi_cur = nxt
        tzb = em.on("vector", (lambda zc=zi_cur: (lambda v: v.tensor_copy(
            ZBt[:, 0:C], ZI[:, zc * C:(zc + 1) * C])))(), wait=tz_tok, inc="v")
        wb = btf.w()
        tzt = em.on("tensor", (lambda zc=zi_cur: (lambda t: t.transpose(
            btf.ap[:, 256:384], ZI[:, zc * C:(zc + 1) * C], IDF[:, :])))(),
            wait=wb + tz_tok + [CIDF], inc="t")
        btf.note_w(tzt)
        tztv = em.on("vector", lambda v: v.tensor_copy(ZTt[:, 0:C], btf.ap[:, 256:384]),
                     wait=btf.r(), inc="v")
        btf.note_r(tztv)

        do_tap("zine", ZBt[:, 0:128], 128, [tzb, tztv])
        do_tap("gtbe", GTB[:, 0:2048], 2048, gtb_toks)
        do_tap("hcbe", HCB[:, 0:2048], 2048, hcb_toks)
        do_tap("gupe", GUP[:, 0:2048], 2048, tgu_toks)
        do_tap("qte", qT[:, 0:2048], 2048, [tok_qT])
        do_tap("kte", kT[:, 0:2048], 2048, [tok_kT])
        do_tap("lje", S1[:, 0:2048], 2048, [tok_ljb])
        do_tap("ucbe", UCB[:, 0:4096], 4096, ucb_toks[0:8])
        rZ = [Reg(), Reg()]
        rZT = [Reg(), Reg()]
        rZ[0].note_w(tzb)
        rZT[0].note_w(tztv)
        rUFT = [Reg(), Reg()]
        rMPT = [Reg(), Reg()]
        x2_toks = [None] * NT
        for c in range(NT):
            zc = c % 2
            zn = (c + 1) % 2
            uo = c % 2
            if c < NT - 1:
                wb = pb[2].w()
                tzm1 = em.on("tensor", (lambda c=c, zc=zc: (lambda t: t.matmul(
                    pb[2].ap[:, 0:128], GTB[:, c * C:(c + 1) * C], ZBt[:, zc * C:(zc + 1) * C],
                    start=True, stop=True)))(),
                    wait=wb + rZ[zc].r() + [gtb_toks[c]], inc="t")
                tzm2 = em.on("tensor", (lambda c=c, zc=zc: (lambda t: t.matmul(
                    pb[2].ap[:, 128:256], ZBt[:, zc * C:(zc + 1) * C], GTB[:, c * C:(c + 1) * C],
                    start=True, stop=True)))(), inc="t")
                pb[2].note_w(tzm1); pb[2].note_w(tzm2)
                rZ[zc].note_r(tzm1); rZ[zc].note_r(tzm2)
                rb = pb[2].r()
                tza1 = em.on("vector", (lambda c=c, zn=zn: (lambda v: v.tensor_add(
                    ZBt[:, zn * C:(zn + 1) * C], HCB[:, c * C:(c + 1) * C],
                    pb[2].ap[:, 0:128])))(), wait=rb + rZ[zn].w() + [hcb_toks[c]], inc="v")
                tza2 = em.on("vector", (lambda c=c, zn=zn: (lambda v: v.tensor_add(
                    ZTt[:, zn * C:(zn + 1) * C], HTB[:, c * C:(c + 1) * C],
                    pb[2].ap[:, 128:256])))(), wait=rZT[zn].w() + [htb_toks[c]], inc="v")
                pb[2].note_r(tza1); pb[2].note_r(tza2)
                rZ[zn] = Reg(); rZ[zn].note_w(tza1)
                rZT[zn] = Reg(); rZT[zn].note_w(tza2)
            wb = pb[3].w()
            tm1 = em.on("tensor", (lambda c=c, zc=zc: (lambda t: t.matmul(
                pb[3].ap[:, 0:128], ZBt[:, zc * C:(zc + 1) * C], PTBs[:, c * C:(c + 1) * C],
                start=True, stop=True)))(),
                wait=wb + rZ[zc].r() + [ptb_toks[c]], inc="t")
            pb[3].note_w(tm1)
            rZ[zc].note_r(tm1)
            tvu = em.on("vector", (lambda c=c, uo=uo: (lambda v: v.tensor_sub(
                UFT[:, uo * 128:(uo + 1) * 128], ULT[:, c * C:(c + 1) * C],
                pb[3].ap[:, 0:128])))(),
                wait=pb[3].r() + rUFT[uo].w() + [ult_toks[c]], inc="v")
            pb[3].note_r(tvu)
            rUFT[uo] = Reg(); rUFT[uo].note_w(tvu)
            wb = pb[3].w()
            tm2 = em.on("tensor", (lambda c=c, uo=uo: (lambda t: t.matmul(
                pb[3].ap[:, 128:256], UFT[:, uo * 128:(uo + 1) * 128],
                qT[:, c * C:(c + 1) * C], start=True, stop=True)))(),
                wait=wb + rUFT[uo].r() + [tok_qT], inc="t")
            pb[3].note_w(tm2)
            rUFT[uo].note_r(tm2)
            tvm = em.on("vector", (lambda c=c, uo=uo: (lambda v: v.tensor_mul(
                MPT[:, uo * 128:(uo + 1) * 128], GUP[:, c * C:(c + 1) * C],
                pb[3].ap[:, 128:256])))(),
                wait=pb[3].r() + rMPT[uo].w() + [tgu_toks[c]], inc="v")
            pb[3].note_r(tvm)
            rMPT[uo] = Reg(); rMPT[uo].note_w(tvm)
            wb = pb[3].w()
            tm3 = em.on("tensor", (lambda c=c, zc=zc: (lambda t: t.matmul(
                pb[3].ap[:, 256:384], ZTt[:, zc * C:(zc + 1) * C], QTT[:, c * C:(c + 1) * C],
                start=True, stop=False)))(),
                wait=wb + rZT[zc].r() + [tqt_toks[c]], inc="t")
            tm4 = em.on("tensor", (lambda c=c, uo=uo: (lambda t: t.matmul(
                pb[3].ap[:, 256:384], KNB[:, c * C:(c + 1) * C], MPT[:, uo * 128:(uo + 1) * 128],
                start=False, stop=True)))(), wait=rMPT[uo].r() + [knb_toks[c]], inc="t")
            pb[3].note_w(tm3); pb[3].note_w(tm4)
            rZT[zc].note_r(tm3)
            rMPT[uo].note_r(tm4)
            tvo = em.on("vector", (lambda c=c, uo=uo: (lambda v: v.tensor_mul(
                UFT[:, uo * 128:(uo + 1) * 128], gT[:, c * C:(c + 1) * C],
                pb[3].ap[:, 256:384])))(),
                wait=pb[3].r() + rUFT[uo].w() + [tok_gT], inc="v")
            pb[3].note_r(tvo)
            rUFT[uo] = Reg(); rUFT[uo].note_w(tvo)
            yo = c % 2
            wy = rYTL.w() if c == 0 else []
            oy = []
            for nn in range(2):
                wb = pb[4].w()
                tm5 = em.on("tensor", (lambda nn=nn, uo=uo: (lambda t: t.matmul(
                    pb[4].ap[:, 0:512], UFT[:, uo * 128:(uo + 1) * 128],
                    WO[:, nn * 512:(nn + 1) * 512], start=True, stop=True)))(),
                    wait=wb + rUFT[uo].r() + [tok_wo], inc="t")
                pb[4].note_w(tm5)
                rUFT[uo].note_r(tm5)
                tv5 = em.on("vector", (lambda c=c, nn=nn, yo=yo: (lambda v: v.scalar_tensor_tensor(
                    YTL[:, yo * D + nn * 512: yo * D + (nn + 1) * 512],
                    pb[4].ap[:, 0:512], 1.0, BOB[:, nn * 512:(nn + 1) * 512],
                    OP.mult, OP.add,
                    accum_out=col(TMPA)[:, nn:nn + 1])))(),
                    wait=pb[4].r() + wy + [CBOB] +
                         ([x2_toks[c - 2]] if c >= 2 else []), inc="v")
                pb[4].note_r(tv5)
                oy.append(tv5)
            tsum = em.on("vector", (lambda c=c: (lambda v: v.tensor_add(
                col(SUM2)[:, c:c + 1], col(TMPA)[:, 0:1], col(TMPA)[:, 1:2])))(),
                wait=oy, inc="v")
            tsq = em.on("scalar", (lambda c=c, yo=yo: (lambda s: s.activation(
                S1[:, 0:1024], YTL[:, yo * D:(yo + 1) * D], AF.Square,
                accum_out=col(SQ2)[:, c:c + 1])))(), wait=oy + rS1.w(), inc="s")
            rS1.note_w(tsq)
            t1 = em.on("vector", (lambda c=c: (lambda v: v.tensor_scalar(
                col(MU2)[:, c:c + 1], col(SUM2)[:, c:c + 1], 1.0 / D, None, OP.mult)))(),
                wait=[tsum], inc="v")
            t2 = em.on("vector", (lambda c=c: (lambda v: v.tensor_scalar(
                col(TMPB)[:, 0:1], col(SQ2)[:, c:c + 1], 1.0 / D, None, OP.mult)))(),
                wait=[tsq], inc="v")
            t3 = em.on("vector", (lambda c=c: (lambda v: v.tensor_mul(
                col(TMPB)[:, 1:2], col(MU2)[:, c:c + 1], col(MU2)[:, c:c + 1])))(),
                wait=[t1], inc="v")
            t4 = em.on("vector", lambda v: v.tensor_sub(
                col(TMPB)[:, 0:1], col(TMPB)[:, 0:1], col(TMPB)[:, 1:2]),
                wait=[t2, t3], inc="v")
            t5 = em.on("scalar", lambda s: s.activation(
                col(TMPB)[:, 1:2], col(TMPB)[:, 0:1], AF.Sqrt, bias=EPSC),
                wait=[t4, tok_epsc], inc="s")
            t6 = em.on("vector", (lambda c=c: (lambda v: v.reciprocal(
                col(R2C)[:, c:c + 1], col(TMPB)[:, 1:2])))(), wait=[t5], inc="v")
            t7 = em.on("vector", (lambda c=c: (lambda v: v.scalar_tensor_tensor(
                col(NM2)[:, c:c + 1], col(MU2)[:, c:c + 1], -1.0, col(R2C)[:, c:c + 1],
                OP.mult, OP.mult)))(), wait=[t6], inc="v")
            t8 = em.on("vector", (lambda c=c, yo=yo: (lambda v: v.tensor_scalar(
                YTL[:, yo * D:(yo + 1) * D], YTL[:, yo * D:(yo + 1) * D],
                col(R2C)[:, c:c + 1], col(NM2)[:, c:c + 1], OP.mult, OP.add)))(),
                wait=[t7, tsq], inc="v")
            t9 = em.on("vector", (lambda c=c, yo=yo: (lambda v: v.scalar_tensor_tensor(
                R1[:, c * D:(c + 1) * D], YTL[:, yo * D:(yo + 1) * D], 1.0,
                R1[:, c * D:(c + 1) * D], OP.mult, OP.add,
                accum_out=col(SUM3)[:, c:c + 1])))(), wait=[t8, tok_x], inc="v")
            rYTL.note_w(t9)
            rYTL.note_r(t9)
            x2_toks[c] = t9

        rS1b = Reg()
        sq3_toks = []
        for c in range(NT):
            tq = em.on("scalar", (lambda c=c: (lambda s: s.activation(
                S1[:, 1024:2048], R1[:, c * D:(c + 1) * D], AF.Square,
                accum_out=col(SQ3)[:, c:c + 1])))(), wait=[x2_toks[c]] + rS1b.w(), inc="s")
            rS1b.note_w(tq)
            sq3_toks.append(tq)
        t1 = em.on("vector", lambda v: v.tensor_scalar(col(MU3), col(SUM3), 1.0 / D, None, OP.mult),
                   wait=x2_toks, inc="v")
        t2 = em.on("vector", lambda v: v.tensor_scalar(col(TMPA), col(SQ3), 1.0 / D, None, OP.mult),
                   wait=sq3_toks, inc="v")
        t3 = em.on("vector", lambda v: v.tensor_mul(col(TMPB), col(MU3), col(MU3)), wait=[t1], inc="v")
        t4 = em.on("vector", lambda v: v.tensor_sub(col(TMPA), col(TMPA), col(TMPB)),
                   wait=[t2, t3], inc="v")
        t5 = em.on("vector", lambda v: v.tensor_scalar(col(TMPB), col(TMPA), EPS, None, OP.add),
                   wait=[t4], inc="v")
        t6a = em.on("vector", lambda v: v.reciprocal(col(TMPC), col(TMPB)), wait=[t5], inc="v")
        t6 = em.on("vector", lambda v: v.tensor_mul(col(TMPC), col(TMPA), col(TMPC)),
                   wait=[t6a], inc="v")
        t8 = em.on("vector", lambda v: v.tensor_scalar(col(TMPC), col(TMPC), EPS, None, OP.add),
                   wait=[t6], inc="v")
        t9 = em.on("vector", lambda v: v.tensor_mul(col(TMPA), col(TMPB), col(TMPC)),
                   wait=[t8], inc="v")
        t10 = em.on("scalar", lambda s: s.activation(col(TMPB), col(TMPA), AF.Sqrt),
                    wait=[t9], inc="s")
        t11 = em.on("vector", lambda v: v.reciprocal(col(RHO), col(TMPB)), wait=[t10], inc="v")
        t12 = em.on("vector", lambda v: v.scalar_tensor_tensor(
            col(NMR), col(MU3), -1.0, col(RHO), OP.mult, OP.mult), wait=[t11], inc="v")
        rho_tok = t12

        tok_b2b = em.on("gpsimd", lambda g: g.dma_start(out=B2B[:, :], in_=b2_b_d),
                        wait=x2_toks, inc="dG", amt=16)
        rZI32 = Reg()
        rXQN = [Reg(), Reg()]
        rS1h = [rS1, rS1b]
        xq_last = None
        for c in range(NT):
            hh = 0
            ho = 0
            rh = rS1h[hh]
            ta = em.on("vector", (lambda c=c, ho=ho: (lambda v: v.tensor_scalar(
                S1[:, ho:ho + D], R1[:, c * D:(c + 1) * D], col(RHO)[:, c:c + 1],
                col(NMR)[:, c:c + 1], OP.mult, OP.add)))(),
                wait=rh.w() + [rho_tok, x2_toks[c]], inc="v")
            rh.note_w(ta)
            tb = em.on("vector", (lambda ho=ho: (lambda v: v.scalar_tensor_tensor(
                S1[:, ho:ho + D], S1[:, ho:ho + D], 1.0, GMB[:, :], OP.mult, OP.mult)))(),
                wait=rh.w() + [CGMB], inc="v")
            rh.note_w(tb)
            tc = em.on("vector", (lambda ho=ho: (lambda v: v.tensor_add(
                S1[:, ho:ho + D], S1[:, ho:ho + D], BTB[:, :])))(),
                wait=rh.w() + [CBTB], inc="v")
            rh.note_w(tc)
            tdm = em.on("vector", (lambda c=c, ho=ho, hh=hh: (lambda v: v.reduce_max(
                col(TMPA)[:, hh:hh + 1], S1[:, ho:ho + D], axis=AX.X,
                apply_absolute_value=True)))(), wait=rh.r(), inc="v")
            rh.note_r(tdm)
            tde = em.on("vector", (lambda hh=hh: (lambda v: v.tensor_scalar(
                col(TMPA)[:, 2 + hh:3 + hh], col(TMPA)[:, hh:hh + 1], EPS, None,
                OP.max)))(), wait=[tdm], inc="v")
            tdf = em.on("vector", (lambda hh=hh: (lambda v: v.reciprocal(
                col(TMPB)[:, hh:hh + 1], col(TMPA)[:, 2 + hh:3 + hh])))(),
                wait=[tde], inc="v")
            tdg = em.on("vector", (lambda c=c, hh=hh: (lambda v: v.tensor_scalar(
                col(SXC)[:, c:c + 1], col(TMPB)[:, hh:hh + 1], 127.0, None, OP.mult)))(),
                wait=[tdf], inc="v")
            tdh = em.on("vector", (lambda c=c, hh=hh: (lambda v: v.scalar_tensor_tensor(
                col(ISX)[:, c:c + 1], col(TMPA)[:, 2 + hh:3 + hh], 1.0 / 127.0,
                col(ISW1)[:, 0:1], OP.mult, OP.mult)))(), wait=[tde, tok_isw1], inc="v")
            xo = c % 2
            wxq = rXQN[xo].w()
            tj = None
            for hf in range(2):
                ti = em.on("vector", (lambda c=c, hf=hf, ho=ho: (lambda v: v.tensor_scalar(
                    ZI32[:, :], S1[:, ho + hf * 512:ho + (hf + 1) * 512],
                    col(SXC)[:, c:c + 1], None, OP.mult)))(),
                    wait=rZI32.w() + rh.r() + [tdg], inc="v")
                rZI32.note_w(ti)
                rh.note_r(ti)
                tj = em.on("vector", (lambda xo=xo, hf=hf: (lambda v: v.tensor_copy(
                    XQN[:, xo * D + hf * 512: xo * D + (hf + 1) * 512], ZI32[:, :])))(),
                    wait=(wxq if hf == 0 else []) + rZI32.r(), inc="v")
                rZI32.note_r(tj)
            rXQN[xo] = Reg(); rXQN[xo].note_w(tj)
            for half in range(2):
                wb = btb.w()
                tts = []
                for kk in range(4):
                    k = half * 4 + kk
                    tt = em.on("tensor", (lambda k=k, kk=kk, xo=xo: (lambda t: t.transpose(
                        btb.ap[:, kk * 128:(kk + 1) * 128],
                        XQN[:, xo * D + k * 128: xo * D + (k + 1) * 128], IDB[:, :])))(),
                        wait=(wb if kk == 0 else []) + rXQN[xo].r() + [tok_idb] +
                             (tts[-1:] if tts else []), inc="t")
                    tts.append(tt)
                    btb.note_w(tt)
                    rXQN[xo].note_r(tt)
                rb = btb.r()
                evs = []
                for kk in range(4):
                    k = half * 4 + kk
                    te = em.on("vector", (lambda c=c, k=k, kk=kk: (lambda v: v.tensor_copy(
                        R2[:, k * TOK + c * 128: k * TOK + (c + 1) * 128],
                        btb.ap[:, kk * 128:(kk + 1) * 128])))(),
                        wait=rb + ([tok_xT] if c == 0 else []), inc="v")
                    btb.note_r(te)
                    evs.append(te)
                xq_last = evs[-1]
        tok_xq = xq_last

        do_tap("x2e", R1[:, 0:4096], 4096, x2_toks)
        do_tap("xqe", R2[:, 0:4096], 4096, [tok_xq])
        do_tap("coe", CO[:, 0:768], 768, x2_toks + [rho_tok])
        tok_sxw1b = bcast_from_col(col(ISX), rF1, F1, rF1.w() + [tdh, tok_xq, tzb, tztv])

        tmx0 = em.on("vector", lambda v: v.memset(F2[:, 0:TOK], 0.0),
                     wait=rF2.w() + rWMb[0].w() + rWMb[1].w(), inc="v")
        rF2.note_w(tmx0)
        rRWM = Reg()
        tmn0 = em.on("vector", lambda v: v.memset(RW[:, 0:TOK], 0.0),
                     wait=rROW0.w() + rL16[0].w() + rL16[1].w() + rB16.w() + rRW16.w() + [tok_sxw1b],
                     inc="v")
        rRWM.note_w(tmn0)
        w1_slice_r = W1T_d.rearrange("(k p) (h j) -> p k h j", p=128, j=128)
        hb_r = Hb.rearrange("(h p) t -> h p t", p=128)
        rW1S = [Reg(), Reg()]
        rW1Q = [Reg(), Reg()]
        rHSL = [Reg(), Reg()]
        tmx_last = tmx0
        w2_slice2 = W2T_d.rearrange("(h p) dd -> h p dd", p=128)
        rW2Sr = Reg()
        w2q_toks = [None] * NH
        for h in range(NH):
            pp = h % 2
            td = em.on("sync", (lambda h=h, pp=pp: (lambda sy: sy.dma_start(
                out=W1S[:, pp * D:(pp + 1) * D].rearrange("p (k j) -> p k j", k=ND),
                in_=w1_slice_r[:, :, h, :])))(),
                wait=rW1S[pp].w() + ([wm1_toks[-1], x2_toks[-1], tok_sxw1b]
                                     if h <= 1 else []),
                inc="dW0" if pp == 0 else "dW1", amt=16)
            rW1S[pp] = Reg(); rW1S[pp].note_w(td)
            tq1 = em.on("vector", (lambda pp=pp: (lambda v: v.tensor_scalar(
                W1Q[:, pp * D:(pp + 1) * D], W1S[:, pp * D:(pp + 1) * D],
                col(SW1C)[:, 0:1], 0.5, OP.mult, OP.is_gt)))(),
                wait=rW1Q[pp].w() + rW1S[pp].r() + [tok_sw1], inc="v")
            rW1S[pp].note_r(tq1)
            tq2 = em.on("vector", (lambda pp=pp: (lambda g: g.tensor_scalar(
                W1S[:, pp * D:(pp + 1) * D], W1S[:, pp * D:(pp + 1) * D],
                col(SW1C)[:, 0:1], -0.5, OP.mult, OP.is_lt)))(),
                wait=rW1S[pp].w() + [tq1, tok_sw1], inc="v")
            rW1S[pp].note_w(tq2)
            tq3 = em.on("vector", (lambda pp=pp: (lambda v: v.tensor_sub(
                W1Q[:, pp * D:(pp + 1) * D], W1Q[:, pp * D:(pp + 1) * D],
                W1S[:, pp * D:(pp + 1) * D])))(), wait=[tq1, tq2], inc="v")
            rW1S[pp].note_r(tq3)
            rW1Q[pp] = Reg(); rW1Q[pp].note_w(tq3)
            hp = h % 2
            whs = rHSL[hp].w()
            ev = []
            for pair in range(2):
                banks2 = [pb[2 * pair], pb[2 * pair + 1]]
                wbs = [b.w() for b in banks2]
                mms = [[], []]
                for k in range(ND):
                    for j in range(2):
                        nchk = 2 * pair + j
                        tm = em.on("tensor", (lambda k=k, nchk=nchk, bank=banks2[j], pp=pp: (lambda t: t.matmul(
                            bank.ap[:, 0:512], W1Q[:, pp * D + k * 128: pp * D + (k + 1) * 128],
                            R2[:, k * TOK + nchk * 512: k * TOK + (nchk + 1) * 512],
                            start=(k == 0), stop=(k == ND - 1))))(),
                            wait=(wbs[j] if k == 0 else []) + rW1Q[pp].r() + [tok_xq],
                            inc="t")
                        mms[j].append(tm)
                for j in range(2):
                    banks2[j].note_w(mms[j][-1])
                    rW1Q[pp].note_r(mms[j][-1])
            for nchk in range(4):
                bank = pb[nchk]
                tv = em.on("vector", (lambda nchk=nchk, bank=bank: (lambda v: v.tensor_mul(
                    S1[:, nchk * 512:(nchk + 1) * 512], bank.ap[:, 0:512],
                    F1[:, nchk * 512:(nchk + 1) * 512])))(),
                    wait=bank.r() + (rS1.w() + rS1b.w() if nchk == 0 else []) +
                         [tok_sxw1b], inc="v")
                bank.note_r(tv)
                rF1.note_r(tv)
                rS1.note_w(tv)
                ts_ = em.on("scalar", (lambda h=h, nchk=nchk, hp=hp: (lambda s: s.activation(
                    HSL[:, hp * TOK + nchk * 512: hp * TOK + (nchk + 1) * 512],
                    S1[:, nchk * 512:(nchk + 1) * 512], AF.Gelu,
                    bias=B1T[:, h:h + 1])))(),
                    wait=rS1.r() + (whs if nchk == 0 else []) + [CB1T], inc="s")
                rS1.note_r(ts_)
                ev.append(ts_)
            rHSL[hp] = Reg(); rHSL[hp].note_w(ev[-1])
            for e in ev:
                rHSL[hp].note_w(e)
            tmx = em.on("vector", (lambda hp=hp: (lambda g: g.tensor_tensor(
                F2[:, 0:TOK], F2[:, 0:TOK], HSL[:, hp * TOK:(hp + 1) * TOK],
                OP.max)))(), wait=ev + rF2.w(), inc="v")
            rF2.note_w(tmx)
            rHSL[hp].note_r(tmx)
            tmn = em.on("vector", (lambda hp=hp: (lambda g: g.scalar_tensor_tensor(
                RW[:, 0:TOK], HSL[:, hp * TOK:(hp + 1) * TOK], -1.0, RW[:, 0:TOK],
                OP.mult, OP.max)))(), wait=ev + rRWM.w(), inc="v")
            rRWM.note_w(tmn)
            rHSL[hp].note_r(tmn)
            tmx_last = tmx
            tmn_last = tmn
            tdh_ = em.on("scalar", (lambda h=h, hp=hp: (lambda s: s.dma_start(
                out=hb_r[h], in_=HSL[:, hp * TOK:(hp + 1) * TOK])))(),
                wait=ev, inc="dA", amt=16)
            rHSL[hp].note_r(tdh_)
            td2 = em.on("sync", (lambda h=h: (lambda sy: sy.dma_start(
                out=W2S[:, 0:D], in_=w2_slice2[h])))(),
                wait=rW2Sr.w() + ([wm2_toks[-1], x2_toks[-1], tok_xq]
                                  if h == 0 else []), inc="dX", amt=16)
            rW2Sr.note_w(td2)
            wq1 = em.on("vector", (lambda h=h: (lambda g: g.tensor_scalar(
                W2Q[:, h * D:(h + 1) * D], W2S[:, 0:D],
                col(SW2C)[:, 0:1], 0.5, OP.mult, OP.is_gt)))(),
                wait=rW2Sr.r() + [tok_sw2] + ([x2_toks[-1], tok_xq] if h == 0 else []),
                inc="v")
            rW2Sr.note_r(wq1)
            wq2 = em.on("vector", (lambda h=h: (lambda g: g.tensor_scalar(
                W2S[:, 0:D], W2S[:, 0:D],
                col(SW2C)[:, 0:1], -0.5, OP.mult, OP.is_lt)))(),
                wait=rW2Sr.w(), inc="v")
            rW2Sr.note_w(wq2)
            wq3 = em.on("vector", (lambda h=h: (lambda g: g.tensor_sub(
                W2Q[:, h * D:(h + 1) * D], W2Q[:, h * D:(h + 1) * D],
                W2S[:, 0:D])))(), wait=[wq1, wq2], inc="v")
            rW2Sr.note_r(wq3)
            w2q_toks[h] = wq3

        hb_done = ("dA", em.cnt["dA"])
        pass1_t = ("t", em.cnt["t"])

        s2r = []
        for c in range(NT):
            wb = btf.w()
            tt = em.on("tensor", (lambda c=c: (lambda t: t.transpose(
                btf.ap[:, 0:128], F2[:, c * 128:(c + 1) * 128], IDF[:, :])))(),
                wait=wb + [tmx_last, CIDF], inc="t")
            tt2 = em.on("tensor", (lambda c=c: (lambda t: t.transpose(
                btf.ap[:, 128:256], RW[:, c * 128:(c + 1) * 128], IDF[:, :])))(),
                wait=[tmn_last], inc="t")
            btf.note_w(tt); btf.note_w(tt2)
            rF2.note_r(tt)
            rRWM.note_r(tt2)
            rb = btf.r()
            tr = em.on("vector", (lambda c=c: (lambda v: v.reduce_max(
                col(TMPA)[:, c:c + 1], btf.ap[:, 0:128], axis=AX.X)))(),
                wait=rb, inc="v")
            tr2 = em.on("vector", (lambda c=c: (lambda v: v.reduce_max(
                col(TMPB)[:, c:c + 1], btf.ap[:, 128:256], axis=AX.X)))(),
                wait=rb, inc="v")
            btf.note_r(tr); btf.note_r(tr2)
            s2r += [tr, tr2]
        tmrg = em.on("vector", lambda v: v.tensor_max(col(TMPA), col(TMPA), col(TMPB)),
                     wait=s2r, inc="v")
        t1 = em.on("vector", lambda v: v.tensor_scalar(col(TMPB), col(TMPA), EPS, None, OP.max),
                   wait=[tmrg], inc="v")
        t2 = em.on("vector", lambda v: v.reciprocal(col(TMPC), col(TMPB)), wait=[t1], inc="v")
        tok_s2 = em.on("vector", lambda v: v.tensor_scalar(col(S2C), col(TMPC), 127.0, None, OP.mult),
                       wait=[t2], inc="v")
        tok_is2 = em.on("vector", lambda v: v.tensor_scalar(
            col(IS2), col(TMPB), col(ISW2)[:, 0:1], 1.0 / 127.0, OP.mult, OP.mult),
            wait=[t1, tok_isw2], inc="v")
        tok_s2b = bcast_from_col(col(S2C), rF2, F2, rF2.w() + rRWM.w() + [tok_s2])

        hb_rd = Hb.rearrange("(h p) (g t) -> p h g t", p=128, t=512)
        out_r = out_d.rearrange("(c p) d -> c p d", p=128)
        rR2 = Reg()
        rR2.note_w(tok_xq)
        rSQS = Reg()
        rZH = Reg()
        rOUT2 = [Reg() for _ in range(4)]
        rOSTG = Reg()
        out_toks = []
        for g_ in range(4):
            tdg_ = em.on("sync", (lambda g_=g_: (lambda sy: sy.dma_start(
                out=R2[:, 0:NH * 512].rearrange("p (h t) -> p h t", h=NH),
                in_=hb_rd[:, :, g_, :])))(),
                wait=rR2.w() + [hb_done, pass1_t], inc="dR", amt=16)
            rR2 = Reg(); rR2.note_w(tdg_)
            q_toks = []
            for h in range(NH):
                tq1 = em.on("vector", (lambda h=h, g_=g_: (lambda v: v.tensor_tensor(
                    ZH32[:, :], R2[:, h * 512:(h + 1) * 512],
                    F2[:, g_ * 512:(g_ + 1) * 512], OP.mult)))(),
                    wait=rZH.w() + rR2.r() + [tok_s2b], inc="v")
                rR2.note_r(tq1)
                rF2.note_r(tq1)
                rZH.note_w(tq1)
                tq2 = em.on("vector", (lambda h=h: (lambda v: v.tensor_copy(
                    R2[:, h * 512:(h + 1) * 512], ZH32[:, :])))(),
                    wait=rZH.r() + [rR2.lw], inc="v")
                rZH.note_r(tq2)
                rR2.note_w(tq2)
                q_toks.append(tq2)
            hq_tok = q_toks[-1]
            for tl in range(4):
                c = g_ * 4 + tl
                ev2 = []
                banks2 = [pb[(tl * 2) % 4], pb[(tl * 2 + 1) % 4]]
                wbs = [b.w() for b in banks2]
                mms = [[], []]
                for h in range(NH):
                    for dch in range(2):
                        tm = em.on("tensor", (lambda h=h, tl=tl, dch=dch, bank=banks2[dch]: (lambda t: t.matmul(
                            bank.ap[:, 0:512], R2[:, h * 512 + tl * 128: h * 512 + (tl + 1) * 128],
                            W2Q[:, h * D + dch * 512: h * D + (dch + 1) * 512],
                            start=(h == 0), stop=(h == NH - 1))))(),
                            wait=(wbs[dch] if h == 0 else []) + [q_toks[h], w2q_toks[h]],
                            inc="t")
                        mms[dch].append(tm)
                for dch in range(2):
                    banks2[dch].note_w(mms[dch][-1])
                    rR2.note_r(mms[dch][-1])
                for dch in range(2):
                    bank = banks2[dch]
                    tv = em.on("vector", (lambda c=c, tl=tl, dch=dch, bank=bank: (lambda v: v.scalar_tensor_tensor(
                        OUT2[:, tl * D + dch * 512: tl * D + (dch + 1) * 512],
                        bank.ap[:, 0:512], col(IS2)[:, c:c + 1],
                        B2B[:, dch * 512:(dch + 1) * 512], OP.mult, OP.add,
                        accum_out=col(TMPA)[:, dch:dch + 1])))(),
                        wait=bank.r() + (rOUT2[tl].w() if dch == 0 else []) +
                             [tok_is2, tok_b2b], inc="v")
                    bank.note_r(tv)
                    ev2.append(tv)
                rOUT2[tl] = Reg(); rOUT2[tl].note_w(ev2[-1])
                for e in ev2:
                    rOUT2[tl].note_w(e)
                tsum = em.on("vector", (lambda c=c: (lambda v: v.tensor_add(
                    col(SUM5)[:, c:c + 1], col(TMPA)[:, 0:1], col(TMPA)[:, 1:2])))(),
                    wait=ev2, inc="v")
                tsq = em.on("scalar", (lambda c=c, tl=tl: (lambda s: s.activation(
                    SQS[:, :], OUT2[:, tl * D:(tl + 1) * D], AF.Square,
                    accum_out=col(SQ5)[:, c:c + 1])))(), wait=ev2 + rSQS.w(), inc="s")
                rSQS.note_w(tsq)
                rOUT2[tl].note_r(tsq)
                t1 = em.on("vector", (lambda c=c: (lambda v: v.tensor_scalar(
                    col(MU5)[:, c:c + 1], col(SUM5)[:, c:c + 1], 1.0 / D, None, OP.mult)))(),
                    wait=[tsum], inc="v")
                t2 = em.on("vector", (lambda c=c: (lambda v: v.tensor_scalar(
                    col(TMPB)[:, 0:1], col(SQ5)[:, c:c + 1], 1.0 / D, None, OP.mult)))(),
                    wait=[tsq], inc="v")
                t3 = em.on("vector", (lambda c=c: (lambda v: v.tensor_mul(
                    col(TMPB)[:, 1:2], col(MU5)[:, c:c + 1], col(MU5)[:, c:c + 1])))(),
                    wait=[t1], inc="v")
                t4 = em.on("vector", lambda v: v.tensor_sub(
                    col(TMPB)[:, 0:1], col(TMPB)[:, 0:1], col(TMPB)[:, 1:2]),
                    wait=[t2, t3], inc="v")
                t5 = em.on("scalar", lambda s: s.activation(
                    col(TMPB)[:, 1:2], col(TMPB)[:, 0:1], AF.Sqrt, bias=EPSC),
                    wait=[t4], inc="s")
                t6 = em.on("vector", (lambda c=c: (lambda v: v.reciprocal(
                    col(R5C)[:, c:c + 1], col(TMPB)[:, 1:2])))(), wait=[t5], inc="v")
                t7 = em.on("vector", (lambda c=c: (lambda v: v.scalar_tensor_tensor(
                    col(NM5)[:, c:c + 1], col(MU5)[:, c:c + 1], -1.0, col(R5C)[:, c:c + 1],
                    OP.mult, OP.mult)))(), wait=[t6], inc="v")
                t8 = em.on("vector", (lambda c=c, tl=tl: (lambda v: v.tensor_scalar(
                    OUT2[:, tl * D:(tl + 1) * D], OUT2[:, tl * D:(tl + 1) * D],
                    col(R5C)[:, c:c + 1], col(NM5)[:, c:c + 1], OP.mult, OP.add)))(),
                    wait=[t7, tsq], inc="v")
                rOUT2[tl].note_w(t8)
                t9 = em.on("vector", (lambda c=c, tl=tl: (lambda v: v.scalar_tensor_tensor(
                    OSTG[:, 0:D], OUT2[:, tl * D:(tl + 1) * D], 1.0,
                    R1[:, c * D:(c + 1) * D], OP.mult, OP.add)))(),
                    wait=rOSTG.w() + [t8, w2q_toks[-1]], inc="v")
                rOUT2[tl].note_r(t9)
                rOSTG.note_w(t9)
                to = em.on("sync", (lambda c=c: (lambda sy: sy.dma_start(
                    out=out_r[c], in_=OSTG[:, 0:D])))(),
                    wait=rOSTG.r(), inc="out", amt=16)
                rOSTG.note_r(to)
                out_toks.append(to)

        tap_points = {
            "x2": (R1[:, 0:4096], 4096),
            "qT": (qT[:, 0:2048], 2048),
            "kT": (kT[:, 0:2048], 2048),
            "vT": (vT[:, 0:2048], 2048),
            "gT": (gT[:, 0:2048], 2048),
            "lj": (S1[:, 0:2048], 2048),
            "gtb": (GTB[:, 0:2048], 2048),
            "hcb": (HCB[:, 0:2048], 2048),
            "ucb": (UCB[:, 0:4096], 4096),
            "zin": (ZBt[:, 0:128], 128),
            "xq": (R2[:, 0:4096], 4096),
            "w2q": (W2Q[:, 0:4096], 4096),
            "co": (CO[:, 0:768], 768),
            "ytl": (YTL[:, 0:2048], 2048),
            "f2": (F2[:, 0:2048], 2048),
            "gup": (GUP[:, 0:2048], 2048),
            "qtt": (QTT[:, 0:2048], 2048),
        }
        if tap is not None and tap in tap_points:
            region, width = tap_points[tap]
        else:
            region, width = CO[:, 0:768], 768
        em.on("gpsimd", (lambda region=region, width=width: (lambda g: g.dma_start(
            out=dbg_d[:, 0:width], in_=region)))(),
            wait=em.barrier(), inc="dG", amt=16)
        em.on("sync", None, wait=[("dG", em.cnt["dG"])], inc=None)
        em.on("sync", None, wait=[("out", em.cnt["out"])], inc=None)

        block = es.enter_context(nc.Block())

        @block.sync
        def _(sy):
            em.run("sync", sy)

        @block.gpsimd
        def _(g):
            em.run("gpsimd", g)

        @block.vector
        def _(v):
            em.run("vector", v)

        @block.scalar
        def _(s):
            em.run("scalar", s)

        @block.tensor
        def _(t):
            em.run("tensor", t)

    return nc


_NC = None
TAP = None
DBG_OUT = {}


def kernel(**inputs):
    global _NC
    f32 = lambda a: np.ascontiguousarray(np.asarray(a, dtype=np.float32))
    x = f32(inputs["L0_attn_x"])
    Wq = f32(inputs["L0_attn_Wq"]); bq = f32(inputs["L0_attn_bq"])
    Wk = f32(inputs["L0_attn_Wk"]); bk = f32(inputs["L0_attn_bk"])
    Wv = f32(inputs["L0_attn_Wv"]); bv = f32(inputs["L0_attn_bv"])
    Wa = f32(inputs["L0_attn_Wa"]); ba = f32(inputs["L0_attn_ba"])
    Wb = f32(inputs["L0_attn_Wb"]); bb = f32(inputs["L0_attn_bb"])
    Wg = f32(inputs["L0_attn_Wg"]); bg = f32(inputs["L0_attn_bg"])
    Wo = f32(inputs["L0_attn_Wo"]); bo = f32(inputs["L0_attn_bo"])
    w1 = f32(inputs["L0_ffn_w1"]); b1 = f32(inputs["L0_ffn_b1"])
    w2 = f32(inputs["L0_ffn_w2"]); b2 = f32(inputs["L0_ffn_b2"])
    gamma = f32(inputs["L0_ffn_gamma"]); beta = f32(inputs["L0_ffn_beta"])

    Wab = np.ascontiguousarray(np.concatenate([Wa, Wb], axis=1))
    bab = np.ascontiguousarray(np.stack([ba[0], bb[0]])[:, None])
    W1T = np.ascontiguousarray(w1.T)
    W2T = np.ascontiguousarray(w2.T)
    bcolf = lambda a: np.ascontiguousarray(a[:, None])
    brd = lambda a: np.ascontiguousarray(np.broadcast_to(a[None, :], (128, a.shape[0])))
    b1T = np.ascontiguousarray(b1.reshape(NH, 128).T)
    idf = np.eye(128, dtype=np.float32)
    idb = np.eye(128, dtype=ml_dtypes.bfloat16)
    ii = np.arange(128)
    maskW = np.where(ii[:, None] > ii[None, :], 0.0, POS).astype(np.float32)
    maskU = np.where(ii[None, :] >= ii[:, None], 0.0, NEG).astype(np.float32)

    global _NC
    if _NC is None:
        _NC = build_nc(TAP)
    nc = _NC

    in_maps = []
    for core in range(8):
        b, s = core // 4, core % 4
        xs = np.ascontiguousarray(x[b, s * TOK:(s + 1) * TOK, :])
        xTs = np.ascontiguousarray(xs.T)
        sm = np.zeros((128, 6), np.float32)
        for p in range(3):
            m = 1.0 if p < s else 0.0
            sm[:, p] = m
            sm[:, 3 + p] = 1.0 - m
        in_maps.append(dict(
            x=xs, xT=xTs, Wq=Wq, Wk=Wk, Wv=Wv, Wg=Wg, Wab=Wab, Wo=Wo,
            bq=bcolf(bq), bk=bcolf(bk), bv=bcolf(bv), bg=bcolf(bg), bab=bab,
            bo_b=brd(bo), b2_b=brd(b2), gam_b=brd(gamma), bet_b=brd(beta),
            b1T=b1T, W1T=W1T, W2T=W2T, idf=idf, idb=idb,
            maskW=maskW, maskU=maskU, smask=sm,
        ))
    res = run_bass_kernel_spmd(nc, in_maps, core_ids=list(range(8)))
    out = np.empty_like(x)
    for core in range(8):
        b, s = core // 4, core % 4
        out[b, s * TOK:(s + 1) * TOK, :] = res.results[core]["out"]
    DBG_OUT.clear()
    for core in range(8):
        DBG_OUT[core] = np.asarray(res.results[core]["dbg"])
    return out


# version 21
# speedup vs baseline: 1.0152x; 1.0152x over previous
import numpy as np
import ml_dtypes
import concourse.bass as bass
import concourse.mybir as mybir
from concourse.bass_utils import run_bass_kernel_spmd

F32 = mybir.dt.float32
BF16 = mybir.dt.bfloat16
I32 = mybir.dt.int32
AF = mybir.ActivationFunctionType
OP = mybir.AluOpType
AX = mybir.AxisListType

TOK = 2048
D = 1024
C = 128
HID = 4096
NT = 16
ND = 8
NH = 32
EPS = 1e-5
NEG = -1e30
POS = 1e30

ENGS = ("sync", "gpsimd", "vector", "scalar", "tensor")


class Emitter:
    def __init__(self):
        self.q = {e: [] for e in ENGS}
        self.cnt = {}
        self.sems = {}

    def add_sem(self, name, handle):
        self.sems[name] = handle
        self.cnt[name] = 0

    def on(self, eng, fn, wait=(), inc=None, amt=1):
        tok = None
        if inc is not None:
            self.cnt[inc] += amt
            tok = (inc, self.cnt[inc])
        wl = tuple(t for t in wait if t is not None)
        self.q[eng].append((fn, wl, tok, amt))
        return tok

    def barrier(self):
        return [(s, self.cnt[s]) for s in self.cnt]

    def run(self, eng_name, eng_obj):
        mark = {}
        for fn, waits, tok, amt in self.q[eng_name]:
            seen = {}
            for (s, v) in waits:
                if v > 0 and v > mark.get(s, 0) and seen.get(s, -1) < v:
                    seen[s] = v
            for s, v in seen.items():
                eng_obj.wait_ge(self.sems[s], v)
                mark[s] = v
            if fn is None:
                continue
            inst = fn(eng_obj)
            if tok is not None:
                inst.then_inc(self.sems[tok[0]], amt)


class Bank:

    def __init__(self, ap):
        self.ap = ap
        self.rd = []
        self.wr = []
        self.mode = "r"

    def w(self):
        if self.mode == "r":
            out = list(self.rd)
            self.rd = []
            self.wr = []
            self.mode = "w"
            return out
        return []

    def note_w(self, tok):
        self.wr.append(tok)

    def r(self):
        if self.mode == "w":
            self.mode = "r"
        return list(self.wr)

    def note_r(self, tok):
        self.rd.append(tok)


class Reg:

    def __init__(self, ap=None):
        self.ap = ap
        self.lw = None
        self.rd = []

    def w(self):
        out = [self.lw] + self.rd
        self.rd = []
        return out

    def note_w(self, tok):
        self.lw = tok

    def r(self):
        return [self.lw]

    def note_r(self, tok):
        self.rd.append(tok)


def build_nc(tap=None):
    nc = bass.Bass("TRN2", target_bir_lowering=False)

    def din(name, shp, dt=F32):
        return nc.dram_tensor(name, shp, dt, kind="ExternalInput").ap()

    x_d = din("x", [TOK, D])
    xT_d = din("xT", [D, TOK])
    Wq_d = din("Wq", [D, C]); Wk_d = din("Wk", [D, C])
    Wv_d = din("Wv", [D, C]); Wg_d = din("Wg", [D, C])
    Wab_d = din("Wab", [D, 2])
    Wo_d = din("Wo", [C, D])
    bq_d = din("bq", [C, 1]); bk_d = din("bk", [C, 1])
    bv_d = din("bv", [C, 1]); bg_d = din("bg", [C, 1])
    bab_d = din("bab", [2, 1])
    bo_b_d = din("bo_b", [128, D])
    b2_b_d = din("b2_b", [128, D])
    gam_b_d = din("gam_b", [128, D])
    bet_b_d = din("bet_b", [128, D])
    b1T_d = din("b1T", [128, NH])
    W1T_d = din("W1T", [D, HID])
    W2T_d = din("W2T", [HID, D])
    idf_d = din("idf", [128, 128])
    idb_d = din("idb", [128, 128], BF16)
    maskW_d = din("maskW", [128, 128])
    maskU_d = din("maskU", [128, 128])
    smask_d = din("smask", [128, 6])
    out_d = nc.dram_tensor("out", [TOK, D], F32, kind="ExternalOutput").ap()
    dbg_d = nc.dram_tensor("dbg", [128, 4096], F32, kind="ExternalOutput").ap()

    cc_in = nc.dram_tensor("cc_in", [128, 256], F32).ap()
    cc_out = nc.dram_tensor("cc_out", [512, 256], F32).ap()
    Hb = nc.dram_tensor("Hb", [HID, TOK], BF16).ap()

    from contextlib import ExitStack
    es = ExitStack()
    SB = lambda n, s, d=F32: es.enter_context(nc.sbuf_tensor(n, s, d))
    PS = lambda n, s, d=F32: es.enter_context(nc.psum_tensor(n, s, d))

    with es:
        FA = SB("FA", [128, 12864], F32)

        def fa(off, ln):
            return FA[:, off:off + ln]
        CO = fa(0, 768)
        RW = fa(768, 2048)
        ONES = fa(2816, 128)
        IDF = fa(2944, 128)
        MKW = fa(3072, 128)
        MKU = fa(3200, 128)
        SMK = fa(3328, 6)
        BCOL = fa(3336, 8)
        B1T = fa(3352, 32)
        S1 = fa(3392, 2048)
        WMS = S1
        F1 = fa(5440, 2048)
        RCV = fa(5440, 768)
        CCP = fa(6208, 256)
        ZI = fa(6464, 256)
        GPT = fa(6720, 128)
        HPT = fa(6848, 128)
        F2 = fa(7488, 2048)
        OUT2 = fa(3392, 4096)
        TR1_ = fa(9536, 128)
        TR2_ = fa(9664, 128)
        YTL = fa(9792, 2048)
        W1S = YTL
        W2S = fa(11840, 1024)
        OSTG = fa(11840, 1024)

        BA = SB("BA", [128, 79104], BF16)

        def ba(off, ln):
            return BA[:, off:off + ln]
        R1 = ba(0, NT * D)
        R2 = ba(16384, ND * TOK)
        AT1o = 32768
        qT = ba(AT1o + 0, 2048)
        kT = ba(AT1o + 2048, 2048)
        vT = ba(AT1o + 4096, 2048)
        gT = ba(AT1o + 6144, 2048)
        UCB = ba(AT1o + 8192, 4096)
        PTBs = ba(AT1o + 12288, 2048)
        ULT = ba(AT1o + 14336, 2048)
        GTB = ba(AT1o + 16384, 2048)
        HCB = ba(AT1o + 18432, 2048)
        HTB = ba(AT1o + 20480, 2048)
        KNB = ba(AT1o + 22528, 2048)
        VNB = ba(AT1o + 24576, 2048)
        KPB = ba(AT1o + 26624, 2048)
        GUP = ba(AT1o + 28672, 2048)
        QTT = ba(AT1o + 30720, 2048)
        W2Q = ba(AT1o, NH * D)
        BF2o = 65536
        WQ = ba(BF2o + 0, 1024)
        WK = ba(BF2o + 1024, 1024)
        WV = ba(BF2o + 2048, 1024)
        WG = ba(BF2o + 3072, 1024)
        WO = ba(BF2o + 4096, 1024)
        WAB = ba(BF2o + 5120, 16)
        GB = ba(BF2o + 0, 2048)
        XQN = ba(BF2o + 2048, 2048)
        HSL = ba(BF2o + 0, 2 * TOK)
        W1Q = ba(BF2o + 4096, 2048)
        BF3o = 71680
        WCH = ba(BF3o + 0, 128)
        XP = ba(BF3o + 128, 256)
        YS = ba(BF3o + 384, 896)
        TPP = ba(BF3o + 1280, 512)
        MPT = ba(BF3o + 1792, 256)
        UFT = ba(BF3o + 2048, 256)
        ZBt = ba(BF3o + 2304, 256)
        ZTt = ba(BF3o + 2560, 256)
        GA = ba(BF3o + 2816, 256)
        GAT = ba(BF3o + 3072, 256)
        HA = ba(BF3o + 3328, 256)
        IDB = ba(BF3o + 3584, 128)
        ONEB = ba(BF3o + 3712, 128)
        GMB = ba(BF3o + 3840, 1024)
        BTB = ba(BF3o + 4864, 1024)
        BOB = ba(BF3o + 5888, 1024)
        B2B = BOB
        SQS = ba(BF2o + 4096, 1024)

        IA = SB("IA", [128, 512], I32)
        ZI32 = IA[:, 0:512]
        ZH32 = IA[:, 0:512]

        pb = [Bank(PS(f"pp{i}", [128, 512], F32)) for i in range(6)]
        btf = Bank(PS("ptf", [128, 512], F32))
        btb = Bank(PS("ptb", [128, 1024], BF16))

        em = Emitter()
        for s in ("dS", "dW0", "dW1", "dX", "dM0", "dM1", "dR0", "dR1", "dR2", "dR3", "dG", "dA", "v", "s", "t", "g", "cc", "out"):
            em.add_sem(s, es.enter_context(nc.semaphore(s)))

        def col(i, w=16):
            return CO[:, i * 16: i * 16 + w]

        def do_tap(name, region, width, waits):
            if tap == name:
                em.on("gpsimd", (lambda region=region, width=width: (lambda g: g.dma_start(
                    out=dbg_d[:, 0:width], in_=region)))(),
                    wait=list(waits), inc="dG", amt=16)

        (SUM1, SQ1, MU1, R1C, MR1, V1A, V1B,
         SUM2, SQ2, MU2, R2C, NM2,
         SUM3, SQ3, MU3, RHO, NMR, SXC, ISX,
         S2C, IS2, SUM5, SQ5, MU5, R5C, NM5,
         LCOL, BCOLc, GCOL, BGC, LLB, EKP, GLR,
         WM1, WM2, SW1C, ISW1, SW2C, ISW2,
         TMPA, TMPB, TMPC, WMT) = range(43)
        NCS = 43

        x_r = x_d.rearrange("(t p) d -> p t d", p=128)
        tok_x = em.on("gpsimd", lambda g: g.dma_start(
            out=R1[:, :].rearrange("p (t d) -> p t d", t=NT), in_=x_r), inc="dG", amt=16)
        xT_r = xT_d.rearrange("(k p) t -> p k t", p=128)
        tok_xT = em.on("gpsimd", lambda g: g.dma_start(
            out=R2[:, :].rearrange("p (k t) -> p k t", k=ND), in_=xT_r), inc="dG", amt=16)

        def ldw(dst, src, m):
            src_r = src.rearrange("(k p) c -> p k c", p=128)
            return lambda g: g.dma_start(
                out=dst[:, :].rearrange("p (k c) -> p k c", k=ND), in_=src_r)
        tok_wq = em.on("gpsimd", ldw(WQ, Wq_d, C), inc="dG", amt=16)
        tok_wk = em.on("gpsimd", ldw(WK, Wk_d, C), inc="dG", amt=16)
        tok_wv = em.on("gpsimd", ldw(WV, Wv_d, C), inc="dG", amt=16)
        tok_wg = em.on("gpsimd", ldw(WG, Wg_d, C), inc="dG", amt=16)
        tok_wab = em.on("gpsimd", ldw(WAB, Wab_d, 2), inc="dG", amt=16)
        tok_wo = em.on("gpsimd", lambda g: g.dma_start(out=WO[:, :], in_=Wo_d), inc="dG", amt=16)

        csts = []
        for dst, src in [(IDF, idf_d), (MKW, maskW_d), (MKU, maskU_d),
                         (B1T, b1T_d)]:
            csts.append(em.on("sync", (lambda dst=dst, src=src: (
                lambda sy: sy.dma_start(out=dst[:, :], in_=src)))(), inc="dS", amt=16))
        CIDF, CMKW, CMKU, CB1T = csts
        CBOB = em.on("gpsimd", lambda g: g.dma_start(out=BOB[:, :], in_=bo_b_d), inc="dG", amt=16)
        CGMB = em.on("gpsimd", lambda g: g.dma_start(out=GMB[:, :], in_=gam_b_d), inc="dG", amt=16)
        CBTB = em.on("gpsimd", lambda g: g.dma_start(out=BTB[:, :], in_=bet_b_d), inc="dG", amt=16)
        tok_idb = em.on("sync", lambda sy: sy.dma_start(out=IDB[:, :], in_=idb_d), inc="dS", amt=16)
        tok_smk = em.on("sync", lambda sy: sy.dma_start(out=SMK[:, :], in_=smask_d), inc="dS", amt=16)
        for i, src in enumerate([bq_d, bk_d, bv_d, bg_d]):
            csts.append(em.on("sync", (lambda i=i, src=src: (
                lambda sy: sy.dma_start(out=BCOL[:, i:i + 1], in_=src)))(), inc="dS", amt=16))
        tok_bab = em.on("sync", lambda sy: sy.dma_start(out=BCOL[0:2, 4:5], in_=bab_d), inc="dS", amt=16)
        csts.append(tok_bab)
        tok_cst = csts[-1]

        tok_ones = em.on("vector", lambda v: v.memset(ONES[:, :], 1.0), inc="v")
        tok_epsc = em.on("vector", lambda v: v.memset(BCOL[:, 5:6], EPS), wait=[tok_bab], inc="v")
        EPSC = BCOL[:, 5:6]
        tok_oneb = em.on("vector", lambda v: v.memset(ONEB[:, 0:1], 1.0), inc="v")
        tok_negb = em.on("vector", lambda v: v.memset(ONEB[:, 1:2], -1.0), inc="v")

        rS1 = Reg()
        st_toks = []
        for t in range(NT):
            ts_ = em.on("vector", (lambda t=t: (lambda v: v.reduce_sum(
                col(SUM1)[:, t:t + 1], R1[:, t * D:(t + 1) * D], axis=AX.X)))(),
                wait=[tok_x], inc="v")
            tq = em.on("scalar", (lambda t=t: (lambda s: s.activation(
                S1[:, 0:D], R1[:, t * D:(t + 1) * D], AF.Square,
                accum_out=col(SQ1)[:, t:t + 1])))(), wait=[tok_x] + rS1.w(), inc="s")
            rS1.note_w(tq)
            st_toks += [ts_, tq]
        t_mu1 = em.on("vector", lambda v: v.tensor_scalar(
            col(MU1), col(SUM1), 1.0 / D, None, OP.mult), wait=st_toks, inc="v")
        t_e2 = em.on("vector", lambda v: v.tensor_scalar(
            col(V1A), col(SQ1), 1.0 / D, None, OP.mult), wait=st_toks, inc="v")
        t_m2 = em.on("vector", lambda v: v.tensor_mul(col(V1B), col(MU1), col(MU1)),
                     wait=[t_mu1], inc="v")
        t_v1 = em.on("vector", lambda v: v.tensor_sub(col(V1A), col(V1A), col(V1B)),
                     wait=[t_e2, t_m2], inc="v")
        t_sd = em.on("scalar", lambda s: s.activation(col(V1B), col(V1A), AF.Sqrt, bias=EPSC),
                     wait=[t_v1, tok_epsc], inc="s")
        t_r1 = em.on("vector", lambda v: v.reciprocal(col(R1C), col(V1B)), wait=[t_sd], inc="v")
        t_mr1 = em.on("vector", lambda v: v.tensor_mul(col(MR1), col(MU1), col(R1C)),
                      wait=[t_r1], inc="v")

        rROW0 = Reg()
        rRW16 = Reg()

        def bcast_from_col(col_ap, dstreg, dst_ap, pre_toks):
            wb = btf.w()
            tt = em.on("tensor", (lambda col_ap=col_ap: (lambda t: t.transpose(
                btf.ap[0:16, 0:128], col_ap, IDF[:, :])))(),
                wait=wb + list(pre_toks) + [CIDF], inc="t")
            btf.note_w(tt)
            tv = em.on("vector", lambda v: v.tensor_copy(RW[32:48, 384:512], btf.ap[0:16, 0:128]),
                       wait=btf.r() + rRW16.w(), inc="v")
            btf.note_r(tv)
            rRW16.note_w(tv)
            td = em.on("gpsimd", lambda g: g.dma_start(out=RW[0:1, 0:2048], in_=RW[32:48, 384:512]),
                       wait=rRW16.r() + rROW0.w(), inc="dG", amt=16)
            rRW16.note_r(td)
            rROW0.note_w(td)
            return bcast_from_row(RW[0:1, 0:2048], rROW0, dstreg, dst_ap, [td])

        def bcast_from_row(row_ap, rowreg, dstreg, dst_ap, pre_toks):
            last = None
            wdst = dstreg.w()
            for nchk in range(4):
                wb = pb[5].w()
                tm = em.on("tensor", (lambda nchk=nchk, row_ap=row_ap: (lambda t: t.matmul(
                    pb[5].ap[:, 0:512], ONES[0:1, :], row_ap[0:1, nchk * 512:(nchk + 1) * 512],
                    start=True, stop=True)))(),
                    wait=wb + list(pre_toks) + [tok_ones], inc="t")
                pb[5].note_w(tm)
                te = em.on("vector", (lambda nchk=nchk, dst_ap=dst_ap: (lambda v: v.tensor_copy(
                    dst_ap[:, nchk * 512:(nchk + 1) * 512], pb[5].ap[:, 0:512])))(),
                    wait=pb[5].r() + (wdst if nchk == 0 else []), inc="v")
                pb[5].note_r(te)
                if rowreg is not None:
                    rowreg.note_r(te)
                last = te
            dstreg.note_w(last)
            return last

        rF1 = Reg()
        rF2 = Reg()
        tok_r1b = bcast_from_col(col(R1C), rF1, F1, [t_r1])
        tok_mr1b = bcast_from_col(col(MR1), rF2, F2, [t_mr1])

        def colsum(Wt, slot, M, wtok):
            wb = btf.w()
            mm = []
            for k in range(ND):
                tm = em.on("tensor", (lambda k=k, Wt=Wt, M=M: (lambda t: t.matmul(
                    btf.ap[0:M, 2:3], Wt[:, k * M:(k + 1) * M], ONEB[:, 1:2],
                    start=(k == 0), stop=(k == ND - 1))))(),
                    wait=(wb if k == 0 else []) + [wtok, tok_negb] + (mm[-1:] if mm else []),
                    inc="t")
                mm.append(tm)
            btf.note_w(mm[-1])
            te = em.on("vector", (lambda M=M, slot=slot: (lambda v: v.tensor_copy(
                col(slot)[0:M, 0:1], btf.ap[0:M, 2:3])))(), wait=btf.r(), inc="v")
            btf.note_r(te)
            return te
        tok_csq = colsum(WQ, NCS + 0, C, tok_wq)
        tok_csk = colsum(WK, NCS + 1, C, tok_wk)
        tok_csv = colsum(WV, NCS + 2, C, tok_wv)
        tok_csg = colsum(WG, NCS + 3, C, tok_wg)
        tok_csab = colsum(WAB, NCS + 4, 2, tok_wab)

        rYTL = Reg()

        def project(Wt, M, cstok, cs_slot, bias_idx, act, dst_ap, wtok, dstreg, xw=()):
            wy = rYTL.w()
            ev = []
            pair_mms = {}
            for pair in range(2):
                banks = [pb[2 * pair], pb[2 * pair + 1]]
                wbs = [b.w() for b in banks]
                mms = [[], []]
                for k in range(ND):
                    for j in range(2):
                        nchk = 2 * pair + j
                        tm = em.on("tensor", (lambda k=k, nchk=nchk, bank=banks[j], Wt=Wt, M=M: (lambda t: t.matmul(
                            bank.ap[0:M, 0:512], Wt[:, k * M:(k + 1) * M],
                            R2[:, k * TOK + nchk * 512: k * TOK + (nchk + 1) * 512],
                            start=(k == 0), stop=(k == ND - 1))))(),
                            wait=(wbs[j] if k == 0 else []) + [wtok, tok_xT],
                            inc="t")
                        mms[j].append(tm)
                for j in range(2):
                    banks[j].note_w(mms[j][-1])
                    pair_mms[2 * pair + j] = mms[j]
            for nchk in range(4):
                bank = pb[nchk]
                te1 = em.on("vector", (lambda nchk=nchk, bank=bank, M=M: (lambda v: v.tensor_mul(
                    YTL[0:M, nchk * 512:(nchk + 1) * 512], bank.ap[0:M, 0:512],
                    F1[0:M, nchk * 512:(nchk + 1) * 512])))(),
                    wait=bank.r() + [tok_r1b] + (wy if nchk == 0 else []), inc="v")
                bank.note_r(te1)
                rF1.note_r(te1)
                te2 = em.on("vector", (lambda nchk=nchk, M=M, cs_slot=cs_slot: (lambda v: v.scalar_tensor_tensor(
                    YTL[0:M, nchk * 512:(nchk + 1) * 512],
                    F2[0:M, nchk * 512:(nchk + 1) * 512], col(cs_slot)[0:M, 0:1],
                    YTL[0:M, nchk * 512:(nchk + 1) * 512], OP.mult, OP.add)))(),
                    wait=[te1, tok_mr1b, cstok], inc="v")
                rF2.note_r(te2)
                ev.append(te2)
            tse = em.on("scalar", (lambda M=M, bias_idx=bias_idx, act=act, dst_ap=dst_ap: (lambda s: s.activation(
                dst_ap, YTL[0:M, 0:TOK], act, bias=BCOL[0:M, bias_idx:bias_idx + 1])))(),
                wait=ev + list(xw) + [tok_cst], inc="s")
            rYTL.note_w(tse)
            rYTL.note_r(tse)
            if dstreg is not None:
                dstreg.note_w(tse)
            return tse

        tok_qT = project(WQ, C, tok_csq, NCS + 0, 0, AF.Identity, qT[:, 0:TOK], tok_wq, None)
        tok_kTr = project(WK, C, tok_csk, NCS + 1, 1, AF.Identity, S1[:, 0:TOK], tok_wk, rS1)
        tok_vT = project(WV, C, tok_csv, NCS + 2, 2, AF.Identity, vT[:, 0:TOK], tok_wv, None)
        tok_gT = project(WG, C, tok_csg, NCS + 3, 3, AF.Sigmoid, gT[:, 0:TOK], tok_wg, None)
        tok_ab = project(WAB, 2, tok_csab, NCS + 4, 4, AF.Sigmoid, RW[0:2, 0:TOK],
                         tok_wab, None, xw=rROW0.w())
        rROW0.note_w(tok_ab)

        t_k2 = em.on("scalar", lambda s: s.activation(F2[:, 0:TOK], S1[:, 0:TOK], AF.Square),
                     wait=rF2.w() + rS1.r() + [tok_kTr, tok_ab], inc="s")
        rF2.note_w(t_k2)
        rS1.note_r(t_k2)
        kn_last = None
        for nchk in range(4):
            wb = btf.w()
            tm = em.on("tensor", (lambda nchk=nchk: (lambda t: t.matmul(
                btf.ap[0:1, 0:512], ONES[:, 0:1], F2[:, nchk * 512:(nchk + 1) * 512],
                start=True, stop=True)))(), wait=wb + rF2.r() + [tok_ones], inc="t")
            btf.note_w(tm)
            rF2.note_r(tm)
            te = em.on("vector", (lambda nchk=nchk: (lambda v: v.tensor_copy(
                YTL[0:1, nchk * 512:(nchk + 1) * 512], btf.ap[0:1, 0:512])))(),
                wait=btf.r() + (rYTL.w() if nchk == 0 else []), inc="v")
            btf.note_r(te)
            kn_last = te
        rYTL.note_w(kn_last)
        t_ks = em.on("scalar", lambda s: s.activation(YTL[0:1, 0:TOK], YTL[0:1, 0:TOK], AF.Sqrt),
                     wait=rYTL.w(), inc="s")
        rYTL.note_w(t_ks)
        t_kc = em.on("vector", lambda v: v.tensor_scalar(
            YTL[0:1, 0:TOK], YTL[0:1, 0:TOK], 1e-12, None, OP.max), wait=rYTL.w(), inc="v")
        rYTL.note_w(t_kc)
        t_kr = em.on("vector", lambda v: v.reciprocal(YTL[0:1, 0:TOK], YTL[0:1, 0:TOK]),
                     wait=rYTL.w(), inc="v")
        rYTL.note_w(t_kr)
        tok_rkb = bcast_from_row(YTL[0:1, 0:TOK], rYTL, rF2, F2, rF2.w() + [t_kr])
        tok_kT = em.on("vector", lambda v: v.tensor_mul(kT[:, 0:TOK], S1[:, 0:TOK], F2[:, 0:TOK]),
                       wait=[tok_rkb] + rS1.r(), inc="v")
        rS1.note_r(tok_kT)
        rF2.note_r(tok_kT)
        w1_r = W1T_d.rearrange("(k p) h -> p k h", p=128)
        w2_r = W2T_d.rearrange("(k p) dd -> p k dd", p=128)
        w1_slices = [w1_r[:, k, hh * 1024:(hh + 1) * 1024] for k in range(ND) for hh in range(4)]
        w2_slices = [w2_r[:, i, :] for i in range(NH)]
        rWMb = [Reg(), Reg()]
        wmall = []
        wm_parts = {WM1: [], WM2: []}
        wm_sub = {WM1: (SUM1, SQ1), WM2: (MU1, R1C)}
        wm_work = [(a, WM1) for a in w1_slices] + [(a, WM2) for a in w2_slices]
        wm_j = [0]

        def emit_wmean(n):
            for _ in range(n):
                j = wm_j[0]
                if j >= len(wm_work):
                    return
                wm_j[0] += 1
                src_ap, slot = wm_work[j]
                bb_ = j % 2
                td = em.on("sync", (lambda src_ap=src_ap, bb_=bb_: (lambda sy: sy.dma_start(
                    out=F2[:, bb_ * 1024:(bb_ + 1) * 1024], in_=src_ap)))(),
                    wait=rWMb[bb_].w() + ([tok_kT] if j <= 1 else []),
                    inc="dM0" if bb_ == 0 else "dM1", amt=16)
                rWMb[bb_].note_w(td)
                i2 = len(wm_parts[slot])
                tslot = wm_sub[slot][0] if i2 < 16 else wm_sub[slot][1]
                ta = em.on("scalar", (lambda bb_=bb_, tslot=tslot, i2=i2: (lambda s: s.activation(
                    F2[:, bb_ * 1024:(bb_ + 1) * 1024], F2[:, bb_ * 1024:(bb_ + 1) * 1024],
                    AF.Abs, accum_out=col(tslot)[:, i2 % 16:i2 % 16 + 1])))(),
                    wait=rWMb[bb_].w(), inc="s")
                rWMb[bb_].note_w(ta)
                wm_parts[slot].append(ta)
                if len(wm_parts[slot]) == 32:
                    tm_ = em.on("vector", (lambda slot=slot: (lambda v: v.tensor_add(
                        col(slot), col(wm_sub[slot][0]), col(wm_sub[slot][1]))))(),
                        wait=wm_parts[slot], inc="v")
                    wm_parts[slot] = [tm_]
                    wmall.append(tm_)
        emit_wmean(len(wm_work))
        wm1_toks = [wmall[0]]
        wm2_toks = [wmall[1]]

        t_la = em.on("scalar", lambda s: s.activation(RW[0:1, 0:TOK], RW[0:1, 0:TOK], AF.Ln),
                     wait=[tok_ab] + rROW0.w(), inc="s")
        rROW0.note_w(t_la)
        rL16 = [Reg(), Reg()]
        rB16 = Reg()
        t_l16 = em.on("gpsimd", lambda g: g.dma_start(out=RW[32:48, 0:128], in_=RW[0:1, 0:TOK]),
                      wait=rROW0.r() + rL16[0].w(), inc="dG", amt=16)
        rROW0.note_r(t_l16)
        rL16[0].note_w(t_l16)
        t_b16 = em.on("gpsimd", lambda g: g.dma_start(out=RW[32:48, 256:384], in_=RW[1:2, 0:TOK]),
                      wait=[tok_ab] + rB16.w(), inc="dG", amt=16)
        rB16.note_w(t_b16)
        srco, dsto = 0, 1
        for lvl in range(7):
            sft = 1 << lvl
            w = rL16[srco].r() + rL16[dsto].w()
            t_c1 = em.on("vector", (lambda srco=srco, dsto=dsto, sft=sft: (lambda v: v.tensor_copy(
                RW[32:48, dsto * 128:dsto * 128 + sft], RW[32:48, srco * 128:srco * 128 + sft])))(),
                wait=w, inc="v")
            t_c2 = em.on("vector", (lambda srco=srco, dsto=dsto, sft=sft: (lambda v: v.tensor_add(
                RW[32:48, dsto * 128 + sft:dsto * 128 + 128],
                RW[32:48, srco * 128 + sft:srco * 128 + 128],
                RW[32:48, srco * 128:srco * 128 + 128 - sft])))(), wait=w, inc="v")
            rL16[srco].note_r(t_c2)
            rL16[dsto].note_w(t_c2)
            rL16[dsto].note_w(t_c1)
            rL16[dsto].note_w(t_c2)
            srco, dsto = dsto, srco
        l16o = srco * 128
        tok_l16 = rL16[srco].lw

        wb = btf.w()
        t_lc = em.on("tensor", (lambda l16o=l16o: (lambda t: t.transpose(
            btf.ap[:, 0:16], RW[32:48, l16o:l16o + 128], IDF[32:48, 32:48])))(),
            wait=wb + rL16[srco].r() + [CIDF], inc="t")
        btf.note_w(t_lc)
        rL16[srco].note_r(t_lc)
        t_lcv = em.on("vector", lambda v: v.tensor_copy(col(LCOL), btf.ap[:, 0:16]),
                      wait=btf.r(), inc="v")
        btf.note_r(t_lcv)
        wb = btf.w()
        t_bc_ = em.on("tensor", lambda t: t.transpose(
            btf.ap[:, 16:32], RW[32:48, 256:384], IDF[32:48, 32:48]),
            wait=wb + rB16.r() + [CIDF], inc="t")
        btf.note_w(t_bc_)
        rB16.note_r(t_bc_)
        t_bcv = em.on("vector", lambda v: v.tensor_copy(col(BCOLc), btf.ap[:, 16:32]),
                      wait=btf.r(), inc="v")
        btf.note_r(t_bcv)
        t_gc = em.on("scalar", lambda s: s.activation(col(GCOL), col(LCOL), AF.Exp),
                     wait=[t_lcv], inc="s")
        t_bg = em.on("vector", lambda v: v.tensor_mul(col(BGC), col(BCOLc), col(GCOL)),
                     wait=[t_bcv, t_gc], inc="v")
        rRW4 = Reg()
        t_ll = em.on("gpsimd", (lambda l16o=l16o: (lambda g: g.dma_start(
            out=col(V1A)[0:1, 0:16], in_=RW[32:48, l16o + 127:l16o + 128])))(),
            wait=rL16[srco].r() + rRW4.w(), inc="dG", amt=16)
        rL16[srco].note_r(t_ll)
        rRW4.note_w(t_ll)
        wb = btf.w()
        t_llb = em.on("tensor", lambda t: t.matmul(
            btf.ap[:, 32:48], ONES[0:1, :], col(V1A)[0:1, 0:16], start=True, stop=True),
            wait=wb + rRW4.r() + [tok_ones], inc="t")
        btf.note_w(t_llb)
        rRW4.note_r(t_llb)
        t_llbv = em.on("vector", lambda v: v.tensor_copy(col(LLB), btf.ap[:, 32:48]),
                       wait=btf.r(), inc="v")
        btf.note_r(t_llbv)
        t_ek1 = em.on("vector", lambda v: v.tensor_sub(col(TMPC), col(LLB), col(LCOL)),
                      wait=[t_llbv, t_lcv], inc="v")
        t_ekp = em.on("scalar", lambda s: s.activation(col(EKP), col(TMPC), AF.Exp),
                      wait=[t_ek1], inc="s")
        t_glr = em.on("scalar", lambda s: s.activation(col(GLR), col(LLB), AF.Exp),
                      wait=[t_llbv], inc="s")

        t_lrow = em.on("gpsimd", (lambda l16o=l16o: (lambda g: g.dma_start(
            out=RW[0:1, 0:TOK], in_=RW[32:48, l16o:l16o + 128])))(),
            wait=rL16[srco].r() + rROW0.w(), inc="dG", amt=16)
        rL16[srco].note_r(t_lrow)
        rROW0.note_w(t_lrow)
        tok_ljb = bcast_from_row(RW[0:1, 0:TOK], rROW0, rS1, S1, rS1.w() + [t_lrow])
        tok_gb = em.on("scalar", lambda s: s.activation(GB[:, 0:TOK], S1[:, 0:TOK], AF.Exp),
                       wait=rS1.r(), inc="s")
        rS1.note_r(tok_gb)

        rWCH = Reg(); rXP = [Reg(), Reg()]; rYS = [Reg() for _ in range(7)]
        rTPP = [Reg(), Reg()]; rTR1 = Reg(); rTR2 = Reg()
        tgu_toks = [None] * NT
        tqt_toks = [None] * NT
        hcb_toks = [None] * NT
        htb_toks = [None] * NT
        gtb_toks = [None] * NT
        knb_toks = [None] * NT
        ucb_toks = [None] * NT
        ptb_toks = [None] * NT
        ult_toks = [None] * NT
        ga_tok = []
        gacc_cur = 0
        for c in range(NT):
            cC = slice(c * C, (c + 1) * C)
            wb = btb.w()
            tt1 = em.on("tensor", (lambda c=c: (lambda t: t.transpose(
                btb.ap[:, 0:128], kT[:, c * C:(c + 1) * C], IDB[:, :])))(),
                wait=wb + [tok_kT, tok_idb], inc="t")
            tt2 = em.on("tensor", (lambda c=c: (lambda t: t.transpose(
                btb.ap[:, 128:256], vT[:, c * C:(c + 1) * C], IDB[:, :])))(),
                wait=[tok_vT], inc="t")
            btb.note_w(tt1); btb.note_w(tt2)
            rb = btb.r()
            tv1 = em.on("vector", (lambda c=c: (lambda v: v.tensor_copy(
                KNB[:, c * C:(c + 1) * C], btb.ap[:, 0:128])))(), wait=rb, inc="v")
            tv2 = em.on("vector", (lambda c=c: (lambda v: v.tensor_copy(
                VNB[:, c * C:(c + 1) * C], btb.ap[:, 128:256])))(), wait=rb, inc="v")
            btb.note_r(tv1); btb.note_r(tv2)
            knb_toks[c] = tv1
            tkp = em.on("vector", (lambda c=c: (lambda v: v.tensor_scalar(
                KPB[:, c * C:(c + 1) * C], KNB[:, c * C:(c + 1) * C],
                col(EKP)[:, c:c + 1], None, OP.mult)))(), wait=[tv1, t_ekp], inc="v")
            wb = pb[2].w()
            tkk = em.on("tensor", (lambda c=c: (lambda t: t.matmul(
                pb[2].ap[:, 0:128], kT[:, c * C:(c + 1) * C], kT[:, c * C:(c + 1) * C],
                start=True, stop=True)))(), wait=wb + [tok_kT], inc="t")
            pb[2].note_w(tkk)
            td1 = em.on("vector", (lambda c=c: (lambda v: v.tensor_scalar(
                TR1_[:, :], S1[:, c * C:(c + 1) * C], col(LCOL)[:, c:c + 1], None,
                OP.subtract)))(), wait=rTR1.w() + [tok_ljb, t_lcv], inc="v")
            rS1.note_r(td1)
            rTR1.note_w(td1)
            td2 = em.on("vector", lambda v: v.tensor_add(TR2_[:, :], TR1_[:, :], MKW[:, :]),
                        wait=rTR2.w() + rTR1.r() + [CMKW], inc="v")
            rTR1.note_r(td2)
            rTR2.note_w(td2)
            tgs = em.on("scalar", lambda s: s.activation(XP[:, 0:128], TR2_[:, :], AF.Exp, scale=-1.0),
                        wait=rTR2.r() + rXP[0].w(), inc="s")
            rTR2.note_r(tgs)
            rXP[0].note_w(tgs)
            tw = em.on("vector", (lambda c=c: (lambda v: v.scalar_tensor_tensor(
                WCH[:, :], pb[2].ap[:, 0:128], col(BCOLc)[:, c:c + 1], XP[:, 0:128],
                OP.mult, OP.mult)))(), wait=pb[2].r() + rXP[0].r() + rWCH.w() + [t_bcv], inc="v")
            pb[2].note_r(tw)
            rXP[0].note_r(tw)
            rWCH.note_w(tw)
            tdu = em.on("vector", lambda v: v.tensor_add(TR2_[:, :], TR1_[:, :], MKU[:, :]),
                        wait=rTR2.w() + rTR1.r() + [CMKU], inc="v")
            rTR1.note_r(tdu)
            rTR2.note_w(tdu)
            tgu = em.on("scalar", (lambda c=c: (lambda s: s.activation(
                GUP[:, c * C:(c + 1) * C], TR2_[:, :], AF.Exp)))(), wait=rTR2.r(), inc="s")
            rTR2.note_r(tgu)
            tgu_toks[c] = tgu
            tqt = em.on("vector", (lambda c=c: (lambda v: v.tensor_mul(
                QTT[:, c * C:(c + 1) * C], qT[:, c * C:(c + 1) * C],
                GB[:, c * C:(c + 1) * C])))(), wait=[tok_qT, tok_gb], inc="v")
            tqt_toks[c] = tqt
            wb = btb.w()
            ty0 = em.on("tensor", lambda t: t.transpose(btb.ap[:, 256:384], WCH[:, :], IDB[:, :]),
                        wait=wb + rWCH.r() + [tok_idb], inc="t")
            btb.note_w(ty0)
            rWCH.note_r(ty0)
            ty0v = em.on("vector", lambda v: v.tensor_copy(YS[:, 0:128], btb.ap[:, 256:384]),
                         wait=btb.r() + rYS[0].w(), inc="v")
            btb.note_r(ty0v)
            rYS[0].note_w(ty0v)
            w0 = rTPP[0].w()
            tr1 = em.on("vector", (lambda c=c: (lambda v: v.tensor_scalar(
                TPP[:, 0:128], VNB[:, c * C:(c + 1) * C], col(BCOLc)[:, c:c + 1],
                None, OP.mult)))(), wait=w0 + [tv2, t_bcv], inc="v")
            tr2 = em.on("vector", (lambda c=c: (lambda v: v.tensor_scalar(
                TPP[:, 128:256], KNB[:, c * C:(c + 1) * C], col(BGC)[:, c:c + 1],
                None, OP.mult)))(), wait=w0 + [tv1, t_bg], inc="v")
            rTPP[0].note_w(tr1)
            rTPP[0].note_w(tr2)
            if c == 0:
                do_tap("wch0", WCH[:, :], 128, [tw])
                do_tap("ys00", YS[:, 0:128], 128, [ty0v])
                do_tap("tpp0", TPP[:, 0:256], 256, [tr1, tr2])
                do_tap("kk0", pb[2].ap[:, 0:128], 128, [tkk]) if False else None
            xcur_reg, xcur_ap = rWCH, WCH[:, :]
            ycur_reg, ycur_ap = rYS[0], YS[:, 0:128]
            for kk in range(1, 7):
                if kk <= 5:
                    xo = kk % 2
                    wb = pb[3].w()
                    tmx = em.on("tensor", (lambda a=ycur_ap, b=xcur_ap: (lambda t: t.matmul(
                        pb[3].ap[:, 0:128], a, b, start=True, stop=True)))(),
                        wait=wb + ycur_reg.r() + xcur_reg.r(), inc="t")
                    pb[3].note_w(tmx)
                    ycur_reg.note_r(tmx)
                    xcur_reg.note_r(tmx)
                    tvx = em.on("vector", (lambda xo=xo: (lambda v: v.tensor_copy(
                        XP[:, xo * 128:(xo + 1) * 128], pb[3].ap[:, 0:128])))(),
                        wait=pb[3].r() + rXP[xo].w(), inc="v")
                    pb[3].note_r(tvx)
                    rXP[xo].note_w(tvx)
                wb = pb[4].w()
                tmy = em.on("tensor", (lambda a=xcur_ap, b=ycur_ap: (lambda t: t.matmul(
                    pb[4].ap[:, 0:128], a, b, start=True, stop=True)))(),
                    wait=wb + ycur_reg.r() + xcur_reg.r(), inc="t")
                pb[4].note_w(tmy)
                ycur_reg.note_r(tmy)
                xcur_reg.note_r(tmy)
                tvy = em.on("vector", (lambda kk=kk: (lambda v: v.tensor_copy(
                    YS[:, kk * 128:(kk + 1) * 128], pb[4].ap[:, 0:128])))(),
                    wait=pb[4].r() + rYS[kk].w(), inc="v")
                pb[4].note_r(tvy)
                rYS[kk].note_w(tvy)
                if kk <= 5:
                    xcur_reg, xcur_ap = rXP[xo], XP[:, xo * 128:(xo + 1) * 128]
                ycur_reg, ycur_ap = rYS[kk], YS[:, kk * 128:(kk + 1) * 128]
            if c == 0:
                do_tap("ys10", YS[:, 128:256], 128, [])
                do_tap("ysall", YS[:, 0:896], 896, [])
            for i, kk in enumerate(range(6, -1, -1)):
                cur, nxt = i % 2, 1 - (i % 2)
                wb = pb[3].w()
                tm = em.on("tensor", (lambda cur=cur, kk=kk: (lambda t: t.matmul(
                    pb[3].ap[:, 0:256], YS[:, kk * 128:(kk + 1) * 128],
                    TPP[:, cur * 256:(cur + 1) * 256], start=True, stop=True)))(),
                    wait=wb + rYS[kk].r() + rTPP[cur].r(), inc="t")
                pb[3].note_w(tm)
                rYS[kk].note_r(tm)
                rTPP[cur].note_r(tm)
                if kk > 0:
                    tv = em.on("vector", (lambda cur=cur, nxt=nxt: (lambda v: v.tensor_add(
                        TPP[:, nxt * 256:(nxt + 1) * 256], TPP[:, cur * 256:(cur + 1) * 256],
                        pb[3].ap[:, 0:256])))(),
                        wait=pb[3].r() + rTPP[nxt].w() + rTPP[cur].r(), inc="v")
                    rTPP[cur].note_r(tv)
                    rTPP[nxt].note_w(tv)
                else:
                    tv = em.on("vector", (lambda c=c, cur=cur: (lambda v: v.tensor_sub(
                        UCB[:, c * 256:(c + 1) * 256], TPP[:, cur * 256:(cur + 1) * 256],
                        pb[3].ap[:, 0:256])))(), wait=pb[3].r() + rTPP[cur].r(), inc="v")
                    rTPP[cur].note_r(tv)
                pb[3].note_r(tv)
            tok_ucb = tv
            ucb_toks[c] = tok_ucb
            wb = btb.w()
            tpt = em.on("tensor", (lambda c=c: (lambda t: t.transpose(
                btb.ap[:, 384:512], UCB[:, c * 256 + 128:c * 256 + 256], IDB[:, :])))(),
                wait=wb + [tok_ucb], inc="t")
            tut = em.on("tensor", (lambda c=c: (lambda t: t.transpose(
                btb.ap[:, 512:640], UCB[:, c * 256:c * 256 + 128], IDB[:, :])))(),
                wait=[tok_ucb], inc="t")
            btb.note_w(tpt); btb.note_w(tut)
            rb = btb.r()
            tptv = em.on("vector", (lambda c=c: (lambda v: v.tensor_copy(
                PTBs[:, c * C:(c + 1) * C], btb.ap[:, 384:512])))(), wait=rb, inc="v")
            tutv = em.on("vector", (lambda c=c: (lambda v: v.tensor_copy(
                ULT[:, c * C:(c + 1) * C], btb.ap[:, 512:640])))(), wait=rb, inc="v")
            btb.note_r(tptv); btb.note_r(tutv)
            ptb_toks[c] = tptv
            ult_toks[c] = tutv
            wb = pb[2].w()
            thm = em.on("tensor", (lambda c=c: (lambda t: t.matmul(
                pb[2].ap[:, 0:128], KPB[:, c * C:(c + 1) * C],
                UCB[:, c * 256:c * 256 + 128], start=True, stop=True)))(),
                wait=wb + [tkp, tok_ucb], inc="t")
            thtm = em.on("tensor", (lambda c=c: (lambda t: t.matmul(
                pb[2].ap[:, 128:256], UCB[:, c * 256:c * 256 + 128],
                KPB[:, c * C:(c + 1) * C], start=True, stop=True)))(), inc="t")
            tgm = em.on("tensor", (lambda c=c: (lambda t: t.matmul(
                pb[2].ap[:, 256:384], UCB[:, c * 256 + 128:c * 256 + 256],
                KPB[:, c * C:(c + 1) * C], start=True, stop=True)))(), inc="t")
            pb[2].note_w(thm); pb[2].note_w(thtm); pb[2].note_w(tgm)
            rb = pb[2].r()
            thv = em.on("vector", (lambda c=c: (lambda v: v.tensor_copy(
                HCB[:, c * C:(c + 1) * C], pb[2].ap[:, 0:128])))(), wait=rb, inc="v")
            thtv = em.on("vector", (lambda c=c: (lambda v: v.tensor_copy(
                HTB[:, c * C:(c + 1) * C], pb[2].ap[:, 128:256])))(), inc="v")
            tgv = em.on("vector", (lambda c=c: (lambda v: v.scalar_tensor_tensor(
                GTB[:, c * C:(c + 1) * C], IDF[:, :], col(GLR)[:, c:c + 1],
                pb[2].ap[:, 256:384], OP.mult, OP.subtract)))(),
                wait=[t_glr, CIDF], inc="v")
            pb[2].note_r(thv); pb[2].note_r(thtv); pb[2].note_r(tgv)
            hcb_toks[c] = thv
            htb_toks[c] = thtv
            gtb_toks[c] = tgv
            if c == 0:
                wb = btb.w()
                tga = em.on("tensor", lambda t: t.transpose(btb.ap[:, 640:768], GTB[:, 0:C], IDB[:, :]),
                            wait=wb + [tgv], inc="t")
                btb.note_w(tga)
                tgav = em.on("vector", lambda v: v.tensor_copy(GA[:, 0:C], btb.ap[:, 640:768]),
                             wait=btb.r(), inc="v")
                btb.note_r(tgav)
                tgatv = em.on("vector", lambda v: v.tensor_copy(GAT[:, 0:C], GTB[:, 0:C]),
                              wait=[tgv], inc="v")
                thav = em.on("vector", lambda v: v.tensor_copy(HA[:, 0:C], HCB[:, 0:C]),
                             wait=[thv], inc="v")
                ga_tok = [tgav, tgatv, thav]
                gacc_cur = 0
            else:
                prv, nxt = gacc_cur, 1 - gacc_cur
                wb = pb[2].w()
                tm1 = em.on("tensor", (lambda c=c, prv=prv: (lambda t: t.matmul(
                    pb[2].ap[:, 384:512], GTB[:, c * C:(c + 1) * C],
                    GA[:, prv * C:(prv + 1) * C], start=True, stop=True)))(),
                    wait=wb + ga_tok + [tgv], inc="t")
                pb[2].note_w(tm1)
                tv1_ = em.on("vector", (lambda nxt=nxt: (lambda v: v.tensor_copy(
                    GA[:, nxt * C:(nxt + 1) * C], pb[2].ap[:, 384:512])))(),
                    wait=pb[2].r(), inc="v")
                pb[2].note_r(tv1_)
                wb = pb[2].w()
                tm2 = em.on("tensor", (lambda c=c, prv=prv: (lambda t: t.matmul(
                    pb[2].ap[:, 384:512], GA[:, prv * C:(prv + 1) * C],
                    GTB[:, c * C:(c + 1) * C], start=True, stop=True)))(),
                    wait=wb + ga_tok + [tgv], inc="t")
                pb[2].note_w(tm2)
                tv2_ = em.on("vector", (lambda nxt=nxt: (lambda v: v.tensor_copy(
                    GAT[:, nxt * C:(nxt + 1) * C], pb[2].ap[:, 384:512])))(),
                    wait=pb[2].r(), inc="v")
                pb[2].note_r(tv2_)
                wb = pb[2].w()
                tm3 = em.on("tensor", (lambda c=c, prv=prv: (lambda t: t.matmul(
                    pb[2].ap[:, 384:512], GTB[:, c * C:(c + 1) * C],
                    HA[:, prv * C:(prv + 1) * C], start=True, stop=True)))(),
                    wait=wb + ga_tok + [tgv], inc="t")
                pb[2].note_w(tm3)
                tv3_ = em.on("vector", (lambda c=c, nxt=nxt: (lambda v: v.tensor_add(
                    HA[:, nxt * C:(nxt + 1) * C], HCB[:, c * C:(c + 1) * C],
                    pb[2].ap[:, 384:512])))(), wait=pb[2].r() + [thv], inc="v")
                pb[2].note_r(tv3_)
                ga_tok = [tv1_, tv2_, tv3_]
                gacc_cur = nxt

        def wmean_finish(wmslot, swslot, iswslot, nelem, wtok):
            t1 = em.on("vector", (lambda wmslot=wmslot: (lambda v: v.reduce_sum(
                col(WMT)[:, 0:1], col(wmslot), axis=AX.X)))(), wait=[wtok], inc="v")
            wb = btf.w()
            tm = em.on("tensor", lambda t: t.matmul(
                btf.ap[0:1, 0:1], ONES[:, 0:1], col(WMT)[:, 0:1], start=True, stop=True),
                wait=wb + [t1, tok_ones], inc="t")
            btf.note_w(tm)
            t2 = em.on("vector", (lambda nelem=nelem, wmslot=wmslot: (lambda v: v.tensor_scalar(
                col(wmslot)[0:1, 1:2], btf.ap[0:1, 0:1], 1.0 / nelem, EPS, OP.mult, OP.max)))(),
                wait=btf.r(), inc="v")
            btf.note_r(t2)
            wb = btf.w()
            tb = em.on("tensor", (lambda wmslot=wmslot: (lambda t: t.matmul(
                btf.ap[:, 1:2], ONES[0:1, :], col(wmslot)[0:1, 1:2], start=True, stop=True)))(),
                wait=wb + [t2], inc="t")
            btf.note_w(tb)
            t3 = em.on("vector", (lambda iswslot=iswslot: (lambda v: v.tensor_copy(
                col(iswslot)[:, 0:1], btf.ap[:, 1:2])))(), wait=btf.r(), inc="v")
            btf.note_r(t3)
            t4 = em.on("vector", (lambda swslot=swslot, iswslot=iswslot: (lambda v: v.reciprocal(
                col(swslot)[:, 0:1], col(iswslot)[:, 0:1])))(), wait=[t3], inc="v")
            return t3, t4
        tok_isw1, tok_sw1 = wmean_finish(WM1, SW1C, ISW1, D * HID, wm1_toks[-1])
        tok_isw2, tok_sw2 = wmean_finish(WM2, SW2C, ISW2, D * HID, wm2_toks[-1])

        tp1 = em.on("vector", (lambda gc=gacc_cur: (lambda v: v.tensor_copy(
            CCP[:, 0:128], GAT[:, gc * C:(gc + 1) * C])))(), wait=ga_tok, inc="v")
        tp2 = em.on("vector", (lambda gc=gacc_cur: (lambda v: v.tensor_copy(
            CCP[:, 128:256], HA[:, gc * C:(gc + 1) * C])))(), wait=ga_tok, inc="v")
        tcc1 = em.on("gpsimd", lambda g: g.dma_start(out=cc_in, in_=CCP[:, :]),
                     wait=[tp1, tp2], inc="dG", amt=16)
        tcc2 = em.on("gpsimd", lambda g: g.collective_compute(
            "AllGather", OP.bypass, replica_groups=[[0, 1, 2, 3], [4, 5, 6, 7]],
            ins=[cc_in], outs=[cc_out]), wait=[tcc1], inc="cc")
        tcc3 = em.on("gpsimd", lambda g: g.dma_start(
            out=RCV[:, :].rearrange("p (s w) -> p s w", s=3),
            in_=cc_out.rearrange("(s p) w -> p s w", p=128)[:, 0:3, :]),
            wait=[tcc2], inc="dG", amt=16)

        tz0 = em.on("vector", lambda v: v.memset(ZI[:, 0:C], 0.0), inc="v")
        zi_cur = 0
        tz_tok = [tz0, tcc3, tok_smk]
        rGPT = Reg(); rHPT = Reg()
        for p in range(3):
            tb1 = em.on("vector", (lambda p=p: (lambda v: v.tensor_scalar(
                GPT[:, :], RCV[:, p * 256:p * 256 + 128], SMK[:, p:p + 1], None,
                OP.mult)))(), wait=tz_tok + rGPT.w(), inc="v")
            rGPT.note_w(tb1)
            tb2 = em.on("vector", (lambda p=p: (lambda v: v.scalar_tensor_tensor(
                GPT[:, :], IDF[:, :], SMK[:, 3 + p:4 + p], GPT[:, :],
                OP.mult, OP.add)))(), wait=rGPT.w() + [CIDF], inc="v")
            rGPT.note_w(tb2)
            tb3 = em.on("vector", (lambda p=p: (lambda v: v.tensor_scalar(
                HPT[:, :], RCV[:, p * 256 + 128:p * 256 + 256], SMK[:, p:p + 1],
                None, OP.mult)))(), wait=tz_tok + rHPT.w(), inc="v")
            rHPT.note_w(tb3)
            nxt = 1 - zi_cur
            wb = btf.w()
            tzm = em.on("tensor", (lambda zc=zi_cur: (lambda t: t.matmul(
                btf.ap[:, 128:256], GPT[:, :], ZI[:, zc * C:(zc + 1) * C],
                start=True, stop=True)))(), wait=wb + rGPT.r() + rHPT.r() + tz_tok, inc="t")
            btf.note_w(tzm)
            rGPT.note_r(tzm)
            tza = em.on("vector", (lambda nxt=nxt: (lambda v: v.tensor_add(
                ZI[:, nxt * C:(nxt + 1) * C], HPT[:, :], btf.ap[:, 128:256])))(),
                wait=btf.r() + rHPT.r(), inc="v")
            btf.note_r(tza)
            rHPT.note_r(tza)
            tz_tok = [tza]
            zi_cur = nxt
        tzb = em.on("vector", (lambda zc=zi_cur: (lambda v: v.tensor_copy(
            ZBt[:, 0:C], ZI[:, zc * C:(zc + 1) * C])))(), wait=tz_tok, inc="v")
        wb = btf.w()
        tzt = em.on("tensor", (lambda zc=zi_cur: (lambda t: t.transpose(
            btf.ap[:, 256:384], ZI[:, zc * C:(zc + 1) * C], IDF[:, :])))(),
            wait=wb + tz_tok + [CIDF], inc="t")
        btf.note_w(tzt)
        tztv = em.on("vector", lambda v: v.tensor_copy(ZTt[:, 0:C], btf.ap[:, 256:384]),
                     wait=btf.r(), inc="v")
        btf.note_r(tztv)

        do_tap("zine", ZBt[:, 0:128], 128, [tzb, tztv])
        do_tap("gtbe", GTB[:, 0:2048], 2048, gtb_toks)
        do_tap("hcbe", HCB[:, 0:2048], 2048, hcb_toks)
        do_tap("gupe", GUP[:, 0:2048], 2048, tgu_toks)
        do_tap("qte", qT[:, 0:2048], 2048, [tok_qT])
        do_tap("kte", kT[:, 0:2048], 2048, [tok_kT])
        do_tap("lje", S1[:, 0:2048], 2048, [tok_ljb])
        do_tap("ucbe", UCB[:, 0:4096], 4096, ucb_toks[0:8])
        rZ = [Reg(), Reg()]
        rZT = [Reg(), Reg()]
        rZ[0].note_w(tzb)
        rZT[0].note_w(tztv)
        rUFT = [Reg(), Reg()]
        rMPT = [Reg(), Reg()]
        x2_toks = [None] * NT
        for c in range(NT):
            zc = c % 2
            zn = (c + 1) % 2
            uo = c % 2
            if c < NT - 1:
                wb = pb[2].w()
                tzm1 = em.on("tensor", (lambda c=c, zc=zc: (lambda t: t.matmul(
                    pb[2].ap[:, 0:128], GTB[:, c * C:(c + 1) * C], ZBt[:, zc * C:(zc + 1) * C],
                    start=True, stop=True)))(),
                    wait=wb + rZ[zc].r() + [gtb_toks[c]], inc="t")
                tzm2 = em.on("tensor", (lambda c=c, zc=zc: (lambda t: t.matmul(
                    pb[2].ap[:, 128:256], ZBt[:, zc * C:(zc + 1) * C], GTB[:, c * C:(c + 1) * C],
                    start=True, stop=True)))(), inc="t")
                pb[2].note_w(tzm1); pb[2].note_w(tzm2)
                rZ[zc].note_r(tzm1); rZ[zc].note_r(tzm2)
                rb = pb[2].r()
                tza1 = em.on("vector", (lambda c=c, zn=zn: (lambda v: v.tensor_add(
                    ZBt[:, zn * C:(zn + 1) * C], HCB[:, c * C:(c + 1) * C],
                    pb[2].ap[:, 0:128])))(), wait=rb + rZ[zn].w() + [hcb_toks[c]], inc="v")
                tza2 = em.on("vector", (lambda c=c, zn=zn: (lambda v: v.tensor_add(
                    ZTt[:, zn * C:(zn + 1) * C], HTB[:, c * C:(c + 1) * C],
                    pb[2].ap[:, 128:256])))(), wait=rZT[zn].w() + [htb_toks[c]], inc="v")
                pb[2].note_r(tza1); pb[2].note_r(tza2)
                rZ[zn] = Reg(); rZ[zn].note_w(tza1)
                rZT[zn] = Reg(); rZT[zn].note_w(tza2)
            wb = pb[3].w()
            tm1 = em.on("tensor", (lambda c=c, zc=zc: (lambda t: t.matmul(
                pb[3].ap[:, 0:128], ZBt[:, zc * C:(zc + 1) * C], PTBs[:, c * C:(c + 1) * C],
                start=True, stop=True)))(),
                wait=wb + rZ[zc].r() + [ptb_toks[c]], inc="t")
            pb[3].note_w(tm1)
            rZ[zc].note_r(tm1)
            tvu = em.on("vector", (lambda c=c, uo=uo: (lambda v: v.tensor_sub(
                UFT[:, uo * 128:(uo + 1) * 128], ULT[:, c * C:(c + 1) * C],
                pb[3].ap[:, 0:128])))(),
                wait=pb[3].r() + rUFT[uo].w() + [ult_toks[c]], inc="v")
            pb[3].note_r(tvu)
            rUFT[uo] = Reg(); rUFT[uo].note_w(tvu)
            wb = pb[3].w()
            tm2 = em.on("tensor", (lambda c=c, uo=uo: (lambda t: t.matmul(
                pb[3].ap[:, 128:256], UFT[:, uo * 128:(uo + 1) * 128],
                qT[:, c * C:(c + 1) * C], start=True, stop=True)))(),
                wait=wb + rUFT[uo].r() + [tok_qT], inc="t")
            pb[3].note_w(tm2)
            rUFT[uo].note_r(tm2)
            tvm = em.on("vector", (lambda c=c, uo=uo: (lambda v: v.tensor_mul(
                MPT[:, uo * 128:(uo + 1) * 128], GUP[:, c * C:(c + 1) * C],
                pb[3].ap[:, 128:256])))(),
                wait=pb[3].r() + rMPT[uo].w() + [tgu_toks[c]], inc="v")
            pb[3].note_r(tvm)
            rMPT[uo] = Reg(); rMPT[uo].note_w(tvm)
            wb = pb[3].w()
            tm3 = em.on("tensor", (lambda c=c, zc=zc: (lambda t: t.matmul(
                pb[3].ap[:, 256:384], ZTt[:, zc * C:(zc + 1) * C], QTT[:, c * C:(c + 1) * C],
                start=True, stop=False)))(),
                wait=wb + rZT[zc].r() + [tqt_toks[c]], inc="t")
            tm4 = em.on("tensor", (lambda c=c, uo=uo: (lambda t: t.matmul(
                pb[3].ap[:, 256:384], KNB[:, c * C:(c + 1) * C], MPT[:, uo * 128:(uo + 1) * 128],
                start=False, stop=True)))(), wait=rMPT[uo].r() + [knb_toks[c]], inc="t")
            pb[3].note_w(tm3); pb[3].note_w(tm4)
            rZT[zc].note_r(tm3)
            rMPT[uo].note_r(tm4)
            tvo = em.on("vector", (lambda c=c, uo=uo: (lambda v: v.tensor_mul(
                UFT[:, uo * 128:(uo + 1) * 128], gT[:, c * C:(c + 1) * C],
                pb[3].ap[:, 256:384])))(),
                wait=pb[3].r() + rUFT[uo].w() + [tok_gT], inc="v")
            pb[3].note_r(tvo)
            rUFT[uo] = Reg(); rUFT[uo].note_w(tvo)
            yo = c % 2
            wy = rYTL.w() if c == 0 else []
            oy = []
            for nn in range(2):
                wb = pb[4].w()
                tm5 = em.on("tensor", (lambda nn=nn, uo=uo: (lambda t: t.matmul(
                    pb[4].ap[:, 0:512], UFT[:, uo * 128:(uo + 1) * 128],
                    WO[:, nn * 512:(nn + 1) * 512], start=True, stop=True)))(),
                    wait=wb + rUFT[uo].r() + [tok_wo], inc="t")
                pb[4].note_w(tm5)
                rUFT[uo].note_r(tm5)
                tv5 = em.on("vector", (lambda c=c, nn=nn, yo=yo: (lambda v: v.scalar_tensor_tensor(
                    YTL[:, yo * D + nn * 512: yo * D + (nn + 1) * 512],
                    pb[4].ap[:, 0:512], 1.0, BOB[:, nn * 512:(nn + 1) * 512],
                    OP.mult, OP.add,
                    accum_out=col(TMPA)[:, nn:nn + 1])))(),
                    wait=pb[4].r() + wy + [CBOB] +
                         ([x2_toks[c - 2]] if c >= 2 else []), inc="v")
                pb[4].note_r(tv5)
                oy.append(tv5)
            tsum = em.on("vector", (lambda c=c: (lambda v: v.tensor_add(
                col(SUM2)[:, c:c + 1], col(TMPA)[:, 0:1], col(TMPA)[:, 1:2])))(),
                wait=oy, inc="v")
            tsq = em.on("scalar", (lambda c=c, yo=yo: (lambda s: s.activation(
                S1[:, 0:1024], YTL[:, yo * D:(yo + 1) * D], AF.Square,
                accum_out=col(SQ2)[:, c:c + 1])))(), wait=oy + rS1.w(), inc="s")
            rS1.note_w(tsq)
            t1 = em.on("vector", (lambda c=c: (lambda v: v.tensor_scalar(
                col(MU2)[:, c:c + 1], col(SUM2)[:, c:c + 1], 1.0 / D, None, OP.mult)))(),
                wait=[tsum], inc="v")
            t2 = em.on("vector", (lambda c=c: (lambda v: v.tensor_scalar(
                col(TMPB)[:, 0:1], col(SQ2)[:, c:c + 1], 1.0 / D, None, OP.mult)))(),
                wait=[tsq], inc="v")
            t3 = em.on("vector", (lambda c=c: (lambda v: v.tensor_mul(
                col(TMPB)[:, 1:2], col(MU2)[:, c:c + 1], col(MU2)[:, c:c + 1])))(),
                wait=[t1], inc="v")
            t4 = em.on("vector", lambda v: v.tensor_sub(
                col(TMPB)[:, 0:1], col(TMPB)[:, 0:1], col(TMPB)[:, 1:2]),
                wait=[t2, t3], inc="v")
            t5 = em.on("scalar", lambda s: s.activation(
                col(TMPB)[:, 1:2], col(TMPB)[:, 0:1], AF.Sqrt, bias=EPSC),
                wait=[t4, tok_epsc], inc="s")
            t6 = em.on("vector", (lambda c=c: (lambda v: v.reciprocal(
                col(R2C)[:, c:c + 1], col(TMPB)[:, 1:2])))(), wait=[t5], inc="v")
            t7 = em.on("vector", (lambda c=c: (lambda v: v.scalar_tensor_tensor(
                col(NM2)[:, c:c + 1], col(MU2)[:, c:c + 1], -1.0, col(R2C)[:, c:c + 1],
                OP.mult, OP.mult)))(), wait=[t6], inc="v")
            t8 = em.on("vector", (lambda c=c, yo=yo: (lambda v: v.tensor_scalar(
                YTL[:, yo * D:(yo + 1) * D], YTL[:, yo * D:(yo + 1) * D],
                col(R2C)[:, c:c + 1], col(NM2)[:, c:c + 1], OP.mult, OP.add)))(),
                wait=[t7, tsq], inc="v")
            t9 = em.on("vector", (lambda c=c, yo=yo: (lambda v: v.scalar_tensor_tensor(
                R1[:, c * D:(c + 1) * D], YTL[:, yo * D:(yo + 1) * D], 1.0,
                R1[:, c * D:(c + 1) * D], OP.mult, OP.add,
                accum_out=col(SUM3)[:, c:c + 1])))(), wait=[t8, tok_x], inc="v")
            rYTL.note_w(t9)
            rYTL.note_r(t9)
            x2_toks[c] = t9

        rS1b = Reg()
        sq3_toks = []
        for c in range(NT):
            tq = em.on("scalar", (lambda c=c: (lambda s: s.activation(
                S1[:, 1024:2048], R1[:, c * D:(c + 1) * D], AF.Square,
                accum_out=col(SQ3)[:, c:c + 1])))(), wait=[x2_toks[c]] + rS1b.w(), inc="s")
            rS1b.note_w(tq)
            sq3_toks.append(tq)
        t1 = em.on("vector", lambda v: v.tensor_scalar(col(MU3), col(SUM3), 1.0 / D, None, OP.mult),
                   wait=x2_toks, inc="v")
        t2 = em.on("vector", lambda v: v.tensor_scalar(col(TMPA), col(SQ3), 1.0 / D, None, OP.mult),
                   wait=sq3_toks, inc="v")
        t3 = em.on("vector", lambda v: v.tensor_mul(col(TMPB), col(MU3), col(MU3)), wait=[t1], inc="v")
        t4 = em.on("vector", lambda v: v.tensor_sub(col(TMPA), col(TMPA), col(TMPB)),
                   wait=[t2, t3], inc="v")
        t5 = em.on("vector", lambda v: v.tensor_scalar(col(TMPB), col(TMPA), EPS, None, OP.add),
                   wait=[t4], inc="v")
        t6a = em.on("vector", lambda v: v.reciprocal(col(TMPC), col(TMPB)), wait=[t5], inc="v")
        t6 = em.on("vector", lambda v: v.tensor_mul(col(TMPC), col(TMPA), col(TMPC)),
                   wait=[t6a], inc="v")
        t8 = em.on("vector", lambda v: v.tensor_scalar(col(TMPC), col(TMPC), EPS, None, OP.add),
                   wait=[t6], inc="v")
        t9 = em.on("vector", lambda v: v.tensor_mul(col(TMPA), col(TMPB), col(TMPC)),
                   wait=[t8], inc="v")
        t10 = em.on("scalar", lambda s: s.activation(col(TMPB), col(TMPA), AF.Sqrt),
                    wait=[t9], inc="s")
        t11 = em.on("vector", lambda v: v.reciprocal(col(RHO), col(TMPB)), wait=[t10], inc="v")
        t12 = em.on("vector", lambda v: v.scalar_tensor_tensor(
            col(NMR), col(MU3), -1.0, col(RHO), OP.mult, OP.mult), wait=[t11], inc="v")
        rho_tok = t12

        tok_b2b = em.on("gpsimd", lambda g: g.dma_start(out=B2B[:, :], in_=b2_b_d),
                        wait=x2_toks, inc="dG", amt=16)
        rZI32 = Reg()
        rXQN = [Reg(), Reg()]
        rS1h = [rS1, rS1b]
        xq_last = None
        for c in range(NT):
            hh = 0
            ho = 0
            rh = rS1h[hh]
            ta = em.on("vector", (lambda c=c, ho=ho: (lambda v: v.tensor_scalar(
                S1[:, ho:ho + D], R1[:, c * D:(c + 1) * D], col(RHO)[:, c:c + 1],
                col(NMR)[:, c:c + 1], OP.mult, OP.add)))(),
                wait=rh.w() + [rho_tok, x2_toks[c]], inc="v")
            rh.note_w(ta)
            tb = em.on("vector", (lambda ho=ho: (lambda v: v.scalar_tensor_tensor(
                S1[:, ho:ho + D], S1[:, ho:ho + D], 1.0, GMB[:, :], OP.mult, OP.mult)))(),
                wait=rh.w() + [CGMB], inc="v")
            rh.note_w(tb)
            tc = em.on("vector", (lambda ho=ho: (lambda v: v.tensor_add(
                S1[:, ho:ho + D], S1[:, ho:ho + D], BTB[:, :])))(),
                wait=rh.w() + [CBTB], inc="v")
            rh.note_w(tc)
            tdm = em.on("vector", (lambda c=c, ho=ho, hh=hh: (lambda v: v.reduce_max(
                col(TMPA)[:, hh:hh + 1], S1[:, ho:ho + D], axis=AX.X,
                apply_absolute_value=True)))(), wait=rh.r(), inc="v")
            rh.note_r(tdm)
            tde = em.on("vector", (lambda hh=hh: (lambda v: v.tensor_scalar(
                col(TMPA)[:, 2 + hh:3 + hh], col(TMPA)[:, hh:hh + 1], EPS, None,
                OP.max)))(), wait=[tdm], inc="v")
            tdf = em.on("vector", (lambda hh=hh: (lambda v: v.reciprocal(
                col(TMPB)[:, hh:hh + 1], col(TMPA)[:, 2 + hh:3 + hh])))(),
                wait=[tde], inc="v")
            tdg = em.on("vector", (lambda c=c, hh=hh: (lambda v: v.tensor_scalar(
                col(SXC)[:, c:c + 1], col(TMPB)[:, hh:hh + 1], 127.0, None, OP.mult)))(),
                wait=[tdf], inc="v")
            tdh = em.on("vector", (lambda c=c, hh=hh: (lambda v: v.scalar_tensor_tensor(
                col(ISX)[:, c:c + 1], col(TMPA)[:, 2 + hh:3 + hh], 1.0 / 127.0,
                col(ISW1)[:, 0:1], OP.mult, OP.mult)))(), wait=[tde, tok_isw1], inc="v")
            xo = c % 2
            wxq = rXQN[xo].w()
            tj = None
            for hf in range(2):
                ti = em.on("vector", (lambda c=c, hf=hf, ho=ho: (lambda v: v.tensor_scalar(
                    ZI32[:, :], S1[:, ho + hf * 512:ho + (hf + 1) * 512],
                    col(SXC)[:, c:c + 1], None, OP.mult)))(),
                    wait=rZI32.w() + rh.r() + [tdg], inc="v")
                rZI32.note_w(ti)
                rh.note_r(ti)
                tj = em.on("vector", (lambda xo=xo, hf=hf: (lambda v: v.tensor_copy(
                    XQN[:, xo * D + hf * 512: xo * D + (hf + 1) * 512], ZI32[:, :])))(),
                    wait=(wxq if hf == 0 else []) + rZI32.r(), inc="v")
                rZI32.note_r(tj)
            rXQN[xo] = Reg(); rXQN[xo].note_w(tj)
            for half in range(2):
                wb = btb.w()
                tts = []
                for kk in range(4):
                    k = half * 4 + kk
                    tt = em.on("tensor", (lambda k=k, kk=kk, xo=xo: (lambda t: t.transpose(
                        btb.ap[:, kk * 128:(kk + 1) * 128],
                        XQN[:, xo * D + k * 128: xo * D + (k + 1) * 128], IDB[:, :])))(),
                        wait=(wb if kk == 0 else []) + rXQN[xo].r() + [tok_idb] +
                             (tts[-1:] if tts else []), inc="t")
                    tts.append(tt)
                    btb.note_w(tt)
                    rXQN[xo].note_r(tt)
                rb = btb.r()
                evs = []
                for kk in range(4):
                    k = half * 4 + kk
                    te = em.on("vector", (lambda c=c, k=k, kk=kk: (lambda v: v.tensor_copy(
                        R2[:, k * TOK + c * 128: k * TOK + (c + 1) * 128],
                        btb.ap[:, kk * 128:(kk + 1) * 128])))(),
                        wait=rb + ([tok_xT] if c == 0 else []), inc="v")
                    btb.note_r(te)
                    evs.append(te)
                xq_last = evs[-1]
        tok_xq = xq_last

        do_tap("x2e", R1[:, 0:4096], 4096, x2_toks)
        do_tap("xqe", R2[:, 0:4096], 4096, [tok_xq])
        do_tap("coe", CO[:, 0:768], 768, x2_toks + [rho_tok])
        tok_sxw1b = bcast_from_col(col(ISX), rF1, F1, rF1.w() + [tdh, tok_xq, tzb, tztv])

        tmx0 = em.on("vector", lambda v: v.memset(F2[:, 0:TOK], 0.0),
                     wait=rF2.w() + rWMb[0].w() + rWMb[1].w(), inc="v")
        rF2.note_w(tmx0)
        rRWM = Reg()
        tmn0 = em.on("vector", lambda v: v.memset(RW[:, 0:TOK], 0.0),
                     wait=rROW0.w() + rL16[0].w() + rL16[1].w() + rB16.w() + rRW16.w() + [tok_sxw1b],
                     inc="v")
        rRWM.note_w(tmn0)
        w1_slice_r = W1T_d.rearrange("(k p) (h j) -> p k h j", p=128, j=128)
        hb_r = Hb.rearrange("(h p) t -> h p t", p=128)
        rW1S = [Reg(), Reg()]
        rW1Q = [Reg(), Reg()]
        rHSL = [Reg(), Reg()]
        tmx_last = tmx0
        w2_slice2 = W2T_d.rearrange("(h p) dd -> h p dd", p=128)
        rW2Sr = Reg()
        w2q_toks = [None] * NH
        for h in range(NH):
            pp = h % 2
            td = em.on("sync", (lambda h=h, pp=pp: (lambda sy: sy.dma_start(
                out=W1S[:, pp * D:(pp + 1) * D].rearrange("p (k j) -> p k j", k=ND),
                in_=w1_slice_r[:, :, h, :])))(),
                wait=rW1S[pp].w() + ([wm1_toks[-1], x2_toks[-1], tok_sxw1b]
                                     if h <= 1 else []),
                inc="dW0" if pp == 0 else "dW1", amt=16)
            rW1S[pp] = Reg(); rW1S[pp].note_w(td)
            tq1 = em.on("vector", (lambda pp=pp: (lambda v: v.tensor_scalar(
                W1Q[:, pp * D:(pp + 1) * D], W1S[:, pp * D:(pp + 1) * D],
                col(SW1C)[:, 0:1], 0.5, OP.mult, OP.is_gt)))(),
                wait=rW1Q[pp].w() + rW1S[pp].r() + [tok_sw1], inc="v")
            rW1S[pp].note_r(tq1)
            tq2 = em.on("vector", (lambda pp=pp: (lambda g: g.tensor_scalar(
                W1S[:, pp * D:(pp + 1) * D], W1S[:, pp * D:(pp + 1) * D],
                col(SW1C)[:, 0:1], -0.5, OP.mult, OP.is_lt)))(),
                wait=rW1S[pp].w() + [tq1, tok_sw1], inc="v")
            rW1S[pp].note_w(tq2)
            tq3 = em.on("vector", (lambda pp=pp: (lambda v: v.tensor_sub(
                W1Q[:, pp * D:(pp + 1) * D], W1Q[:, pp * D:(pp + 1) * D],
                W1S[:, pp * D:(pp + 1) * D])))(), wait=[tq1, tq2], inc="v")
            rW1S[pp].note_r(tq3)
            rW1Q[pp] = Reg(); rW1Q[pp].note_w(tq3)
            hp = h % 2
            whs = rHSL[hp].w()
            ev = []
            for pair in range(2):
                banks2 = [pb[2 * pair], pb[2 * pair + 1]]
                wbs = [b.w() for b in banks2]
                mms = [[], []]
                for k in range(ND):
                    for j in range(2):
                        nchk = 2 * pair + j
                        tm = em.on("tensor", (lambda k=k, nchk=nchk, bank=banks2[j], pp=pp: (lambda t: t.matmul(
                            bank.ap[:, 0:512], W1Q[:, pp * D + k * 128: pp * D + (k + 1) * 128],
                            R2[:, k * TOK + nchk * 512: k * TOK + (nchk + 1) * 512],
                            start=(k == 0), stop=(k == ND - 1))))(),
                            wait=(wbs[j] if k == 0 else []) + rW1Q[pp].r() + [tok_xq],
                            inc="t")
                        mms[j].append(tm)
                for j in range(2):
                    banks2[j].note_w(mms[j][-1])
                    rW1Q[pp].note_r(mms[j][-1])
            for nchk in range(4):
                bank = pb[nchk]
                tv = em.on("vector", (lambda nchk=nchk, bank=bank: (lambda v: v.tensor_mul(
                    S1[:, nchk * 512:(nchk + 1) * 512], bank.ap[:, 0:512],
                    F1[:, nchk * 512:(nchk + 1) * 512])))(),
                    wait=bank.r() + (rS1.w() + rS1b.w() if nchk == 0 else []) +
                         [tok_sxw1b], inc="v")
                bank.note_r(tv)
                rF1.note_r(tv)
                rS1.note_w(tv)
                ts_ = em.on("scalar", (lambda h=h, nchk=nchk, hp=hp: (lambda s: s.activation(
                    HSL[:, hp * TOK + nchk * 512: hp * TOK + (nchk + 1) * 512],
                    S1[:, nchk * 512:(nchk + 1) * 512], AF.Gelu,
                    bias=B1T[:, h:h + 1])))(),
                    wait=rS1.r() + (whs if nchk == 0 else []) + [CB1T], inc="s")
                rS1.note_r(ts_)
                ev.append(ts_)
            rHSL[hp] = Reg(); rHSL[hp].note_w(ev[-1])
            for e in ev:
                rHSL[hp].note_w(e)
            tmx = em.on("vector", (lambda hp=hp: (lambda g: g.tensor_tensor(
                F2[:, 0:TOK], F2[:, 0:TOK], HSL[:, hp * TOK:(hp + 1) * TOK],
                OP.max)))(), wait=ev + rF2.w(), inc="v")
            rF2.note_w(tmx)
            rHSL[hp].note_r(tmx)
            tmn = em.on("vector", (lambda hp=hp: (lambda g: g.scalar_tensor_tensor(
                RW[:, 0:TOK], HSL[:, hp * TOK:(hp + 1) * TOK], -1.0, RW[:, 0:TOK],
                OP.mult, OP.max)))(), wait=ev + rRWM.w(), inc="v")
            rRWM.note_w(tmn)
            rHSL[hp].note_r(tmn)
            tmx_last = tmx
            tmn_last = tmn
            tdh_ = em.on("scalar", (lambda h=h, hp=hp: (lambda s: s.dma_start(
                out=hb_r[h], in_=HSL[:, hp * TOK:(hp + 1) * TOK])))(),
                wait=ev, inc="dA", amt=16)
            rHSL[hp].note_r(tdh_)
            td2 = em.on("sync", (lambda h=h: (lambda sy: sy.dma_start(
                out=W2S[:, 0:D], in_=w2_slice2[h])))(),
                wait=rW2Sr.w() + ([wm2_toks[-1], x2_toks[-1], tok_xq]
                                  if h == 0 else []), inc="dX", amt=16)
            rW2Sr.note_w(td2)
            wq1 = em.on("vector", (lambda h=h: (lambda g: g.tensor_scalar(
                W2Q[:, h * D:(h + 1) * D], W2S[:, 0:D],
                col(SW2C)[:, 0:1], 0.5, OP.mult, OP.is_gt)))(),
                wait=rW2Sr.r() + [tok_sw2] + ([x2_toks[-1], tok_xq] if h == 0 else []),
                inc="v")
            rW2Sr.note_r(wq1)
            wq2 = em.on("vector", (lambda h=h: (lambda g: g.tensor_scalar(
                W2S[:, 0:D], W2S[:, 0:D],
                col(SW2C)[:, 0:1], -0.5, OP.mult, OP.is_lt)))(),
                wait=rW2Sr.w(), inc="v")
            rW2Sr.note_w(wq2)
            wq3 = em.on("vector", (lambda h=h: (lambda g: g.tensor_sub(
                W2Q[:, h * D:(h + 1) * D], W2Q[:, h * D:(h + 1) * D],
                W2S[:, 0:D])))(), wait=[wq1, wq2], inc="v")
            rW2Sr.note_r(wq3)
            w2q_toks[h] = wq3

        hb_done = ("dA", em.cnt["dA"])
        pass1_t = ("t", em.cnt["t"])

        s2r = []
        for c in range(NT):
            wb = btf.w()
            tt = em.on("tensor", (lambda c=c: (lambda t: t.transpose(
                btf.ap[:, 0:128], F2[:, c * 128:(c + 1) * 128], IDF[:, :])))(),
                wait=wb + [tmx_last, CIDF], inc="t")
            tt2 = em.on("tensor", (lambda c=c: (lambda t: t.transpose(
                btf.ap[:, 128:256], RW[:, c * 128:(c + 1) * 128], IDF[:, :])))(),
                wait=[tmn_last], inc="t")
            btf.note_w(tt); btf.note_w(tt2)
            rF2.note_r(tt)
            rRWM.note_r(tt2)
            rb = btf.r()
            tr = em.on("vector", (lambda c=c: (lambda v: v.reduce_max(
                col(TMPA)[:, c:c + 1], btf.ap[:, 0:128], axis=AX.X)))(),
                wait=rb, inc="v")
            tr2 = em.on("vector", (lambda c=c: (lambda v: v.reduce_max(
                col(TMPB)[:, c:c + 1], btf.ap[:, 128:256], axis=AX.X)))(),
                wait=rb, inc="v")
            btf.note_r(tr); btf.note_r(tr2)
            s2r += [tr, tr2]
        tmrg = em.on("vector", lambda v: v.tensor_max(col(TMPA), col(TMPA), col(TMPB)),
                     wait=s2r, inc="v")
        t1 = em.on("vector", lambda v: v.tensor_scalar(col(TMPB), col(TMPA), EPS, None, OP.max),
                   wait=[tmrg], inc="v")
        t2 = em.on("vector", lambda v: v.reciprocal(col(TMPC), col(TMPB)), wait=[t1], inc="v")
        tok_s2 = em.on("vector", lambda v: v.tensor_scalar(col(S2C), col(TMPC), 127.0, None, OP.mult),
                       wait=[t2], inc="v")
        tok_is2 = em.on("vector", lambda v: v.tensor_scalar(
            col(IS2), col(TMPB), col(ISW2)[:, 0:1], 1.0 / 127.0, OP.mult, OP.mult),
            wait=[t1, tok_isw2], inc="v")
        tok_s2b = bcast_from_col(col(S2C), rF2, F2, rF2.w() + rRWM.w() + [tok_s2])

        hb_rd = Hb.rearrange("(h p) (g t) -> p h g t", p=128, t=512)
        out_r = out_d.rearrange("(c p) d -> c p d", p=128)
        rR2 = Reg()
        rR2.note_w(tok_xq)
        rSQS = Reg()
        rZH = Reg()
        rOUT2 = [Reg() for _ in range(4)]
        rOSTG = Reg()
        out_toks = []
        prev_last_rd = None
        for g_ in range(4):
            sub_toks = []
            for sj in range(4):
                if prev_last_rd is None:
                    wsub = rR2.w() + [hb_done, pass1_t]
                else:
                    wsub = [prev_last_rd[sj * 8 + 7]]
                tds = em.on("sync", (lambda g_=g_, sj=sj: (lambda sy: sy.dma_start(
                    out=R2[:, sj * 8 * 512:(sj + 1) * 8 * 512].rearrange(
                        "p (h t) -> p h t", h=8),
                    in_=hb_rd[:, sj * 8:(sj + 1) * 8, g_, :])))(),
                    wait=wsub, inc="dR%d" % sj, amt=16)
                sub_toks.append(tds)
                rR2.note_w(tds)
            q_toks = []
            for h in range(NH):
                tq1 = em.on("vector", (lambda h=h, g_=g_: (lambda v: v.tensor_tensor(
                    ZH32[:, :], R2[:, h * 512:(h + 1) * 512],
                    F2[:, g_ * 512:(g_ + 1) * 512], OP.mult)))(),
                    wait=rZH.w() + [sub_toks[h // 8], tok_s2b], inc="v")
                rR2.note_r(tq1)
                rF2.note_r(tq1)
                rZH.note_w(tq1)
                tq2 = em.on("vector", (lambda h=h: (lambda v: v.tensor_copy(
                    R2[:, h * 512:(h + 1) * 512], ZH32[:, :])))(),
                    wait=rZH.r() + [rR2.lw], inc="v")
                rZH.note_r(tq2)
                rR2.note_w(tq2)
                q_toks.append(tq2)
            hq_tok = q_toks[-1]
            for tl in range(4):
                c = g_ * 4 + tl
                ev2 = []
                banks2 = [pb[(tl * 2) % 4], pb[(tl * 2 + 1) % 4]]
                wbs = [b.w() for b in banks2]
                mms = [[], []]
                for h in range(NH):
                    for dch in range(2):
                        tm = em.on("tensor", (lambda h=h, tl=tl, dch=dch, bank=banks2[dch]: (lambda t: t.matmul(
                            bank.ap[:, 0:512], R2[:, h * 512 + tl * 128: h * 512 + (tl + 1) * 128],
                            W2Q[:, h * D + dch * 512: h * D + (dch + 1) * 512],
                            start=(h == 0), stop=(h == NH - 1))))(),
                            wait=(wbs[dch] if h == 0 else []) + [q_toks[h], w2q_toks[h]],
                            inc="t")
                        mms[dch].append(tm)
                for dch in range(2):
                    banks2[dch].note_w(mms[dch][-1])
                    rR2.note_r(mms[dch][-1])
                if tl == 3:
                    prev_last_rd = mms[1]
                for dch in range(2):
                    bank = banks2[dch]
                    tv = em.on("vector", (lambda c=c, tl=tl, dch=dch, bank=bank: (lambda v: v.scalar_tensor_tensor(
                        OUT2[:, tl * D + dch * 512: tl * D + (dch + 1) * 512],
                        bank.ap[:, 0:512], col(IS2)[:, c:c + 1],
                        B2B[:, dch * 512:(dch + 1) * 512], OP.mult, OP.add,
                        accum_out=col(TMPA)[:, dch:dch + 1])))(),
                        wait=bank.r() + (rOUT2[tl].w() if dch == 0 else []) +
                             [tok_is2, tok_b2b], inc="v")
                    bank.note_r(tv)
                    ev2.append(tv)
                rOUT2[tl] = Reg(); rOUT2[tl].note_w(ev2[-1])
                for e in ev2:
                    rOUT2[tl].note_w(e)
                tsum = em.on("vector", (lambda c=c: (lambda v: v.tensor_add(
                    col(SUM5)[:, c:c + 1], col(TMPA)[:, 0:1], col(TMPA)[:, 1:2])))(),
                    wait=ev2, inc="v")
                tsq = em.on("scalar", (lambda c=c, tl=tl: (lambda s: s.activation(
                    SQS[:, :], OUT2[:, tl * D:(tl + 1) * D], AF.Square,
                    accum_out=col(SQ5)[:, c:c + 1])))(), wait=ev2 + rSQS.w(), inc="s")
                rSQS.note_w(tsq)
                rOUT2[tl].note_r(tsq)
                t1 = em.on("vector", (lambda c=c: (lambda v: v.tensor_scalar(
                    col(MU5)[:, c:c + 1], col(SUM5)[:, c:c + 1], 1.0 / D, None, OP.mult)))(),
                    wait=[tsum], inc="v")
                t2 = em.on("vector", (lambda c=c: (lambda v: v.tensor_scalar(
                    col(TMPB)[:, 0:1], col(SQ5)[:, c:c + 1], 1.0 / D, None, OP.mult)))(),
                    wait=[tsq], inc="v")
                t3 = em.on("vector", (lambda c=c: (lambda v: v.tensor_mul(
                    col(TMPB)[:, 1:2], col(MU5)[:, c:c + 1], col(MU5)[:, c:c + 1])))(),
                    wait=[t1], inc="v")
                t4 = em.on("vector", lambda v: v.tensor_sub(
                    col(TMPB)[:, 0:1], col(TMPB)[:, 0:1], col(TMPB)[:, 1:2]),
                    wait=[t2, t3], inc="v")
                t5 = em.on("scalar", lambda s: s.activation(
                    col(TMPB)[:, 1:2], col(TMPB)[:, 0:1], AF.Sqrt, bias=EPSC),
                    wait=[t4], inc="s")
                t6 = em.on("vector", (lambda c=c: (lambda v: v.reciprocal(
                    col(R5C)[:, c:c + 1], col(TMPB)[:, 1:2])))(), wait=[t5], inc="v")
                t7 = em.on("vector", (lambda c=c: (lambda v: v.scalar_tensor_tensor(
                    col(NM5)[:, c:c + 1], col(MU5)[:, c:c + 1], -1.0, col(R5C)[:, c:c + 1],
                    OP.mult, OP.mult)))(), wait=[t6], inc="v")
                t8 = em.on("vector", (lambda c=c, tl=tl: (lambda v: v.tensor_scalar(
                    OUT2[:, tl * D:(tl + 1) * D], OUT2[:, tl * D:(tl + 1) * D],
                    col(R5C)[:, c:c + 1], col(NM5)[:, c:c + 1], OP.mult, OP.add)))(),
                    wait=[t7, tsq], inc="v")
                rOUT2[tl].note_w(t8)
                t9 = em.on("vector", (lambda c=c, tl=tl: (lambda v: v.scalar_tensor_tensor(
                    OSTG[:, 0:D], OUT2[:, tl * D:(tl + 1) * D], 1.0,
                    R1[:, c * D:(c + 1) * D], OP.mult, OP.add)))(),
                    wait=rOSTG.w() + [t8, w2q_toks[-1]], inc="v")
                rOUT2[tl].note_r(t9)
                rOSTG.note_w(t9)
                to = em.on("sync", (lambda c=c: (lambda sy: sy.dma_start(
                    out=out_r[c], in_=OSTG[:, 0:D])))(),
                    wait=rOSTG.r(), inc="out", amt=16)
                rOSTG.note_r(to)
                out_toks.append(to)

        tap_points = {
            "x2": (R1[:, 0:4096], 4096),
            "qT": (qT[:, 0:2048], 2048),
            "kT": (kT[:, 0:2048], 2048),
            "vT": (vT[:, 0:2048], 2048),
            "gT": (gT[:, 0:2048], 2048),
            "lj": (S1[:, 0:2048], 2048),
            "gtb": (GTB[:, 0:2048], 2048),
            "hcb": (HCB[:, 0:2048], 2048),
            "ucb": (UCB[:, 0:4096], 4096),
            "zin": (ZBt[:, 0:128], 128),
            "xq": (R2[:, 0:4096], 4096),
            "w2q": (W2Q[:, 0:4096], 4096),
            "co": (CO[:, 0:768], 768),
            "ytl": (YTL[:, 0:2048], 2048),
            "f2": (F2[:, 0:2048], 2048),
            "gup": (GUP[:, 0:2048], 2048),
            "qtt": (QTT[:, 0:2048], 2048),
        }
        if tap is not None and tap in tap_points:
            region, width = tap_points[tap]
        else:
            region, width = CO[:, 0:768], 768
        em.on("gpsimd", (lambda region=region, width=width: (lambda g: g.dma_start(
            out=dbg_d[:, 0:width], in_=region)))(),
            wait=em.barrier(), inc="dG", amt=16)
        em.on("sync", None, wait=[("dG", em.cnt["dG"])], inc=None)
        em.on("sync", None, wait=[("out", em.cnt["out"])], inc=None)

        block = es.enter_context(nc.Block())

        @block.sync
        def _(sy):
            em.run("sync", sy)

        @block.gpsimd
        def _(g):
            em.run("gpsimd", g)

        @block.vector
        def _(v):
            em.run("vector", v)

        @block.scalar
        def _(s):
            em.run("scalar", s)

        @block.tensor
        def _(t):
            em.run("tensor", t)

    return nc


_NC = None
TAP = None
DBG_OUT = {}


def kernel(**inputs):
    global _NC
    f32 = lambda a: np.ascontiguousarray(np.asarray(a, dtype=np.float32))
    x = f32(inputs["L0_attn_x"])
    Wq = f32(inputs["L0_attn_Wq"]); bq = f32(inputs["L0_attn_bq"])
    Wk = f32(inputs["L0_attn_Wk"]); bk = f32(inputs["L0_attn_bk"])
    Wv = f32(inputs["L0_attn_Wv"]); bv = f32(inputs["L0_attn_bv"])
    Wa = f32(inputs["L0_attn_Wa"]); ba = f32(inputs["L0_attn_ba"])
    Wb = f32(inputs["L0_attn_Wb"]); bb = f32(inputs["L0_attn_bb"])
    Wg = f32(inputs["L0_attn_Wg"]); bg = f32(inputs["L0_attn_bg"])
    Wo = f32(inputs["L0_attn_Wo"]); bo = f32(inputs["L0_attn_bo"])
    w1 = f32(inputs["L0_ffn_w1"]); b1 = f32(inputs["L0_ffn_b1"])
    w2 = f32(inputs["L0_ffn_w2"]); b2 = f32(inputs["L0_ffn_b2"])
    gamma = f32(inputs["L0_ffn_gamma"]); beta = f32(inputs["L0_ffn_beta"])

    Wab = np.ascontiguousarray(np.concatenate([Wa, Wb], axis=1))
    bab = np.ascontiguousarray(np.stack([ba[0], bb[0]])[:, None])
    W1T = np.ascontiguousarray(w1.T)
    W2T = np.ascontiguousarray(w2.T)
    bcolf = lambda a: np.ascontiguousarray(a[:, None])
    brd = lambda a: np.ascontiguousarray(np.broadcast_to(a[None, :], (128, a.shape[0])))
    b1T = np.ascontiguousarray(b1.reshape(NH, 128).T)
    idf = np.eye(128, dtype=np.float32)
    idb = np.eye(128, dtype=ml_dtypes.bfloat16)
    ii = np.arange(128)
    maskW = np.where(ii[:, None] > ii[None, :], 0.0, POS).astype(np.float32)
    maskU = np.where(ii[None, :] >= ii[:, None], 0.0, NEG).astype(np.float32)

    global _NC
    if _NC is None:
        _NC = build_nc(TAP)
    nc = _NC

    in_maps = []
    for core in range(8):
        b, s = core // 4, core % 4
        xs = np.ascontiguousarray(x[b, s * TOK:(s + 1) * TOK, :])
        xTs = np.ascontiguousarray(xs.T)
        sm = np.zeros((128, 6), np.float32)
        for p in range(3):
            m = 1.0 if p < s else 0.0
            sm[:, p] = m
            sm[:, 3 + p] = 1.0 - m
        in_maps.append(dict(
            x=xs, xT=xTs, Wq=Wq, Wk=Wk, Wv=Wv, Wg=Wg, Wab=Wab, Wo=Wo,
            bq=bcolf(bq), bk=bcolf(bk), bv=bcolf(bv), bg=bcolf(bg), bab=bab,
            bo_b=brd(bo), b2_b=brd(b2), gam_b=brd(gamma), bet_b=brd(beta),
            b1T=b1T, W1T=W1T, W2T=W2T, idf=idf, idb=idb,
            maskW=maskW, maskU=maskU, smask=sm,
        ))
    res = run_bass_kernel_spmd(nc, in_maps, core_ids=list(range(8)))
    out = np.empty_like(x)
    for core in range(8):
        b, s = core // 4, core % 4
        out[b, s * TOK:(s + 1) * TOK, :] = res.results[core]["out"]
    DBG_OUT.clear()
    for core in range(8):
        DBG_OUT[core] = np.asarray(res.results[core]["dbg"])
    return out


# version 22
# speedup vs baseline: 1.1298x; 1.1130x over previous
import numpy as np
import ml_dtypes
import concourse.bass as bass
import concourse.mybir as mybir
from concourse.bass_utils import run_bass_kernel_spmd

F32 = mybir.dt.float32
BF16 = mybir.dt.bfloat16
I32 = mybir.dt.int32
AF = mybir.ActivationFunctionType
OP = mybir.AluOpType
AX = mybir.AxisListType

TOK = 2048
D = 1024
C = 128
HID = 4096
NT = 16
ND = 8
NH = 32
EPS = 1e-5
NEG = -1e30
POS = 1e30

ENGS = ("sync", "gpsimd", "vector", "scalar", "tensor")


class Emitter:
    def __init__(self):
        self.q = {e: [] for e in ENGS}
        self.cnt = {}
        self.sems = {}

    def add_sem(self, name, handle):
        self.sems[name] = handle
        self.cnt[name] = 0

    def on(self, eng, fn, wait=(), inc=None, amt=1):
        tok = None
        if inc is not None:
            self.cnt[inc] += amt
            tok = (inc, self.cnt[inc])
        wl = tuple(t for t in wait if t is not None)
        self.q[eng].append((fn, wl, tok, amt))
        return tok

    def barrier(self):
        return [(s, self.cnt[s]) for s in self.cnt]

    def run(self, eng_name, eng_obj):
        mark = {}
        for fn, waits, tok, amt in self.q[eng_name]:
            seen = {}
            for (s, v) in waits:
                if v > 0 and v > mark.get(s, 0) and seen.get(s, -1) < v:
                    seen[s] = v
            for s, v in seen.items():
                eng_obj.wait_ge(self.sems[s], v)
                mark[s] = v
            if fn is None:
                continue
            inst = fn(eng_obj)
            if tok is not None:
                inst.then_inc(self.sems[tok[0]], amt)


class Bank:

    def __init__(self, ap):
        self.ap = ap
        self.rd = []
        self.wr = []
        self.mode = "r"

    def w(self):
        if self.mode == "r":
            out = list(self.rd)
            self.rd = []
            self.wr = []
            self.mode = "w"
            return out
        return []

    def note_w(self, tok):
        self.wr.append(tok)

    def r(self):
        if self.mode == "w":
            self.mode = "r"
        return list(self.wr)

    def note_r(self, tok):
        self.rd.append(tok)


class Reg:

    def __init__(self, ap=None):
        self.ap = ap
        self.lw = None
        self.rd = []

    def w(self):
        out = [self.lw] + self.rd
        self.rd = []
        return out

    def note_w(self, tok):
        self.lw = tok

    def r(self):
        return [self.lw]

    def note_r(self, tok):
        self.rd.append(tok)


def build_nc(tap=None):
    nc = bass.Bass("TRN2", target_bir_lowering=False)

    def din(name, shp, dt=F32):
        return nc.dram_tensor(name, shp, dt, kind="ExternalInput").ap()

    x_d = din("x", [TOK, D])
    xT_d = din("xT", [D, TOK])
    Wq_d = din("Wq", [D, C]); Wk_d = din("Wk", [D, C])
    Wv_d = din("Wv", [D, C]); Wg_d = din("Wg", [D, C])
    Wab_d = din("Wab", [D, 2])
    Wo_d = din("Wo", [C, D])
    bq_d = din("bq", [C, 1]); bk_d = din("bk", [C, 1])
    bv_d = din("bv", [C, 1]); bg_d = din("bg", [C, 1])
    bab_d = din("bab", [2, 1])
    bo_b_d = din("bo_b", [128, D])
    b2_b_d = din("b2_b", [128, D])
    gam_b_d = din("gam_b", [128, D])
    bet_b_d = din("bet_b", [128, D])
    b1T_d = din("b1T", [128, NH])
    W1T_d = din("W1T", [D, HID])
    W2T_d = din("W2T", [HID, D])
    idf_d = din("idf", [128, 128])
    idb_d = din("idb", [128, 128], BF16)
    maskW_d = din("maskW", [128, 128])
    maskU_d = din("maskU", [128, 128])
    smask_d = din("smask", [128, 6])
    out_d = nc.dram_tensor("out", [TOK, D], F32, kind="ExternalOutput").ap()
    dbg_d = nc.dram_tensor("dbg", [128, 4096], F32, kind="ExternalOutput").ap()

    cc_in = nc.dram_tensor("cc_in", [128, 256], F32).ap()
    cc_out = nc.dram_tensor("cc_out", [512, 256], F32).ap()
    Hb = nc.dram_tensor("Hb", [HID, TOK], BF16).ap()

    from contextlib import ExitStack
    es = ExitStack()
    SB = lambda n, s, d=F32: es.enter_context(nc.sbuf_tensor(n, s, d))
    PS = lambda n, s, d=F32: es.enter_context(nc.psum_tensor(n, s, d))

    with es:
        FA = SB("FA", [128, 12864], F32)

        def fa(off, ln):
            return FA[:, off:off + ln]
        CO = fa(0, 768)
        RW = fa(768, 2048)
        ONES = fa(2816, 128)
        IDF = fa(2944, 128)
        MKW = fa(3072, 128)
        MKU = fa(3200, 128)
        SMK = fa(3328, 6)
        BCOL = fa(3336, 8)
        B1T = fa(3352, 32)
        S1 = fa(3392, 2048)
        WMS = S1
        F1 = fa(5440, 2048)
        RCV = fa(5440, 768)
        CCP = fa(6208, 256)
        ZI = fa(6464, 256)
        GPT = fa(6720, 128)
        HPT = fa(6848, 128)
        F2 = fa(7488, 2048)
        OUT2 = fa(3392, 4096)
        TR1_ = fa(9536, 128)
        TR2_ = fa(9664, 128)
        YTL = fa(9792, 2048)
        W1S = YTL
        W2S = fa(11840, 1024)
        OSTG = fa(11840, 1024)

        BA = SB("BA", [128, 79104], BF16)

        def ba(off, ln):
            return BA[:, off:off + ln]
        R1 = ba(0, NT * D)
        R2 = ba(16384, ND * TOK)
        AT1o = 32768
        qT = ba(AT1o + 0, 2048)
        kT = ba(AT1o + 2048, 2048)
        vT = ba(AT1o + 4096, 2048)
        gT = ba(AT1o + 6144, 2048)
        UCB = ba(AT1o + 8192, 4096)
        PTBs = ba(AT1o + 12288, 2048)
        ULT = ba(AT1o + 14336, 2048)
        GTB = ba(AT1o + 16384, 2048)
        HCB = ba(AT1o + 18432, 2048)
        HTB = ba(AT1o + 20480, 2048)
        KNB = ba(AT1o + 22528, 2048)
        VNB = ba(AT1o + 24576, 2048)
        KPB = ba(AT1o + 26624, 2048)
        GUP = ba(AT1o + 28672, 2048)
        QTT = ba(AT1o + 30720, 2048)
        W2Q = ba(AT1o, NH * D)
        BF2o = 65536
        WQ = ba(BF2o + 0, 1024)
        WK = ba(BF2o + 1024, 1024)
        WV = ba(BF2o + 2048, 1024)
        WG = ba(BF2o + 3072, 1024)
        WO = ba(BF2o + 4096, 1024)
        WAB = ba(BF2o + 5120, 16)
        GB = ba(BF2o + 0, 2048)
        XQN = ba(BF2o + 2048, 2048)
        HSL = ba(BF2o + 0, 2 * TOK)
        W1Q = ba(BF2o + 4096, 2048)
        BF3o = 71680
        WCH = ba(BF3o + 0, 128)
        XP = ba(BF3o + 128, 256)
        YS = ba(BF3o + 384, 896)
        TPP = ba(BF3o + 1280, 512)
        MPT = ba(BF3o + 1792, 256)
        UFT = ba(BF3o + 2048, 256)
        ZBt = ba(BF3o + 2304, 256)
        ZTt = ba(BF3o + 2560, 256)
        GA = ba(BF3o + 2816, 256)
        GAT = ba(BF3o + 3072, 256)
        HA = ba(BF3o + 3328, 256)
        IDB = ba(BF3o + 3584, 128)
        ONEB = ba(BF3o + 3712, 128)
        GMB = ba(BF3o + 3840, 1024)
        BTB = ba(BF3o + 4864, 1024)
        BOB = ba(BF3o + 5888, 1024)
        B2B = BOB
        SQS = ba(BF2o + 4096, 1024)

        IA = SB("IA", [128, 512], I32)
        ZI32 = IA[:, 0:512]
        ZH32 = IA[:, 0:512]

        pb = [Bank(PS(f"pp{i}", [128, 512], F32)) for i in range(6)]
        btf = Bank(PS("ptf", [128, 512], F32))
        btb = Bank(PS("ptb", [128, 1024], BF16))

        em = Emitter()
        for s in ("dS", "dW0", "dW1", "dX", "dM0", "dM1", "dR0", "dR1", "dR2", "dR3", "dG", "dA", "v", "s", "t", "g", "cc", "out"):
            em.add_sem(s, es.enter_context(nc.semaphore(s)))

        def col(i, w=16):
            return CO[:, i * 16: i * 16 + w]

        def do_tap(name, region, width, waits):
            if tap == name:
                em.on("gpsimd", (lambda region=region, width=width: (lambda g: g.dma_start(
                    out=dbg_d[:, 0:width], in_=region)))(),
                    wait=list(waits), inc="dG", amt=16)

        (SUM1, SQ1, MU1, R1C, MR1, V1A, V1B,
         SUM2, SQ2, MU2, R2C, NM2,
         SUM3, SQ3, MU3, RHO, NMR, SXC, ISX,
         S2C, IS2, SUM5, SQ5, MU5, R5C, NM5,
         LCOL, BCOLc, GCOL, BGC, LLB, EKP, GLR,
         WM1, WM2, SW1C, ISW1, SW2C, ISW2,
         TMPA, TMPB, TMPC, WMT) = range(43)
        NCS = 43

        x_r = x_d.rearrange("(t p) d -> p t d", p=128)
        tok_x = em.on("gpsimd", lambda g: g.dma_start(
            out=R1[:, :].rearrange("p (t d) -> p t d", t=NT), in_=x_r), inc="dG", amt=16)
        xT_r = xT_d.rearrange("(k p) t -> p k t", p=128)
        tok_xT = em.on("gpsimd", lambda g: g.dma_start(
            out=R2[:, :].rearrange("p (k t) -> p k t", k=ND), in_=xT_r), inc="dG", amt=16)

        def ldw(dst, src, m):
            src_r = src.rearrange("(k p) c -> p k c", p=128)
            return lambda g: g.dma_start(
                out=dst[:, :].rearrange("p (k c) -> p k c", k=ND), in_=src_r)
        tok_wq = em.on("gpsimd", ldw(WQ, Wq_d, C), inc="dG", amt=16)
        tok_wk = em.on("gpsimd", ldw(WK, Wk_d, C), inc="dG", amt=16)
        tok_wv = em.on("gpsimd", ldw(WV, Wv_d, C), inc="dG", amt=16)
        tok_wg = em.on("gpsimd", ldw(WG, Wg_d, C), inc="dG", amt=16)
        tok_wab = em.on("gpsimd", ldw(WAB, Wab_d, 2), inc="dG", amt=16)
        tok_wo = em.on("gpsimd", lambda g: g.dma_start(out=WO[:, :], in_=Wo_d), inc="dG", amt=16)

        csts = []
        for dst, src in [(IDF, idf_d), (MKW, maskW_d), (MKU, maskU_d),
                         (B1T, b1T_d)]:
            csts.append(em.on("sync", (lambda dst=dst, src=src: (
                lambda sy: sy.dma_start(out=dst[:, :], in_=src)))(), inc="dS", amt=16))
        CIDF, CMKW, CMKU, CB1T = csts
        CBOB = em.on("gpsimd", lambda g: g.dma_start(out=BOB[:, :], in_=bo_b_d), inc="dG", amt=16)
        CGMB = em.on("gpsimd", lambda g: g.dma_start(out=GMB[:, :], in_=gam_b_d), inc="dG", amt=16)
        CBTB = em.on("gpsimd", lambda g: g.dma_start(out=BTB[:, :], in_=bet_b_d), inc="dG", amt=16)
        tok_idb = em.on("sync", lambda sy: sy.dma_start(out=IDB[:, :], in_=idb_d), inc="dS", amt=16)
        tok_smk = em.on("sync", lambda sy: sy.dma_start(out=SMK[:, :], in_=smask_d), inc="dS", amt=16)
        for i, src in enumerate([bq_d, bk_d, bv_d, bg_d]):
            csts.append(em.on("sync", (lambda i=i, src=src: (
                lambda sy: sy.dma_start(out=BCOL[:, i:i + 1], in_=src)))(), inc="dS", amt=16))
        tok_bab = em.on("sync", lambda sy: sy.dma_start(out=BCOL[0:2, 4:5], in_=bab_d), inc="dS", amt=16)
        csts.append(tok_bab)
        tok_cst = csts[-1]

        tok_ones = em.on("vector", lambda v: v.memset(ONES[:, :], 1.0), inc="v")
        tok_epsc = em.on("vector", lambda v: v.memset(BCOL[:, 5:6], EPS), wait=[tok_bab], inc="v")
        EPSC = BCOL[:, 5:6]
        tok_oneb = em.on("vector", lambda v: v.memset(ONEB[:, 0:1], 1.0), inc="v")
        tok_negb = em.on("vector", lambda v: v.memset(ONEB[:, 1:2], -1.0), inc="v")

        rS1 = Reg()
        st_toks = []
        for t in range(NT):
            ts_ = em.on("vector", (lambda t=t: (lambda v: v.reduce_sum(
                col(SUM1)[:, t:t + 1], R1[:, t * D:(t + 1) * D], axis=AX.X)))(),
                wait=[tok_x], inc="v")
            tq = em.on("scalar", (lambda t=t: (lambda s: s.activation(
                S1[:, 0:D], R1[:, t * D:(t + 1) * D], AF.Square,
                accum_out=col(SQ1)[:, t:t + 1])))(), wait=[tok_x] + rS1.w(), inc="s")
            rS1.note_w(tq)
            st_toks += [ts_, tq]
        t_mu1 = em.on("vector", lambda v: v.tensor_scalar(
            col(MU1), col(SUM1), 1.0 / D, None, OP.mult), wait=st_toks, inc="v")
        t_e2 = em.on("vector", lambda v: v.tensor_scalar(
            col(V1A), col(SQ1), 1.0 / D, None, OP.mult), wait=st_toks, inc="v")
        t_m2 = em.on("vector", lambda v: v.tensor_mul(col(V1B), col(MU1), col(MU1)),
                     wait=[t_mu1], inc="v")
        t_v1 = em.on("vector", lambda v: v.tensor_sub(col(V1A), col(V1A), col(V1B)),
                     wait=[t_e2, t_m2], inc="v")
        t_sd = em.on("scalar", lambda s: s.activation(col(V1B), col(V1A), AF.Sqrt, bias=EPSC),
                     wait=[t_v1, tok_epsc], inc="s")
        t_r1 = em.on("vector", lambda v: v.reciprocal(col(R1C), col(V1B)), wait=[t_sd], inc="v")
        t_mr1 = em.on("vector", lambda v: v.tensor_mul(col(MR1), col(MU1), col(R1C)),
                      wait=[t_r1], inc="v")

        rROW0 = Reg()
        rRW16 = Reg()

        def bcast_from_col(col_ap, dstreg, dst_ap, pre_toks):
            wb = btf.w()
            tt = em.on("tensor", (lambda col_ap=col_ap: (lambda t: t.transpose(
                btf.ap[0:16, 0:128], col_ap, IDF[:, :])))(),
                wait=wb + list(pre_toks) + [CIDF], inc="t")
            btf.note_w(tt)
            tv = em.on("vector", lambda v: v.tensor_copy(RW[32:48, 384:512], btf.ap[0:16, 0:128]),
                       wait=btf.r() + rRW16.w(), inc="v")
            btf.note_r(tv)
            rRW16.note_w(tv)
            td = em.on("gpsimd", lambda g: g.dma_start(out=RW[0:1, 0:2048], in_=RW[32:48, 384:512]),
                       wait=rRW16.r() + rROW0.w(), inc="dG", amt=16)
            rRW16.note_r(td)
            rROW0.note_w(td)
            return bcast_from_row(RW[0:1, 0:2048], rROW0, dstreg, dst_ap, [td])

        def bcast_from_row(row_ap, rowreg, dstreg, dst_ap, pre_toks):
            last = None
            wdst = dstreg.w()
            for nchk in range(4):
                wb = pb[5].w()
                tm = em.on("tensor", (lambda nchk=nchk, row_ap=row_ap: (lambda t: t.matmul(
                    pb[5].ap[:, 0:512], ONES[0:1, :], row_ap[0:1, nchk * 512:(nchk + 1) * 512],
                    start=True, stop=True)))(),
                    wait=wb + list(pre_toks) + [tok_ones], inc="t")
                pb[5].note_w(tm)
                te = em.on("vector", (lambda nchk=nchk, dst_ap=dst_ap: (lambda v: v.tensor_copy(
                    dst_ap[:, nchk * 512:(nchk + 1) * 512], pb[5].ap[:, 0:512])))(),
                    wait=pb[5].r() + (wdst if nchk == 0 else []), inc="v")
                pb[5].note_r(te)
                if rowreg is not None:
                    rowreg.note_r(te)
                last = te
            dstreg.note_w(last)
            return last

        rF1 = Reg()
        rF2 = Reg()
        tok_r1b = bcast_from_col(col(R1C), rF1, F1, [t_r1])
        tok_mr1b = bcast_from_col(col(MR1), rF2, F2, [t_mr1])

        def colsum(Wt, slot, M, wtok):
            wb = btf.w()
            mm = []
            for k in range(ND):
                tm = em.on("tensor", (lambda k=k, Wt=Wt, M=M: (lambda t: t.matmul(
                    btf.ap[0:M, 2:3], Wt[:, k * M:(k + 1) * M], ONEB[:, 1:2],
                    start=(k == 0), stop=(k == ND - 1))))(),
                    wait=(wb if k == 0 else []) + [wtok, tok_negb] + (mm[-1:] if mm else []),
                    inc="t")
                mm.append(tm)
            btf.note_w(mm[-1])
            te = em.on("vector", (lambda M=M, slot=slot: (lambda v: v.tensor_copy(
                col(slot)[0:M, 0:1], btf.ap[0:M, 2:3])))(), wait=btf.r(), inc="v")
            btf.note_r(te)
            return te
        tok_csq = colsum(WQ, NCS + 0, C, tok_wq)
        tok_csk = colsum(WK, NCS + 1, C, tok_wk)
        tok_csv = colsum(WV, NCS + 2, C, tok_wv)
        tok_csg = colsum(WG, NCS + 3, C, tok_wg)
        tok_csab = colsum(WAB, NCS + 4, 2, tok_wab)

        rYTL = Reg()

        def project(Wt, M, cstok, cs_slot, bias_idx, act, dst_ap, wtok, dstreg, xw=()):
            wy = rYTL.w()
            ev = []
            pair_mms = {}
            for pair in range(2):
                banks = [pb[2 * pair], pb[2 * pair + 1]]
                wbs = [b.w() for b in banks]
                mms = [[], []]
                for k in range(ND):
                    for j in range(2):
                        nchk = 2 * pair + j
                        tm = em.on("tensor", (lambda k=k, nchk=nchk, bank=banks[j], Wt=Wt, M=M: (lambda t: t.matmul(
                            bank.ap[0:M, 0:512], Wt[:, k * M:(k + 1) * M],
                            R2[:, k * TOK + nchk * 512: k * TOK + (nchk + 1) * 512],
                            start=(k == 0), stop=(k == ND - 1))))(),
                            wait=(wbs[j] if k == 0 else []) + [wtok, tok_xT],
                            inc="t")
                        mms[j].append(tm)
                for j in range(2):
                    banks[j].note_w(mms[j][-1])
                    pair_mms[2 * pair + j] = mms[j]
            for nchk in range(4):
                bank = pb[nchk]
                te1 = em.on("vector", (lambda nchk=nchk, bank=bank, M=M: (lambda v: v.tensor_mul(
                    YTL[0:M, nchk * 512:(nchk + 1) * 512], bank.ap[0:M, 0:512],
                    F1[0:M, nchk * 512:(nchk + 1) * 512])))(),
                    wait=bank.r() + [tok_r1b] + (wy if nchk == 0 else []), inc="v")
                bank.note_r(te1)
                rF1.note_r(te1)
                te2 = em.on("vector", (lambda nchk=nchk, M=M, cs_slot=cs_slot: (lambda v: v.scalar_tensor_tensor(
                    YTL[0:M, nchk * 512:(nchk + 1) * 512],
                    F2[0:M, nchk * 512:(nchk + 1) * 512], col(cs_slot)[0:M, 0:1],
                    YTL[0:M, nchk * 512:(nchk + 1) * 512], OP.mult, OP.add)))(),
                    wait=[te1, tok_mr1b, cstok], inc="v")
                rF2.note_r(te2)
                ev.append(te2)
            tse = em.on("scalar", (lambda M=M, bias_idx=bias_idx, act=act, dst_ap=dst_ap: (lambda s: s.activation(
                dst_ap, YTL[0:M, 0:TOK], act, bias=BCOL[0:M, bias_idx:bias_idx + 1])))(),
                wait=ev + list(xw) + [tok_cst], inc="s")
            rYTL.note_w(tse)
            rYTL.note_r(tse)
            if dstreg is not None:
                dstreg.note_w(tse)
            return tse

        tok_qT = project(WQ, C, tok_csq, NCS + 0, 0, AF.Identity, qT[:, 0:TOK], tok_wq, None)
        tok_kTr = project(WK, C, tok_csk, NCS + 1, 1, AF.Identity, S1[:, 0:TOK], tok_wk, rS1)
        tok_vT = project(WV, C, tok_csv, NCS + 2, 2, AF.Identity, vT[:, 0:TOK], tok_wv, None)
        tok_gT = project(WG, C, tok_csg, NCS + 3, 3, AF.Sigmoid, gT[:, 0:TOK], tok_wg, None)
        tok_ab = project(WAB, 2, tok_csab, NCS + 4, 4, AF.Sigmoid, RW[0:2, 0:TOK],
                         tok_wab, None, xw=rROW0.w())
        rROW0.note_w(tok_ab)

        t_k2 = em.on("scalar", lambda s: s.activation(F2[:, 0:TOK], S1[:, 0:TOK], AF.Square),
                     wait=rF2.w() + rS1.r() + [tok_kTr, tok_ab], inc="s")
        rF2.note_w(t_k2)
        rS1.note_r(t_k2)
        kn_last = None
        for nchk in range(4):
            wb = btf.w()
            tm = em.on("tensor", (lambda nchk=nchk: (lambda t: t.matmul(
                btf.ap[0:1, 0:512], ONES[:, 0:1], F2[:, nchk * 512:(nchk + 1) * 512],
                start=True, stop=True)))(), wait=wb + rF2.r() + [tok_ones], inc="t")
            btf.note_w(tm)
            rF2.note_r(tm)
            te = em.on("vector", (lambda nchk=nchk: (lambda v: v.tensor_copy(
                YTL[0:1, nchk * 512:(nchk + 1) * 512], btf.ap[0:1, 0:512])))(),
                wait=btf.r() + (rYTL.w() if nchk == 0 else []), inc="v")
            btf.note_r(te)
            kn_last = te
        rYTL.note_w(kn_last)
        t_ks = em.on("scalar", lambda s: s.activation(YTL[0:1, 0:TOK], YTL[0:1, 0:TOK], AF.Sqrt),
                     wait=rYTL.w(), inc="s")
        rYTL.note_w(t_ks)
        t_kc = em.on("vector", lambda v: v.tensor_scalar(
            YTL[0:1, 0:TOK], YTL[0:1, 0:TOK], 1e-12, None, OP.max), wait=rYTL.w(), inc="v")
        rYTL.note_w(t_kc)
        t_kr = em.on("vector", lambda v: v.reciprocal(YTL[0:1, 0:TOK], YTL[0:1, 0:TOK]),
                     wait=rYTL.w(), inc="v")
        rYTL.note_w(t_kr)
        tok_rkb = bcast_from_row(YTL[0:1, 0:TOK], rYTL, rF2, F2, rF2.w() + [t_kr])
        tok_kT = em.on("vector", lambda v: v.tensor_mul(kT[:, 0:TOK], S1[:, 0:TOK], F2[:, 0:TOK]),
                       wait=[tok_rkb] + rS1.r(), inc="v")
        rS1.note_r(tok_kT)
        rF2.note_r(tok_kT)
        w1_r = W1T_d.rearrange("(k p) h -> p k h", p=128)
        w2_r = W2T_d.rearrange("(k p) dd -> p k dd", p=128)
        w1_slices = [w1_r[:, k, hh * 1024:(hh + 1) * 1024] for k in range(ND) for hh in range(4)]
        w2_slices = [w2_r[:, i, :] for i in range(NH)]
        rWMb = [Reg(), Reg()]
        wmall = []
        wm_parts = {WM1: [], WM2: []}
        wm_sub = {WM1: (SUM1, SQ1), WM2: (MU1, R1C)}
        wm_work = [(a, WM1) for a in w1_slices] + [(a, WM2) for a in w2_slices]
        wm_j = [0]

        def emit_wmean(n):
            for _ in range(n):
                j = wm_j[0]
                if j >= len(wm_work):
                    return
                wm_j[0] += 1
                src_ap, slot = wm_work[j]
                bb_ = j % 2
                td = em.on("sync", (lambda src_ap=src_ap, bb_=bb_: (lambda sy: sy.dma_start(
                    out=F2[:, bb_ * 1024:(bb_ + 1) * 1024], in_=src_ap)))(),
                    wait=rWMb[bb_].w() + ([tok_kT] if j <= 1 else []),
                    inc="dM0" if bb_ == 0 else "dM1", amt=16)
                rWMb[bb_].note_w(td)
                i2 = len(wm_parts[slot])
                tslot = wm_sub[slot][0] if i2 < 16 else wm_sub[slot][1]
                ta = em.on("scalar", (lambda bb_=bb_, tslot=tslot, i2=i2: (lambda s: s.activation(
                    F2[:, bb_ * 1024:(bb_ + 1) * 1024], F2[:, bb_ * 1024:(bb_ + 1) * 1024],
                    AF.Abs, accum_out=col(tslot)[:, i2 % 16:i2 % 16 + 1])))(),
                    wait=rWMb[bb_].w(), inc="s")
                rWMb[bb_].note_w(ta)
                wm_parts[slot].append(ta)
                if len(wm_parts[slot]) == 32:
                    tm_ = em.on("vector", (lambda slot=slot: (lambda v: v.tensor_add(
                        col(slot), col(wm_sub[slot][0]), col(wm_sub[slot][1]))))(),
                        wait=wm_parts[slot], inc="v")
                    wm_parts[slot] = [tm_]
                    wmall.append(tm_)
        emit_wmean(len(wm_work))
        wm1_toks = [wmall[0]]
        wm2_toks = [wmall[1]]

        t_la = em.on("scalar", lambda s: s.activation(RW[0:1, 0:TOK], RW[0:1, 0:TOK], AF.Ln),
                     wait=[tok_ab] + rROW0.w(), inc="s")
        rROW0.note_w(t_la)
        rL16 = [Reg(), Reg()]
        rB16 = Reg()
        t_l16 = em.on("gpsimd", lambda g: g.dma_start(out=RW[32:48, 0:128], in_=RW[0:1, 0:TOK]),
                      wait=rROW0.r() + rL16[0].w(), inc="dG", amt=16)
        rROW0.note_r(t_l16)
        rL16[0].note_w(t_l16)
        t_b16 = em.on("gpsimd", lambda g: g.dma_start(out=RW[32:48, 256:384], in_=RW[1:2, 0:TOK]),
                      wait=[tok_ab] + rB16.w(), inc="dG", amt=16)
        rB16.note_w(t_b16)
        srco, dsto = 0, 1
        for lvl in range(7):
            sft = 1 << lvl
            w = rL16[srco].r() + rL16[dsto].w()
            t_c1 = em.on("vector", (lambda srco=srco, dsto=dsto, sft=sft: (lambda v: v.tensor_copy(
                RW[32:48, dsto * 128:dsto * 128 + sft], RW[32:48, srco * 128:srco * 128 + sft])))(),
                wait=w, inc="v")
            t_c2 = em.on("vector", (lambda srco=srco, dsto=dsto, sft=sft: (lambda v: v.tensor_add(
                RW[32:48, dsto * 128 + sft:dsto * 128 + 128],
                RW[32:48, srco * 128 + sft:srco * 128 + 128],
                RW[32:48, srco * 128:srco * 128 + 128 - sft])))(), wait=w, inc="v")
            rL16[srco].note_r(t_c2)
            rL16[dsto].note_w(t_c2)
            rL16[dsto].note_w(t_c1)
            rL16[dsto].note_w(t_c2)
            srco, dsto = dsto, srco
        l16o = srco * 128
        tok_l16 = rL16[srco].lw

        wb = btf.w()
        t_lc = em.on("tensor", (lambda l16o=l16o: (lambda t: t.transpose(
            btf.ap[:, 0:16], RW[32:48, l16o:l16o + 128], IDF[32:48, 32:48])))(),
            wait=wb + rL16[srco].r() + [CIDF], inc="t")
        btf.note_w(t_lc)
        rL16[srco].note_r(t_lc)
        t_lcv = em.on("vector", lambda v: v.tensor_copy(col(LCOL), btf.ap[:, 0:16]),
                      wait=btf.r(), inc="v")
        btf.note_r(t_lcv)
        wb = btf.w()
        t_bc_ = em.on("tensor", lambda t: t.transpose(
            btf.ap[:, 16:32], RW[32:48, 256:384], IDF[32:48, 32:48]),
            wait=wb + rB16.r() + [CIDF], inc="t")
        btf.note_w(t_bc_)
        rB16.note_r(t_bc_)
        t_bcv = em.on("vector", lambda v: v.tensor_copy(col(BCOLc), btf.ap[:, 16:32]),
                      wait=btf.r(), inc="v")
        btf.note_r(t_bcv)
        t_gc = em.on("scalar", lambda s: s.activation(col(GCOL), col(LCOL), AF.Exp),
                     wait=[t_lcv], inc="s")
        t_bg = em.on("vector", lambda v: v.tensor_mul(col(BGC), col(BCOLc), col(GCOL)),
                     wait=[t_bcv, t_gc], inc="v")
        rRW4 = Reg()
        t_ll = em.on("gpsimd", (lambda l16o=l16o: (lambda g: g.dma_start(
            out=col(V1A)[0:1, 0:16], in_=RW[32:48, l16o + 127:l16o + 128])))(),
            wait=rL16[srco].r() + rRW4.w(), inc="dG", amt=16)
        rL16[srco].note_r(t_ll)
        rRW4.note_w(t_ll)
        wb = btf.w()
        t_llb = em.on("tensor", lambda t: t.matmul(
            btf.ap[:, 32:48], ONES[0:1, :], col(V1A)[0:1, 0:16], start=True, stop=True),
            wait=wb + rRW4.r() + [tok_ones], inc="t")
        btf.note_w(t_llb)
        rRW4.note_r(t_llb)
        t_llbv = em.on("vector", lambda v: v.tensor_copy(col(LLB), btf.ap[:, 32:48]),
                       wait=btf.r(), inc="v")
        btf.note_r(t_llbv)
        t_ek1 = em.on("vector", lambda v: v.tensor_sub(col(TMPC), col(LLB), col(LCOL)),
                      wait=[t_llbv, t_lcv], inc="v")
        t_ekp = em.on("scalar", lambda s: s.activation(col(EKP), col(TMPC), AF.Exp),
                      wait=[t_ek1], inc="s")
        t_glr = em.on("scalar", lambda s: s.activation(col(GLR), col(LLB), AF.Exp),
                      wait=[t_llbv], inc="s")

        t_lrow = em.on("gpsimd", (lambda l16o=l16o: (lambda g: g.dma_start(
            out=RW[0:1, 0:TOK], in_=RW[32:48, l16o:l16o + 128])))(),
            wait=rL16[srco].r() + rROW0.w(), inc="dG", amt=16)
        rL16[srco].note_r(t_lrow)
        rROW0.note_w(t_lrow)
        tok_ljb = bcast_from_row(RW[0:1, 0:TOK], rROW0, rS1, S1, rS1.w() + [t_lrow])
        tok_gb = em.on("scalar", lambda s: s.activation(GB[:, 0:TOK], S1[:, 0:TOK], AF.Exp),
                       wait=rS1.r(), inc="s")
        rS1.note_r(tok_gb)

        rWCH = Reg(); rXP = [Reg(), Reg()]; rYS = [Reg() for _ in range(7)]
        rTPP = [Reg(), Reg()]; rTR1 = Reg(); rTR2 = Reg()
        tgu_toks = [None] * NT
        tqt_toks = [None] * NT
        hcb_toks = [None] * NT
        htb_toks = [None] * NT
        gtb_toks = [None] * NT
        knb_toks = [None] * NT
        ucb_toks = [None] * NT
        ptb_toks = [None] * NT
        ult_toks = [None] * NT
        ga_tok = []
        gacc_cur = 0
        for c in range(NT):
            cC = slice(c * C, (c + 1) * C)
            wb = btb.w()
            tt1 = em.on("tensor", (lambda c=c: (lambda t: t.transpose(
                btb.ap[:, 0:128], kT[:, c * C:(c + 1) * C], IDB[:, :])))(),
                wait=wb + [tok_kT, tok_idb], inc="t")
            tt2 = em.on("tensor", (lambda c=c: (lambda t: t.transpose(
                btb.ap[:, 128:256], vT[:, c * C:(c + 1) * C], IDB[:, :])))(),
                wait=[tok_vT], inc="t")
            btb.note_w(tt1); btb.note_w(tt2)
            rb = btb.r()
            tv1 = em.on("vector", (lambda c=c: (lambda v: v.tensor_copy(
                KNB[:, c * C:(c + 1) * C], btb.ap[:, 0:128])))(), wait=rb, inc="v")
            tv2 = em.on("vector", (lambda c=c: (lambda v: v.tensor_copy(
                VNB[:, c * C:(c + 1) * C], btb.ap[:, 128:256])))(), wait=rb, inc="v")
            btb.note_r(tv1); btb.note_r(tv2)
            knb_toks[c] = tv1
            tkp = em.on("vector", (lambda c=c: (lambda v: v.tensor_scalar(
                KPB[:, c * C:(c + 1) * C], KNB[:, c * C:(c + 1) * C],
                col(EKP)[:, c:c + 1], None, OP.mult)))(), wait=[tv1, t_ekp], inc="v")
            wb = pb[2].w()
            tkk = em.on("tensor", (lambda c=c: (lambda t: t.matmul(
                pb[2].ap[:, 0:128], kT[:, c * C:(c + 1) * C], kT[:, c * C:(c + 1) * C],
                start=True, stop=True)))(), wait=wb + [tok_kT], inc="t")
            pb[2].note_w(tkk)
            td1 = em.on("vector", (lambda c=c: (lambda v: v.tensor_scalar(
                TR1_[:, :], S1[:, c * C:(c + 1) * C], col(LCOL)[:, c:c + 1], None,
                OP.subtract)))(), wait=rTR1.w() + [tok_ljb, t_lcv], inc="v")
            rS1.note_r(td1)
            rTR1.note_w(td1)
            td2 = em.on("vector", lambda v: v.tensor_add(TR2_[:, :], TR1_[:, :], MKW[:, :]),
                        wait=rTR2.w() + rTR1.r() + [CMKW], inc="v")
            rTR1.note_r(td2)
            rTR2.note_w(td2)
            tgs = em.on("scalar", lambda s: s.activation(XP[:, 0:128], TR2_[:, :], AF.Exp, scale=-1.0),
                        wait=rTR2.r() + rXP[0].w(), inc="s")
            rTR2.note_r(tgs)
            rXP[0].note_w(tgs)
            tw = em.on("vector", (lambda c=c: (lambda v: v.scalar_tensor_tensor(
                WCH[:, :], pb[2].ap[:, 0:128], col(BCOLc)[:, c:c + 1], XP[:, 0:128],
                OP.mult, OP.mult)))(), wait=pb[2].r() + rXP[0].r() + rWCH.w() + [t_bcv], inc="v")
            pb[2].note_r(tw)
            rXP[0].note_r(tw)
            rWCH.note_w(tw)
            tdu = em.on("vector", lambda v: v.tensor_add(TR2_[:, :], TR1_[:, :], MKU[:, :]),
                        wait=rTR2.w() + rTR1.r() + [CMKU], inc="v")
            rTR1.note_r(tdu)
            rTR2.note_w(tdu)
            tgu = em.on("scalar", (lambda c=c: (lambda s: s.activation(
                GUP[:, c * C:(c + 1) * C], TR2_[:, :], AF.Exp)))(), wait=rTR2.r(), inc="s")
            rTR2.note_r(tgu)
            tgu_toks[c] = tgu
            tqt = em.on("vector", (lambda c=c: (lambda v: v.tensor_mul(
                QTT[:, c * C:(c + 1) * C], qT[:, c * C:(c + 1) * C],
                GB[:, c * C:(c + 1) * C])))(), wait=[tok_qT, tok_gb], inc="v")
            tqt_toks[c] = tqt
            wb = btb.w()
            ty0 = em.on("tensor", lambda t: t.transpose(btb.ap[:, 256:384], WCH[:, :], IDB[:, :]),
                        wait=wb + rWCH.r() + [tok_idb], inc="t")
            btb.note_w(ty0)
            rWCH.note_r(ty0)
            ty0v = em.on("vector", lambda v: v.tensor_copy(YS[:, 0:128], btb.ap[:, 256:384]),
                         wait=btb.r() + rYS[0].w(), inc="v")
            btb.note_r(ty0v)
            rYS[0].note_w(ty0v)
            w0 = rTPP[0].w()
            tr1 = em.on("vector", (lambda c=c: (lambda v: v.tensor_scalar(
                TPP[:, 0:128], VNB[:, c * C:(c + 1) * C], col(BCOLc)[:, c:c + 1],
                None, OP.mult)))(), wait=w0 + [tv2, t_bcv], inc="v")
            tr2 = em.on("vector", (lambda c=c: (lambda v: v.tensor_scalar(
                TPP[:, 128:256], KNB[:, c * C:(c + 1) * C], col(BGC)[:, c:c + 1],
                None, OP.mult)))(), wait=w0 + [tv1, t_bg], inc="v")
            rTPP[0].note_w(tr1)
            rTPP[0].note_w(tr2)
            if c == 0:
                do_tap("wch0", WCH[:, :], 128, [tw])
                do_tap("ys00", YS[:, 0:128], 128, [ty0v])
                do_tap("tpp0", TPP[:, 0:256], 256, [tr1, tr2])
                do_tap("kk0", pb[2].ap[:, 0:128], 128, [tkk]) if False else None
            xcur_reg, xcur_ap = rWCH, WCH[:, :]
            ycur_reg, ycur_ap = rYS[0], YS[:, 0:128]
            for kk in range(1, 7):
                if kk <= 5:
                    xo = kk % 2
                    wb = pb[3].w()
                    tmx = em.on("tensor", (lambda a=ycur_ap, b=xcur_ap: (lambda t: t.matmul(
                        pb[3].ap[:, 0:128], a, b, start=True, stop=True)))(),
                        wait=wb + ycur_reg.r() + xcur_reg.r(), inc="t")
                    pb[3].note_w(tmx)
                    ycur_reg.note_r(tmx)
                    xcur_reg.note_r(tmx)
                    tvx = em.on("vector", (lambda xo=xo: (lambda v: v.tensor_copy(
                        XP[:, xo * 128:(xo + 1) * 128], pb[3].ap[:, 0:128])))(),
                        wait=pb[3].r() + rXP[xo].w(), inc="v")
                    pb[3].note_r(tvx)
                    rXP[xo].note_w(tvx)
                wb = pb[4].w()
                tmy = em.on("tensor", (lambda a=xcur_ap, b=ycur_ap: (lambda t: t.matmul(
                    pb[4].ap[:, 0:128], a, b, start=True, stop=True)))(),
                    wait=wb + ycur_reg.r() + xcur_reg.r(), inc="t")
                pb[4].note_w(tmy)
                ycur_reg.note_r(tmy)
                xcur_reg.note_r(tmy)
                tvy = em.on("vector", (lambda kk=kk: (lambda v: v.tensor_copy(
                    YS[:, kk * 128:(kk + 1) * 128], pb[4].ap[:, 0:128])))(),
                    wait=pb[4].r() + rYS[kk].w(), inc="v")
                pb[4].note_r(tvy)
                rYS[kk].note_w(tvy)
                if kk <= 5:
                    xcur_reg, xcur_ap = rXP[xo], XP[:, xo * 128:(xo + 1) * 128]
                ycur_reg, ycur_ap = rYS[kk], YS[:, kk * 128:(kk + 1) * 128]
            if c == 0:
                do_tap("ys10", YS[:, 128:256], 128, [])
                do_tap("ysall", YS[:, 0:896], 896, [])
            for i, kk in enumerate(range(6, -1, -1)):
                cur, nxt = i % 2, 1 - (i % 2)
                wb = pb[3].w()
                tm = em.on("tensor", (lambda cur=cur, kk=kk: (lambda t: t.matmul(
                    pb[3].ap[:, 0:256], YS[:, kk * 128:(kk + 1) * 128],
                    TPP[:, cur * 256:(cur + 1) * 256], start=True, stop=True)))(),
                    wait=wb + rYS[kk].r() + rTPP[cur].r(), inc="t")
                pb[3].note_w(tm)
                rYS[kk].note_r(tm)
                rTPP[cur].note_r(tm)
                if kk > 0:
                    tv = em.on("vector", (lambda cur=cur, nxt=nxt: (lambda v: v.tensor_add(
                        TPP[:, nxt * 256:(nxt + 1) * 256], TPP[:, cur * 256:(cur + 1) * 256],
                        pb[3].ap[:, 0:256])))(),
                        wait=pb[3].r() + rTPP[nxt].w() + rTPP[cur].r(), inc="v")
                    rTPP[cur].note_r(tv)
                    rTPP[nxt].note_w(tv)
                else:
                    tv = em.on("vector", (lambda c=c, cur=cur: (lambda v: v.tensor_sub(
                        UCB[:, c * 256:(c + 1) * 256], TPP[:, cur * 256:(cur + 1) * 256],
                        pb[3].ap[:, 0:256])))(), wait=pb[3].r() + rTPP[cur].r(), inc="v")
                    rTPP[cur].note_r(tv)
                pb[3].note_r(tv)
            tok_ucb = tv
            ucb_toks[c] = tok_ucb
            wb = btb.w()
            tpt = em.on("tensor", (lambda c=c: (lambda t: t.transpose(
                btb.ap[:, 384:512], UCB[:, c * 256 + 128:c * 256 + 256], IDB[:, :])))(),
                wait=wb + [tok_ucb], inc="t")
            tut = em.on("tensor", (lambda c=c: (lambda t: t.transpose(
                btb.ap[:, 512:640], UCB[:, c * 256:c * 256 + 128], IDB[:, :])))(),
                wait=[tok_ucb], inc="t")
            btb.note_w(tpt); btb.note_w(tut)
            rb = btb.r()
            tptv = em.on("vector", (lambda c=c: (lambda v: v.tensor_copy(
                PTBs[:, c * C:(c + 1) * C], btb.ap[:, 384:512])))(), wait=rb, inc="v")
            tutv = em.on("vector", (lambda c=c: (lambda v: v.tensor_copy(
                ULT[:, c * C:(c + 1) * C], btb.ap[:, 512:640])))(), wait=rb, inc="v")
            btb.note_r(tptv); btb.note_r(tutv)
            ptb_toks[c] = tptv
            ult_toks[c] = tutv
            wb = pb[2].w()
            thm = em.on("tensor", (lambda c=c: (lambda t: t.matmul(
                pb[2].ap[:, 0:128], KPB[:, c * C:(c + 1) * C],
                UCB[:, c * 256:c * 256 + 128], start=True, stop=True)))(),
                wait=wb + [tkp, tok_ucb], inc="t")
            thtm = em.on("tensor", (lambda c=c: (lambda t: t.matmul(
                pb[2].ap[:, 128:256], UCB[:, c * 256:c * 256 + 128],
                KPB[:, c * C:(c + 1) * C], start=True, stop=True)))(), inc="t")
            tgm = em.on("tensor", (lambda c=c: (lambda t: t.matmul(
                pb[2].ap[:, 256:384], UCB[:, c * 256 + 128:c * 256 + 256],
                KPB[:, c * C:(c + 1) * C], start=True, stop=True)))(), inc="t")
            pb[2].note_w(thm); pb[2].note_w(thtm); pb[2].note_w(tgm)
            rb = pb[2].r()
            thv = em.on("vector", (lambda c=c: (lambda v: v.tensor_copy(
                HCB[:, c * C:(c + 1) * C], pb[2].ap[:, 0:128])))(), wait=rb, inc="v")
            thtv = em.on("vector", (lambda c=c: (lambda v: v.tensor_copy(
                HTB[:, c * C:(c + 1) * C], pb[2].ap[:, 128:256])))(), inc="v")
            tgv = em.on("vector", (lambda c=c: (lambda v: v.scalar_tensor_tensor(
                GTB[:, c * C:(c + 1) * C], IDF[:, :], col(GLR)[:, c:c + 1],
                pb[2].ap[:, 256:384], OP.mult, OP.subtract)))(),
                wait=[t_glr, CIDF], inc="v")
            pb[2].note_r(thv); pb[2].note_r(thtv); pb[2].note_r(tgv)
            hcb_toks[c] = thv
            htb_toks[c] = thtv
            gtb_toks[c] = tgv
            if c == 0:
                wb = btb.w()
                tga = em.on("tensor", lambda t: t.transpose(btb.ap[:, 640:768], GTB[:, 0:C], IDB[:, :]),
                            wait=wb + [tgv], inc="t")
                btb.note_w(tga)
                tgav = em.on("vector", lambda v: v.tensor_copy(GA[:, 0:C], btb.ap[:, 640:768]),
                             wait=btb.r(), inc="v")
                btb.note_r(tgav)
                tgatv = em.on("vector", lambda v: v.tensor_copy(GAT[:, 0:C], GTB[:, 0:C]),
                              wait=[tgv], inc="v")
                thav = em.on("vector", lambda v: v.tensor_copy(HA[:, 0:C], HCB[:, 0:C]),
                             wait=[thv], inc="v")
                ga_tok = [tgav, tgatv, thav]
                gacc_cur = 0
            else:
                prv, nxt = gacc_cur, 1 - gacc_cur
                wb = pb[2].w()
                tm1 = em.on("tensor", (lambda c=c, prv=prv: (lambda t: t.matmul(
                    pb[2].ap[:, 384:512], GTB[:, c * C:(c + 1) * C],
                    GA[:, prv * C:(prv + 1) * C], start=True, stop=True)))(),
                    wait=wb + ga_tok + [tgv], inc="t")
                pb[2].note_w(tm1)
                tv1_ = em.on("vector", (lambda nxt=nxt: (lambda v: v.tensor_copy(
                    GA[:, nxt * C:(nxt + 1) * C], pb[2].ap[:, 384:512])))(),
                    wait=pb[2].r(), inc="v")
                pb[2].note_r(tv1_)
                wb = pb[2].w()
                tm2 = em.on("tensor", (lambda c=c, prv=prv: (lambda t: t.matmul(
                    pb[2].ap[:, 384:512], GA[:, prv * C:(prv + 1) * C],
                    GTB[:, c * C:(c + 1) * C], start=True, stop=True)))(),
                    wait=wb + ga_tok + [tgv], inc="t")
                pb[2].note_w(tm2)
                tv2_ = em.on("vector", (lambda nxt=nxt: (lambda v: v.tensor_copy(
                    GAT[:, nxt * C:(nxt + 1) * C], pb[2].ap[:, 384:512])))(),
                    wait=pb[2].r(), inc="v")
                pb[2].note_r(tv2_)
                wb = pb[2].w()
                tm3 = em.on("tensor", (lambda c=c, prv=prv: (lambda t: t.matmul(
                    pb[2].ap[:, 384:512], GTB[:, c * C:(c + 1) * C],
                    HA[:, prv * C:(prv + 1) * C], start=True, stop=True)))(),
                    wait=wb + ga_tok + [tgv], inc="t")
                pb[2].note_w(tm3)
                tv3_ = em.on("vector", (lambda c=c, nxt=nxt: (lambda v: v.tensor_add(
                    HA[:, nxt * C:(nxt + 1) * C], HCB[:, c * C:(c + 1) * C],
                    pb[2].ap[:, 384:512])))(), wait=pb[2].r() + [thv], inc="v")
                pb[2].note_r(tv3_)
                ga_tok = [tv1_, tv2_, tv3_]
                gacc_cur = nxt

        def wmean_finish(wmslot, swslot, iswslot, nelem, wtok):
            t1 = em.on("vector", (lambda wmslot=wmslot: (lambda v: v.reduce_sum(
                col(WMT)[:, 0:1], col(wmslot), axis=AX.X)))(), wait=[wtok], inc="v")
            wb = btf.w()
            tm = em.on("tensor", lambda t: t.matmul(
                btf.ap[0:1, 0:1], ONES[:, 0:1], col(WMT)[:, 0:1], start=True, stop=True),
                wait=wb + [t1, tok_ones], inc="t")
            btf.note_w(tm)
            t2 = em.on("vector", (lambda nelem=nelem, wmslot=wmslot: (lambda v: v.tensor_scalar(
                col(wmslot)[0:1, 1:2], btf.ap[0:1, 0:1], 1.0 / nelem, EPS, OP.mult, OP.max)))(),
                wait=btf.r(), inc="v")
            btf.note_r(t2)
            wb = btf.w()
            tb = em.on("tensor", (lambda wmslot=wmslot: (lambda t: t.matmul(
                btf.ap[:, 1:2], ONES[0:1, :], col(wmslot)[0:1, 1:2], start=True, stop=True)))(),
                wait=wb + [t2], inc="t")
            btf.note_w(tb)
            t3 = em.on("vector", (lambda iswslot=iswslot: (lambda v: v.tensor_copy(
                col(iswslot)[:, 0:1], btf.ap[:, 1:2])))(), wait=btf.r(), inc="v")
            btf.note_r(t3)
            t4 = em.on("vector", (lambda swslot=swslot, iswslot=iswslot: (lambda v: v.reciprocal(
                col(swslot)[:, 0:1], col(iswslot)[:, 0:1])))(), wait=[t3], inc="v")
            return t3, t4
        tok_isw1, tok_sw1 = wmean_finish(WM1, SW1C, ISW1, D * HID, wm1_toks[-1])
        tok_isw2, tok_sw2 = wmean_finish(WM2, SW2C, ISW2, D * HID, wm2_toks[-1])

        tp1 = em.on("vector", (lambda gc=gacc_cur: (lambda v: v.tensor_copy(
            CCP[:, 0:128], GAT[:, gc * C:(gc + 1) * C])))(), wait=ga_tok, inc="v")
        tp2 = em.on("vector", (lambda gc=gacc_cur: (lambda v: v.tensor_copy(
            CCP[:, 128:256], HA[:, gc * C:(gc + 1) * C])))(), wait=ga_tok, inc="v")
        tcc1 = em.on("gpsimd", lambda g: g.dma_start(out=cc_in, in_=CCP[:, :]),
                     wait=[tp1, tp2], inc="dG", amt=16)
        tcc2 = em.on("gpsimd", lambda g: g.collective_compute(
            "AllGather", OP.bypass, replica_groups=[[0, 1, 2, 3], [4, 5, 6, 7]],
            ins=[cc_in], outs=[cc_out]), wait=[tcc1], inc="cc")
        tcc3 = em.on("gpsimd", lambda g: g.dma_start(
            out=RCV[:, :].rearrange("p (s w) -> p s w", s=3),
            in_=cc_out.rearrange("(s p) w -> p s w", p=128)[:, 0:3, :]),
            wait=[tcc2], inc="dG", amt=16)

        tz0 = em.on("vector", lambda v: v.memset(ZI[:, 0:C], 0.0), inc="v")
        zi_cur = 0
        tz_tok = [tz0, tcc3, tok_smk]
        rGPT = Reg(); rHPT = Reg()
        for p in range(3):
            tb1 = em.on("vector", (lambda p=p: (lambda v: v.tensor_scalar(
                GPT[:, :], RCV[:, p * 256:p * 256 + 128], SMK[:, p:p + 1], None,
                OP.mult)))(), wait=tz_tok + rGPT.w(), inc="v")
            rGPT.note_w(tb1)
            tb2 = em.on("vector", (lambda p=p: (lambda v: v.scalar_tensor_tensor(
                GPT[:, :], IDF[:, :], SMK[:, 3 + p:4 + p], GPT[:, :],
                OP.mult, OP.add)))(), wait=rGPT.w() + [CIDF], inc="v")
            rGPT.note_w(tb2)
            tb3 = em.on("vector", (lambda p=p: (lambda v: v.tensor_scalar(
                HPT[:, :], RCV[:, p * 256 + 128:p * 256 + 256], SMK[:, p:p + 1],
                None, OP.mult)))(), wait=tz_tok + rHPT.w(), inc="v")
            rHPT.note_w(tb3)
            nxt = 1 - zi_cur
            wb = btf.w()
            tzm = em.on("tensor", (lambda zc=zi_cur: (lambda t: t.matmul(
                btf.ap[:, 128:256], GPT[:, :], ZI[:, zc * C:(zc + 1) * C],
                start=True, stop=True)))(), wait=wb + rGPT.r() + rHPT.r() + tz_tok, inc="t")
            btf.note_w(tzm)
            rGPT.note_r(tzm)
            tza = em.on("vector", (lambda nxt=nxt: (lambda v: v.tensor_add(
                ZI[:, nxt * C:(nxt + 1) * C], HPT[:, :], btf.ap[:, 128:256])))(),
                wait=btf.r() + rHPT.r(), inc="v")
            btf.note_r(tza)
            rHPT.note_r(tza)
            tz_tok = [tza]
            zi_cur = nxt
        tzb = em.on("vector", (lambda zc=zi_cur: (lambda v: v.tensor_copy(
            ZBt[:, 0:C], ZI[:, zc * C:(zc + 1) * C])))(), wait=tz_tok, inc="v")
        wb = btf.w()
        tzt = em.on("tensor", (lambda zc=zi_cur: (lambda t: t.transpose(
            btf.ap[:, 256:384], ZI[:, zc * C:(zc + 1) * C], IDF[:, :])))(),
            wait=wb + tz_tok + [CIDF], inc="t")
        btf.note_w(tzt)
        tztv = em.on("vector", lambda v: v.tensor_copy(ZTt[:, 0:C], btf.ap[:, 256:384]),
                     wait=btf.r(), inc="v")
        btf.note_r(tztv)

        do_tap("zine", ZBt[:, 0:128], 128, [tzb, tztv])
        do_tap("gtbe", GTB[:, 0:2048], 2048, gtb_toks)
        do_tap("hcbe", HCB[:, 0:2048], 2048, hcb_toks)
        do_tap("gupe", GUP[:, 0:2048], 2048, tgu_toks)
        do_tap("qte", qT[:, 0:2048], 2048, [tok_qT])
        do_tap("kte", kT[:, 0:2048], 2048, [tok_kT])
        do_tap("lje", S1[:, 0:2048], 2048, [tok_ljb])
        do_tap("ucbe", UCB[:, 0:4096], 4096, ucb_toks[0:8])
        rZ = [Reg(), Reg()]
        rZT = [Reg(), Reg()]
        rZ[0].note_w(tzb)
        rZT[0].note_w(tztv)
        rUFT = [Reg(), Reg()]
        rMPT = [Reg(), Reg()]
        x2_toks = [None] * NT
        for c in range(NT):
            zc = c % 2
            zn = (c + 1) % 2
            uo = c % 2
            if c < NT - 1:
                wb = pb[2].w()
                tzm1 = em.on("tensor", (lambda c=c, zc=zc: (lambda t: t.matmul(
                    pb[2].ap[:, 0:128], GTB[:, c * C:(c + 1) * C], ZBt[:, zc * C:(zc + 1) * C],
                    start=True, stop=True)))(),
                    wait=wb + rZ[zc].r() + [gtb_toks[c]], inc="t")
                tzm2 = em.on("tensor", (lambda c=c, zc=zc: (lambda t: t.matmul(
                    pb[2].ap[:, 128:256], ZBt[:, zc * C:(zc + 1) * C], GTB[:, c * C:(c + 1) * C],
                    start=True, stop=True)))(), inc="t")
                pb[2].note_w(tzm1); pb[2].note_w(tzm2)
                rZ[zc].note_r(tzm1); rZ[zc].note_r(tzm2)
                rb = pb[2].r()
                tza1 = em.on("vector", (lambda c=c, zn=zn: (lambda v: v.tensor_add(
                    ZBt[:, zn * C:(zn + 1) * C], HCB[:, c * C:(c + 1) * C],
                    pb[2].ap[:, 0:128])))(), wait=rb + rZ[zn].w() + [hcb_toks[c]], inc="v")
                tza2 = em.on("vector", (lambda c=c, zn=zn: (lambda v: v.tensor_add(
                    ZTt[:, zn * C:(zn + 1) * C], HTB[:, c * C:(c + 1) * C],
                    pb[2].ap[:, 128:256])))(), wait=rZT[zn].w() + [htb_toks[c]], inc="v")
                pb[2].note_r(tza1); pb[2].note_r(tza2)
                rZ[zn] = Reg(); rZ[zn].note_w(tza1)
                rZT[zn] = Reg(); rZT[zn].note_w(tza2)
            wb = pb[3].w()
            tm1 = em.on("tensor", (lambda c=c, zc=zc: (lambda t: t.matmul(
                pb[3].ap[:, 0:128], ZBt[:, zc * C:(zc + 1) * C], PTBs[:, c * C:(c + 1) * C],
                start=True, stop=True)))(),
                wait=wb + rZ[zc].r() + [ptb_toks[c]], inc="t")
            pb[3].note_w(tm1)
            rZ[zc].note_r(tm1)
            tvu = em.on("vector", (lambda c=c, uo=uo: (lambda v: v.tensor_sub(
                UFT[:, uo * 128:(uo + 1) * 128], ULT[:, c * C:(c + 1) * C],
                pb[3].ap[:, 0:128])))(),
                wait=pb[3].r() + rUFT[uo].w() + [ult_toks[c]], inc="v")
            pb[3].note_r(tvu)
            rUFT[uo] = Reg(); rUFT[uo].note_w(tvu)
            wb = pb[3].w()
            tm2 = em.on("tensor", (lambda c=c, uo=uo: (lambda t: t.matmul(
                pb[3].ap[:, 128:256], UFT[:, uo * 128:(uo + 1) * 128],
                qT[:, c * C:(c + 1) * C], start=True, stop=True)))(),
                wait=wb + rUFT[uo].r() + [tok_qT], inc="t")
            pb[3].note_w(tm2)
            rUFT[uo].note_r(tm2)
            tvm = em.on("vector", (lambda c=c, uo=uo: (lambda v: v.tensor_mul(
                MPT[:, uo * 128:(uo + 1) * 128], GUP[:, c * C:(c + 1) * C],
                pb[3].ap[:, 128:256])))(),
                wait=pb[3].r() + rMPT[uo].w() + [tgu_toks[c]], inc="v")
            pb[3].note_r(tvm)
            rMPT[uo] = Reg(); rMPT[uo].note_w(tvm)
            wb = pb[3].w()
            tm3 = em.on("tensor", (lambda c=c, zc=zc: (lambda t: t.matmul(
                pb[3].ap[:, 256:384], ZTt[:, zc * C:(zc + 1) * C], QTT[:, c * C:(c + 1) * C],
                start=True, stop=False)))(),
                wait=wb + rZT[zc].r() + [tqt_toks[c]], inc="t")
            tm4 = em.on("tensor", (lambda c=c, uo=uo: (lambda t: t.matmul(
                pb[3].ap[:, 256:384], KNB[:, c * C:(c + 1) * C], MPT[:, uo * 128:(uo + 1) * 128],
                start=False, stop=True)))(), wait=rMPT[uo].r() + [knb_toks[c]], inc="t")
            pb[3].note_w(tm3); pb[3].note_w(tm4)
            rZT[zc].note_r(tm3)
            rMPT[uo].note_r(tm4)
            tvo = em.on("vector", (lambda c=c, uo=uo: (lambda v: v.tensor_mul(
                UFT[:, uo * 128:(uo + 1) * 128], gT[:, c * C:(c + 1) * C],
                pb[3].ap[:, 256:384])))(),
                wait=pb[3].r() + rUFT[uo].w() + [tok_gT], inc="v")
            pb[3].note_r(tvo)
            rUFT[uo] = Reg(); rUFT[uo].note_w(tvo)
            yo = c % 2
            wy = rYTL.w() if c == 0 else []
            oy = []
            for nn in range(2):
                wb = pb[4].w()
                tm5 = em.on("tensor", (lambda nn=nn, uo=uo: (lambda t: t.matmul(
                    pb[4].ap[:, 0:512], UFT[:, uo * 128:(uo + 1) * 128],
                    WO[:, nn * 512:(nn + 1) * 512], start=True, stop=True)))(),
                    wait=wb + rUFT[uo].r() + [tok_wo], inc="t")
                pb[4].note_w(tm5)
                rUFT[uo].note_r(tm5)
                tv5 = em.on("vector", (lambda c=c, nn=nn, yo=yo: (lambda v: v.scalar_tensor_tensor(
                    YTL[:, yo * D + nn * 512: yo * D + (nn + 1) * 512],
                    pb[4].ap[:, 0:512], 1.0, BOB[:, nn * 512:(nn + 1) * 512],
                    OP.mult, OP.add,
                    accum_out=col(TMPA)[:, nn:nn + 1])))(),
                    wait=pb[4].r() + wy + [CBOB] +
                         ([x2_toks[c - 2]] if c >= 2 else []), inc="v")
                pb[4].note_r(tv5)
                oy.append(tv5)
            tsum = em.on("vector", (lambda c=c: (lambda v: v.tensor_add(
                col(SUM2)[:, c:c + 1], col(TMPA)[:, 0:1], col(TMPA)[:, 1:2])))(),
                wait=oy, inc="v")
            tsq = em.on("scalar", (lambda c=c, yo=yo: (lambda s: s.activation(
                S1[:, 0:1024], YTL[:, yo * D:(yo + 1) * D], AF.Square,
                accum_out=col(SQ2)[:, c:c + 1])))(), wait=oy + rS1.w(), inc="s")
            rS1.note_w(tsq)
            t1 = em.on("vector", (lambda c=c: (lambda v: v.tensor_scalar(
                col(MU2)[:, c:c + 1], col(SUM2)[:, c:c + 1], 1.0 / D, None, OP.mult)))(),
                wait=[tsum], inc="v")
            t2 = em.on("vector", (lambda c=c: (lambda v: v.tensor_scalar(
                col(TMPB)[:, 0:1], col(SQ2)[:, c:c + 1], 1.0 / D, None, OP.mult)))(),
                wait=[tsq], inc="v")
            t3 = em.on("vector", (lambda c=c: (lambda v: v.tensor_mul(
                col(TMPB)[:, 1:2], col(MU2)[:, c:c + 1], col(MU2)[:, c:c + 1])))(),
                wait=[t1], inc="v")
            t4 = em.on("vector", lambda v: v.tensor_sub(
                col(TMPB)[:, 0:1], col(TMPB)[:, 0:1], col(TMPB)[:, 1:2]),
                wait=[t2, t3], inc="v")
            t5 = em.on("scalar", lambda s: s.activation(
                col(TMPB)[:, 1:2], col(TMPB)[:, 0:1], AF.Sqrt, bias=EPSC),
                wait=[t4, tok_epsc], inc="s")
            t6 = em.on("vector", (lambda c=c: (lambda v: v.reciprocal(
                col(R2C)[:, c:c + 1], col(TMPB)[:, 1:2])))(), wait=[t5], inc="v")
            t7 = em.on("vector", (lambda c=c: (lambda v: v.scalar_tensor_tensor(
                col(NM2)[:, c:c + 1], col(MU2)[:, c:c + 1], -1.0, col(R2C)[:, c:c + 1],
                OP.mult, OP.mult)))(), wait=[t6], inc="v")
            t8 = em.on("vector", (lambda c=c, yo=yo: (lambda v: v.tensor_scalar(
                YTL[:, yo * D:(yo + 1) * D], YTL[:, yo * D:(yo + 1) * D],
                col(R2C)[:, c:c + 1], col(NM2)[:, c:c + 1], OP.mult, OP.add)))(),
                wait=[t7, tsq], inc="v")
            t9 = em.on("vector", (lambda c=c, yo=yo: (lambda v: v.scalar_tensor_tensor(
                R1[:, c * D:(c + 1) * D], YTL[:, yo * D:(yo + 1) * D], 1.0,
                R1[:, c * D:(c + 1) * D], OP.mult, OP.add,
                accum_out=col(SUM3)[:, c:c + 1])))(), wait=[t8, tok_x], inc="v")
            rYTL.note_w(t9)
            rYTL.note_r(t9)
            x2_toks[c] = t9

        rS1b = Reg()
        sq3_toks = []
        for c in range(NT):
            tq = em.on("scalar", (lambda c=c: (lambda s: s.activation(
                S1[:, 1024:2048], R1[:, c * D:(c + 1) * D], AF.Square,
                accum_out=col(SQ3)[:, c:c + 1])))(), wait=[x2_toks[c]] + rS1b.w(), inc="s")
            rS1b.note_w(tq)
            sq3_toks.append(tq)
        t1 = em.on("vector", lambda v: v.tensor_scalar(col(MU3), col(SUM3), 1.0 / D, None, OP.mult),
                   wait=x2_toks, inc="v")
        t2 = em.on("vector", lambda v: v.tensor_scalar(col(TMPA), col(SQ3), 1.0 / D, None, OP.mult),
                   wait=sq3_toks, inc="v")
        t3 = em.on("vector", lambda v: v.tensor_mul(col(TMPB), col(MU3), col(MU3)), wait=[t1], inc="v")
        t4 = em.on("vector", lambda v: v.tensor_sub(col(TMPA), col(TMPA), col(TMPB)),
                   wait=[t2, t3], inc="v")
        t5 = em.on("vector", lambda v: v.tensor_scalar(col(TMPB), col(TMPA), EPS, None, OP.add),
                   wait=[t4], inc="v")
        t6a = em.on("vector", lambda v: v.reciprocal(col(TMPC), col(TMPB)), wait=[t5], inc="v")
        t6 = em.on("vector", lambda v: v.tensor_mul(col(TMPC), col(TMPA), col(TMPC)),
                   wait=[t6a], inc="v")
        t8 = em.on("vector", lambda v: v.tensor_scalar(col(TMPC), col(TMPC), EPS, None, OP.add),
                   wait=[t6], inc="v")
        t9 = em.on("vector", lambda v: v.tensor_mul(col(TMPA), col(TMPB), col(TMPC)),
                   wait=[t8], inc="v")
        t10 = em.on("scalar", lambda s: s.activation(col(TMPB), col(TMPA), AF.Sqrt),
                    wait=[t9], inc="s")
        t11 = em.on("vector", lambda v: v.reciprocal(col(RHO), col(TMPB)), wait=[t10], inc="v")
        t12 = em.on("vector", lambda v: v.scalar_tensor_tensor(
            col(NMR), col(MU3), -1.0, col(RHO), OP.mult, OP.mult), wait=[t11], inc="v")
        rho_tok = t12

        tok_b2b = em.on("gpsimd", lambda g: g.dma_start(out=B2B[:, :], in_=b2_b_d),
                        wait=x2_toks, inc="dG", amt=16)
        rZI32 = Reg()
        rXQN = [Reg(), Reg()]
        rS1h = [rS1, rS1b]
        xq_last = None
        for c in range(NT):
            hh = 0
            ho = 0
            rh = rS1h[hh]
            ta = em.on("vector", (lambda c=c, ho=ho: (lambda v: v.tensor_scalar(
                S1[:, ho:ho + D], R1[:, c * D:(c + 1) * D], col(RHO)[:, c:c + 1],
                col(NMR)[:, c:c + 1], OP.mult, OP.add)))(),
                wait=rh.w() + [rho_tok, x2_toks[c]], inc="v")
            rh.note_w(ta)
            tb = em.on("vector", (lambda ho=ho: (lambda v: v.scalar_tensor_tensor(
                S1[:, ho:ho + D], S1[:, ho:ho + D], 1.0, GMB[:, :], OP.mult, OP.mult)))(),
                wait=rh.w() + [CGMB], inc="v")
            rh.note_w(tb)
            tc = em.on("vector", (lambda ho=ho: (lambda v: v.tensor_add(
                S1[:, ho:ho + D], S1[:, ho:ho + D], BTB[:, :])))(),
                wait=rh.w() + [CBTB], inc="v")
            rh.note_w(tc)
            tdm = em.on("vector", (lambda c=c, ho=ho, hh=hh: (lambda v: v.reduce_max(
                col(TMPA)[:, hh:hh + 1], S1[:, ho:ho + D], axis=AX.X,
                apply_absolute_value=True)))(), wait=rh.r(), inc="v")
            rh.note_r(tdm)
            tde = em.on("vector", (lambda hh=hh: (lambda v: v.tensor_scalar(
                col(TMPA)[:, 2 + hh:3 + hh], col(TMPA)[:, hh:hh + 1], EPS, None,
                OP.max)))(), wait=[tdm], inc="v")
            tdf = em.on("vector", (lambda hh=hh: (lambda v: v.reciprocal(
                col(TMPB)[:, hh:hh + 1], col(TMPA)[:, 2 + hh:3 + hh])))(),
                wait=[tde], inc="v")
            tdg = em.on("vector", (lambda c=c, hh=hh: (lambda v: v.tensor_scalar(
                col(SXC)[:, c:c + 1], col(TMPB)[:, hh:hh + 1], 127.0, None, OP.mult)))(),
                wait=[tdf], inc="v")
            tdh = em.on("vector", (lambda c=c, hh=hh: (lambda v: v.scalar_tensor_tensor(
                col(ISX)[:, c:c + 1], col(TMPA)[:, 2 + hh:3 + hh], 1.0 / 127.0,
                col(ISW1)[:, 0:1], OP.mult, OP.mult)))(), wait=[tde, tok_isw1], inc="v")
            xo = c % 2
            wxq = rXQN[xo].w()
            tj = None
            for hf in range(2):
                ti = em.on("vector", (lambda c=c, hf=hf, ho=ho: (lambda v: v.tensor_scalar(
                    ZI32[:, :], S1[:, ho + hf * 512:ho + (hf + 1) * 512],
                    col(SXC)[:, c:c + 1], None, OP.mult)))(),
                    wait=rZI32.w() + rh.r() + [tdg], inc="v")
                rZI32.note_w(ti)
                rh.note_r(ti)
                tj = em.on("vector", (lambda xo=xo, hf=hf: (lambda v: v.tensor_copy(
                    XQN[:, xo * D + hf * 512: xo * D + (hf + 1) * 512], ZI32[:, :])))(),
                    wait=(wxq if hf == 0 else []) + rZI32.r(), inc="v")
                rZI32.note_r(tj)
            rXQN[xo] = Reg(); rXQN[xo].note_w(tj)
            for half in range(2):
                wb = btb.w()
                tts = []
                for kk in range(4):
                    k = half * 4 + kk
                    tt = em.on("tensor", (lambda k=k, kk=kk, xo=xo: (lambda t: t.transpose(
                        btb.ap[:, kk * 128:(kk + 1) * 128],
                        XQN[:, xo * D + k * 128: xo * D + (k + 1) * 128], IDB[:, :])))(),
                        wait=(wb if kk == 0 else []) + rXQN[xo].r() + [tok_idb] +
                             (tts[-1:] if tts else []), inc="t")
                    tts.append(tt)
                    btb.note_w(tt)
                    rXQN[xo].note_r(tt)
                rb = btb.r()
                evs = []
                for kk in range(4):
                    k = half * 4 + kk
                    te = em.on("vector", (lambda c=c, k=k, kk=kk: (lambda v: v.tensor_copy(
                        R2[:, k * TOK + c * 128: k * TOK + (c + 1) * 128],
                        btb.ap[:, kk * 128:(kk + 1) * 128])))(),
                        wait=rb + ([tok_xT] if c == 0 else []), inc="v")
                    btb.note_r(te)
                    evs.append(te)
                xq_last = evs[-1]
        tok_xq = xq_last

        do_tap("x2e", R1[:, 0:4096], 4096, x2_toks)
        do_tap("xqe", R2[:, 0:4096], 4096, [tok_xq])
        do_tap("coe", CO[:, 0:768], 768, x2_toks + [rho_tok])
        tok_sxw1b = bcast_from_col(col(ISX), rF1, F1, rF1.w() + [tdh, tok_xq, tzb, tztv])

        tmx0 = em.on("vector", lambda v: v.memset(F2[:, 0:TOK], 0.0),
                     wait=rF2.w() + rWMb[0].w() + rWMb[1].w(), inc="v")
        rF2.note_w(tmx0)
        rRWM = Reg()
        tmn0 = em.on("vector", lambda v: v.memset(RW[:, 0:TOK], 0.0),
                     wait=rROW0.w() + rL16[0].w() + rL16[1].w() + rB16.w() + rRW16.w() + [tok_sxw1b],
                     inc="v")
        rRWM.note_w(tmn0)
        w1_slice_r = W1T_d.rearrange("(k p) (h j) -> p k h j", p=128, j=128)
        hb_r = Hb.rearrange("(h p) t -> h p t", p=128)
        rW1S = [Reg(), Reg()]
        rW1Q = [Reg(), Reg()]
        rHSL = [Reg(), Reg()]
        tmx_last = tmx0
        w2_slice2 = W2T_d.rearrange("(h p) dd -> h p dd", p=128)
        rW2Sr = Reg()
        w2q_toks = [None] * NH
        def w1_load(h):
            pp = h % 2
            td = em.on("sync", (lambda h=h, pp=pp: (lambda sy: sy.dma_start(
                out=W1S[:, pp * D:(pp + 1) * D].rearrange("p (k j) -> p k j", k=ND),
                in_=w1_slice_r[:, :, h, :])))(),
                wait=rW1S[pp].w() + ([wm1_toks[-1], x2_toks[-1], tok_sxw1b]
                                     if h <= 1 else []),
                inc="dW0" if pp == 0 else "dW1", amt=16)
            rW1S[pp] = Reg(); rW1S[pp].note_w(td)

        def w1_quant(h):
            pp = h % 2
            tq1 = em.on("vector", (lambda pp=pp: (lambda v: v.tensor_scalar(
                W1Q[:, pp * D:(pp + 1) * D], W1S[:, pp * D:(pp + 1) * D],
                col(SW1C)[:, 0:1], 0.5, OP.mult, OP.is_gt)))(),
                wait=rW1Q[pp].w() + rW1S[pp].r() + [tok_sw1], inc="v")
            rW1S[pp].note_r(tq1)
            tq2 = em.on("vector", (lambda pp=pp: (lambda g: g.tensor_scalar(
                W1S[:, pp * D:(pp + 1) * D], W1S[:, pp * D:(pp + 1) * D],
                col(SW1C)[:, 0:1], -0.5, OP.mult, OP.is_lt)))(),
                wait=rW1S[pp].w() + [tq1, tok_sw1], inc="v")
            rW1S[pp].note_w(tq2)
            tq3 = em.on("vector", (lambda pp=pp: (lambda v: v.tensor_sub(
                W1Q[:, pp * D:(pp + 1) * D], W1Q[:, pp * D:(pp + 1) * D],
                W1S[:, pp * D:(pp + 1) * D])))(), wait=[tq1, tq2], inc="v")
            rW1S[pp].note_r(tq3)
            rW1Q[pp] = Reg(); rW1Q[pp].note_w(tq3)

        w1_load(0)
        w1_load(1)
        w1_quant(0)
        for h in range(NH):
            pp = h % 2
            hp = h % 2
            whs = rHSL[hp].w()
            ev = []
            for pair in range(2):
                banks2 = [pb[2 * pair], pb[2 * pair + 1]]
                wbs = [b.w() for b in banks2]
                mms = [[], []]
                for k in range(ND):
                    for j in range(2):
                        nchk = 2 * pair + j
                        tm = em.on("tensor", (lambda k=k, nchk=nchk, bank=banks2[j], pp=pp: (lambda t: t.matmul(
                            bank.ap[:, 0:512], W1Q[:, pp * D + k * 128: pp * D + (k + 1) * 128],
                            R2[:, k * TOK + nchk * 512: k * TOK + (nchk + 1) * 512],
                            start=(k == 0), stop=(k == ND - 1))))(),
                            wait=(wbs[j] if k == 0 else []) + rW1Q[pp].r() + [tok_xq],
                            inc="t")
                        mms[j].append(tm)
                for j in range(2):
                    banks2[j].note_w(mms[j][-1])
                    rW1Q[pp].note_r(mms[j][-1])
            if h + 2 < NH:
                w1_load(h + 2)
            if h + 1 < NH:
                w1_quant(h + 1)
            for nchk in range(4):
                bank = pb[nchk]
                tv = em.on("vector", (lambda nchk=nchk, bank=bank: (lambda v: v.tensor_mul(
                    S1[:, nchk * 512:(nchk + 1) * 512], bank.ap[:, 0:512],
                    F1[:, nchk * 512:(nchk + 1) * 512])))(),
                    wait=bank.r() + (rS1.w() + rS1b.w() if nchk == 0 else []) +
                         [tok_sxw1b], inc="v")
                bank.note_r(tv)
                rF1.note_r(tv)
                rS1.note_w(tv)
                ts_ = em.on("scalar", (lambda h=h, nchk=nchk, hp=hp: (lambda s: s.activation(
                    HSL[:, hp * TOK + nchk * 512: hp * TOK + (nchk + 1) * 512],
                    S1[:, nchk * 512:(nchk + 1) * 512], AF.Gelu,
                    bias=B1T[:, h:h + 1])))(),
                    wait=rS1.r() + (whs if nchk == 0 else []) + [CB1T], inc="s")
                rS1.note_r(ts_)
                ev.append(ts_)
            rHSL[hp] = Reg(); rHSL[hp].note_w(ev[-1])
            for e in ev:
                rHSL[hp].note_w(e)
            tmx = em.on("vector", (lambda hp=hp: (lambda g: g.tensor_tensor(
                F2[:, 0:TOK], F2[:, 0:TOK], HSL[:, hp * TOK:(hp + 1) * TOK],
                OP.max)))(), wait=ev + rF2.w(), inc="v")
            rF2.note_w(tmx)
            rHSL[hp].note_r(tmx)
            tmn = em.on("vector", (lambda hp=hp: (lambda g: g.scalar_tensor_tensor(
                RW[:, 0:TOK], HSL[:, hp * TOK:(hp + 1) * TOK], -1.0, RW[:, 0:TOK],
                OP.mult, OP.max)))(), wait=ev + rRWM.w(), inc="v")
            rRWM.note_w(tmn)
            rHSL[hp].note_r(tmn)
            tmx_last = tmx
            tmn_last = tmn
            tdh_ = em.on("scalar", (lambda h=h, hp=hp: (lambda s: s.dma_start(
                out=hb_r[h], in_=HSL[:, hp * TOK:(hp + 1) * TOK])))(),
                wait=ev, inc="dA", amt=16)
            rHSL[hp].note_r(tdh_)
            td2 = em.on("sync", (lambda h=h: (lambda sy: sy.dma_start(
                out=W2S[:, 0:D], in_=w2_slice2[h])))(),
                wait=rW2Sr.w() + ([wm2_toks[-1], x2_toks[-1], tok_xq]
                                  if h == 0 else []), inc="dX", amt=16)
            rW2Sr.note_w(td2)
            wq1 = em.on("vector", (lambda h=h: (lambda g: g.tensor_scalar(
                W2Q[:, h * D:(h + 1) * D], W2S[:, 0:D],
                col(SW2C)[:, 0:1], 0.5, OP.mult, OP.is_gt)))(),
                wait=rW2Sr.r() + [tok_sw2] + ([x2_toks[-1], tok_xq] if h == 0 else []),
                inc="v")
            rW2Sr.note_r(wq1)
            wq2 = em.on("vector", (lambda h=h: (lambda g: g.tensor_scalar(
                W2S[:, 0:D], W2S[:, 0:D],
                col(SW2C)[:, 0:1], -0.5, OP.mult, OP.is_lt)))(),
                wait=rW2Sr.w(), inc="v")
            rW2Sr.note_w(wq2)
            wq3 = em.on("vector", (lambda h=h: (lambda g: g.tensor_sub(
                W2Q[:, h * D:(h + 1) * D], W2Q[:, h * D:(h + 1) * D],
                W2S[:, 0:D])))(), wait=[wq1, wq2], inc="v")
            rW2Sr.note_r(wq3)
            w2q_toks[h] = wq3

        hb_done = ("dA", em.cnt["dA"])
        pass1_t = ("t", em.cnt["t"])

        s2r = []
        for c in range(NT):
            wb = btf.w()
            tt = em.on("tensor", (lambda c=c: (lambda t: t.transpose(
                btf.ap[:, 0:128], F2[:, c * 128:(c + 1) * 128], IDF[:, :])))(),
                wait=wb + [tmx_last, CIDF], inc="t")
            tt2 = em.on("tensor", (lambda c=c: (lambda t: t.transpose(
                btf.ap[:, 128:256], RW[:, c * 128:(c + 1) * 128], IDF[:, :])))(),
                wait=[tmn_last], inc="t")
            btf.note_w(tt); btf.note_w(tt2)
            rF2.note_r(tt)
            rRWM.note_r(tt2)
            rb = btf.r()
            tr = em.on("vector", (lambda c=c: (lambda v: v.reduce_max(
                col(TMPA)[:, c:c + 1], btf.ap[:, 0:128], axis=AX.X)))(),
                wait=rb, inc="v")
            tr2 = em.on("vector", (lambda c=c: (lambda v: v.reduce_max(
                col(TMPB)[:, c:c + 1], btf.ap[:, 128:256], axis=AX.X)))(),
                wait=rb, inc="v")
            btf.note_r(tr); btf.note_r(tr2)
            s2r += [tr, tr2]
        tmrg = em.on("vector", lambda v: v.tensor_max(col(TMPA), col(TMPA), col(TMPB)),
                     wait=s2r, inc="v")
        t1 = em.on("vector", lambda v: v.tensor_scalar(col(TMPB), col(TMPA), EPS, None, OP.max),
                   wait=[tmrg], inc="v")
        t2 = em.on("vector", lambda v: v.reciprocal(col(TMPC), col(TMPB)), wait=[t1], inc="v")
        tok_s2 = em.on("vector", lambda v: v.tensor_scalar(col(S2C), col(TMPC), 127.0, None, OP.mult),
                       wait=[t2], inc="v")
        tok_is2 = em.on("vector", lambda v: v.tensor_scalar(
            col(IS2), col(TMPB), col(ISW2)[:, 0:1], 1.0 / 127.0, OP.mult, OP.mult),
            wait=[t1, tok_isw2], inc="v")
        tok_s2b = bcast_from_col(col(S2C), rF2, F2, rF2.w() + rRWM.w() + [tok_s2])

        hb_rd = Hb.rearrange("(h p) (g t) -> p h g t", p=128, t=512)
        out_r = out_d.rearrange("(c p) d -> c p d", p=128)
        rR2 = Reg()
        rR2.note_w(tok_xq)
        rSQS = Reg()
        rZH = Reg()
        rOUT2 = [Reg() for _ in range(4)]
        rOSTG = Reg()
        out_toks = []
        prev_last_rd = None
        for g_ in range(4):
            sub_toks = []
            for sj in range(4):
                if prev_last_rd is None:
                    wsub = rR2.w() + [hb_done, pass1_t]
                else:
                    wsub = [prev_last_rd[sj * 8 + 7]]
                tds = em.on("sync", (lambda g_=g_, sj=sj: (lambda sy: sy.dma_start(
                    out=R2[:, sj * 8 * 512:(sj + 1) * 8 * 512].rearrange(
                        "p (h t) -> p h t", h=8),
                    in_=hb_rd[:, sj * 8:(sj + 1) * 8, g_, :])))(),
                    wait=wsub, inc="dR%d" % sj, amt=16)
                sub_toks.append(tds)
                rR2.note_w(tds)
            q_toks = []
            for h in range(NH):
                tq1 = em.on("vector", (lambda h=h, g_=g_: (lambda v: v.tensor_tensor(
                    ZH32[:, :], R2[:, h * 512:(h + 1) * 512],
                    F2[:, g_ * 512:(g_ + 1) * 512], OP.mult)))(),
                    wait=rZH.w() + [sub_toks[h // 8], tok_s2b], inc="v")
                rR2.note_r(tq1)
                rF2.note_r(tq1)
                rZH.note_w(tq1)
                tq2 = em.on("vector", (lambda h=h: (lambda v: v.tensor_copy(
                    R2[:, h * 512:(h + 1) * 512], ZH32[:, :])))(),
                    wait=rZH.r() + [rR2.lw], inc="v")
                rZH.note_r(tq2)
                rR2.note_w(tq2)
                q_toks.append(tq2)
            hq_tok = q_toks[-1]
            for tl in range(4):
                c = g_ * 4 + tl
                ev2 = []
                banks2 = [pb[(tl * 2) % 4], pb[(tl * 2 + 1) % 4]]
                wbs = [b.w() for b in banks2]
                mms = [[], []]
                for h in range(NH):
                    for dch in range(2):
                        tm = em.on("tensor", (lambda h=h, tl=tl, dch=dch, bank=banks2[dch]: (lambda t: t.matmul(
                            bank.ap[:, 0:512], R2[:, h * 512 + tl * 128: h * 512 + (tl + 1) * 128],
                            W2Q[:, h * D + dch * 512: h * D + (dch + 1) * 512],
                            start=(h == 0), stop=(h == NH - 1))))(),
                            wait=(wbs[dch] if h == 0 else []) + [q_toks[h], w2q_toks[h]],
                            inc="t")
                        mms[dch].append(tm)
                for dch in range(2):
                    banks2[dch].note_w(mms[dch][-1])
                    rR2.note_r(mms[dch][-1])
                if tl == 3:
                    prev_last_rd = mms[1]
                for dch in range(2):
                    bank = banks2[dch]
                    tv = em.on("vector", (lambda c=c, tl=tl, dch=dch, bank=bank: (lambda v: v.scalar_tensor_tensor(
                        OUT2[:, tl * D + dch * 512: tl * D + (dch + 1) * 512],
                        bank.ap[:, 0:512], col(IS2)[:, c:c + 1],
                        B2B[:, dch * 512:(dch + 1) * 512], OP.mult, OP.add,
                        accum_out=col(TMPA)[:, dch:dch + 1])))(),
                        wait=bank.r() + (rOUT2[tl].w() if dch == 0 else []) +
                             [tok_is2, tok_b2b], inc="v")
                    bank.note_r(tv)
                    ev2.append(tv)
                rOUT2[tl] = Reg(); rOUT2[tl].note_w(ev2[-1])
                for e in ev2:
                    rOUT2[tl].note_w(e)
                tsum = em.on("vector", (lambda c=c: (lambda v: v.tensor_add(
                    col(SUM5)[:, c:c + 1], col(TMPA)[:, 0:1], col(TMPA)[:, 1:2])))(),
                    wait=ev2, inc="v")
                tsq = em.on("scalar", (lambda c=c, tl=tl: (lambda s: s.activation(
                    SQS[:, :], OUT2[:, tl * D:(tl + 1) * D], AF.Square,
                    accum_out=col(SQ5)[:, c:c + 1])))(), wait=ev2 + rSQS.w(), inc="s")
                rSQS.note_w(tsq)
                rOUT2[tl].note_r(tsq)
                t1 = em.on("vector", (lambda c=c: (lambda v: v.tensor_scalar(
                    col(MU5)[:, c:c + 1], col(SUM5)[:, c:c + 1], 1.0 / D, None, OP.mult)))(),
                    wait=[tsum], inc="v")
                t2 = em.on("vector", (lambda c=c: (lambda v: v.tensor_scalar(
                    col(TMPB)[:, 0:1], col(SQ5)[:, c:c + 1], 1.0 / D, None, OP.mult)))(),
                    wait=[tsq], inc="v")
                t3 = em.on("vector", (lambda c=c: (lambda v: v.tensor_mul(
                    col(TMPB)[:, 1:2], col(MU5)[:, c:c + 1], col(MU5)[:, c:c + 1])))(),
                    wait=[t1], inc="v")
                t4 = em.on("vector", lambda v: v.tensor_sub(
                    col(TMPB)[:, 0:1], col(TMPB)[:, 0:1], col(TMPB)[:, 1:2]),
                    wait=[t2, t3], inc="v")
                t5 = em.on("scalar", lambda s: s.activation(
                    col(TMPB)[:, 1:2], col(TMPB)[:, 0:1], AF.Sqrt, bias=EPSC),
                    wait=[t4], inc="s")
                t6 = em.on("vector", (lambda c=c: (lambda v: v.reciprocal(
                    col(R5C)[:, c:c + 1], col(TMPB)[:, 1:2])))(), wait=[t5], inc="v")
                t7 = em.on("vector", (lambda c=c: (lambda v: v.scalar_tensor_tensor(
                    col(NM5)[:, c:c + 1], col(MU5)[:, c:c + 1], -1.0, col(R5C)[:, c:c + 1],
                    OP.mult, OP.mult)))(), wait=[t6], inc="v")
                t8 = em.on("vector", (lambda c=c, tl=tl: (lambda v: v.tensor_scalar(
                    OUT2[:, tl * D:(tl + 1) * D], OUT2[:, tl * D:(tl + 1) * D],
                    col(R5C)[:, c:c + 1], col(NM5)[:, c:c + 1], OP.mult, OP.add)))(),
                    wait=[t7, tsq], inc="v")
                rOUT2[tl].note_w(t8)
                t9 = em.on("vector", (lambda c=c, tl=tl: (lambda v: v.scalar_tensor_tensor(
                    OSTG[:, 0:D], OUT2[:, tl * D:(tl + 1) * D], 1.0,
                    R1[:, c * D:(c + 1) * D], OP.mult, OP.add)))(),
                    wait=rOSTG.w() + [t8, w2q_toks[-1]], inc="v")
                rOUT2[tl].note_r(t9)
                rOSTG.note_w(t9)
                to = em.on("sync", (lambda c=c: (lambda sy: sy.dma_start(
                    out=out_r[c], in_=OSTG[:, 0:D])))(),
                    wait=rOSTG.r(), inc="out", amt=16)
                rOSTG.note_r(to)
                out_toks.append(to)

        tap_points = {
            "x2": (R1[:, 0:4096], 4096),
            "qT": (qT[:, 0:2048], 2048),
            "kT": (kT[:, 0:2048], 2048),
            "vT": (vT[:, 0:2048], 2048),
            "gT": (gT[:, 0:2048], 2048),
            "lj": (S1[:, 0:2048], 2048),
            "gtb": (GTB[:, 0:2048], 2048),
            "hcb": (HCB[:, 0:2048], 2048),
            "ucb": (UCB[:, 0:4096], 4096),
            "zin": (ZBt[:, 0:128], 128),
            "xq": (R2[:, 0:4096], 4096),
            "w2q": (W2Q[:, 0:4096], 4096),
            "co": (CO[:, 0:768], 768),
            "ytl": (YTL[:, 0:2048], 2048),
            "f2": (F2[:, 0:2048], 2048),
            "gup": (GUP[:, 0:2048], 2048),
            "qtt": (QTT[:, 0:2048], 2048),
        }
        if tap is not None and tap in tap_points:
            region, width = tap_points[tap]
        else:
            region, width = CO[:, 0:768], 768
        em.on("gpsimd", (lambda region=region, width=width: (lambda g: g.dma_start(
            out=dbg_d[:, 0:width], in_=region)))(),
            wait=em.barrier(), inc="dG", amt=16)
        em.on("sync", None, wait=[("dG", em.cnt["dG"])], inc=None)
        em.on("sync", None, wait=[("out", em.cnt["out"])], inc=None)

        block = es.enter_context(nc.Block())

        @block.sync
        def _(sy):
            em.run("sync", sy)

        @block.gpsimd
        def _(g):
            em.run("gpsimd", g)

        @block.vector
        def _(v):
            em.run("vector", v)

        @block.scalar
        def _(s):
            em.run("scalar", s)

        @block.tensor
        def _(t):
            em.run("tensor", t)

    return nc


_NC = None
TAP = None
DBG_OUT = {}


def kernel(**inputs):
    global _NC
    f32 = lambda a: np.ascontiguousarray(np.asarray(a, dtype=np.float32))
    x = f32(inputs["L0_attn_x"])
    Wq = f32(inputs["L0_attn_Wq"]); bq = f32(inputs["L0_attn_bq"])
    Wk = f32(inputs["L0_attn_Wk"]); bk = f32(inputs["L0_attn_bk"])
    Wv = f32(inputs["L0_attn_Wv"]); bv = f32(inputs["L0_attn_bv"])
    Wa = f32(inputs["L0_attn_Wa"]); ba = f32(inputs["L0_attn_ba"])
    Wb = f32(inputs["L0_attn_Wb"]); bb = f32(inputs["L0_attn_bb"])
    Wg = f32(inputs["L0_attn_Wg"]); bg = f32(inputs["L0_attn_bg"])
    Wo = f32(inputs["L0_attn_Wo"]); bo = f32(inputs["L0_attn_bo"])
    w1 = f32(inputs["L0_ffn_w1"]); b1 = f32(inputs["L0_ffn_b1"])
    w2 = f32(inputs["L0_ffn_w2"]); b2 = f32(inputs["L0_ffn_b2"])
    gamma = f32(inputs["L0_ffn_gamma"]); beta = f32(inputs["L0_ffn_beta"])

    Wab = np.ascontiguousarray(np.concatenate([Wa, Wb], axis=1))
    bab = np.ascontiguousarray(np.stack([ba[0], bb[0]])[:, None])
    W1T = np.ascontiguousarray(w1.T)
    W2T = np.ascontiguousarray(w2.T)
    bcolf = lambda a: np.ascontiguousarray(a[:, None])
    brd = lambda a: np.ascontiguousarray(np.broadcast_to(a[None, :], (128, a.shape[0])))
    b1T = np.ascontiguousarray(b1.reshape(NH, 128).T)
    idf = np.eye(128, dtype=np.float32)
    idb = np.eye(128, dtype=ml_dtypes.bfloat16)
    ii = np.arange(128)
    maskW = np.where(ii[:, None] > ii[None, :], 0.0, POS).astype(np.float32)
    maskU = np.where(ii[None, :] >= ii[:, None], 0.0, NEG).astype(np.float32)

    global _NC
    if _NC is None:
        _NC = build_nc(TAP)
    nc = _NC

    in_maps = []
    for core in range(8):
        b, s = core // 4, core % 4
        xs = np.ascontiguousarray(x[b, s * TOK:(s + 1) * TOK, :])
        xTs = np.ascontiguousarray(xs.T)
        sm = np.zeros((128, 6), np.float32)
        for p in range(3):
            m = 1.0 if p < s else 0.0
            sm[:, p] = m
            sm[:, 3 + p] = 1.0 - m
        in_maps.append(dict(
            x=xs, xT=xTs, Wq=Wq, Wk=Wk, Wv=Wv, Wg=Wg, Wab=Wab, Wo=Wo,
            bq=bcolf(bq), bk=bcolf(bk), bv=bcolf(bv), bg=bcolf(bg), bab=bab,
            bo_b=brd(bo), b2_b=brd(b2), gam_b=brd(gamma), bet_b=brd(beta),
            b1T=b1T, W1T=W1T, W2T=W2T, idf=idf, idb=idb,
            maskW=maskW, maskU=maskU, smask=sm,
        ))
    res = run_bass_kernel_spmd(nc, in_maps, core_ids=list(range(8)))
    out = np.empty_like(x)
    for core in range(8):
        b, s = core // 4, core % 4
        out[b, s * TOK:(s + 1) * TOK, :] = res.results[core]["out"]
    DBG_OUT.clear()
    for core in range(8):
        DBG_OUT[core] = np.asarray(res.results[core]["dbg"])
    return out


# version 23 (final)
# speedup vs baseline: 1.1602x; 1.0269x over previous
import numpy as np
import ml_dtypes
import concourse.bass as bass
import concourse.mybir as mybir
from concourse.bass_utils import run_bass_kernel_spmd

F32 = mybir.dt.float32
BF16 = mybir.dt.bfloat16
I32 = mybir.dt.int32
AF = mybir.ActivationFunctionType
OP = mybir.AluOpType
AX = mybir.AxisListType

TOK = 2048
D = 1024
C = 128
HID = 4096
NT = 16
ND = 8
NH = 32
EPS = 1e-5
NEG = -1e30
POS = 1e30

ENGS = ("sync", "gpsimd", "vector", "scalar", "tensor")


class Emitter:
    def __init__(self):
        self.q = {e: [] for e in ENGS}
        self.cnt = {}
        self.sems = {}

    def add_sem(self, name, handle):
        self.sems[name] = handle
        self.cnt[name] = 0

    def on(self, eng, fn, wait=(), inc=None, amt=1):
        tok = None
        if inc is not None:
            self.cnt[inc] += amt
            tok = (inc, self.cnt[inc])
        wl = tuple(t for t in wait if t is not None)
        self.q[eng].append((fn, wl, tok, amt))
        return tok

    def barrier(self):
        return [(s, self.cnt[s]) for s in self.cnt]

    def run(self, eng_name, eng_obj):
        mark = {}
        for fn, waits, tok, amt in self.q[eng_name]:
            seen = {}
            for (s, v) in waits:
                if v > 0 and v > mark.get(s, 0) and seen.get(s, -1) < v:
                    seen[s] = v
            for s, v in seen.items():
                eng_obj.wait_ge(self.sems[s], v)
                mark[s] = v
            if fn is None:
                continue
            inst = fn(eng_obj)
            if tok is not None:
                inst.then_inc(self.sems[tok[0]], amt)


class Bank:

    def __init__(self, ap):
        self.ap = ap
        self.rd = []
        self.wr = []
        self.mode = "r"

    def w(self):
        if self.mode == "r":
            out = list(self.rd)
            self.rd = []
            self.wr = []
            self.mode = "w"
            return out
        return []

    def note_w(self, tok):
        self.wr.append(tok)

    def r(self):
        if self.mode == "w":
            self.mode = "r"
        return list(self.wr)

    def note_r(self, tok):
        self.rd.append(tok)


class Reg:

    def __init__(self, ap=None):
        self.ap = ap
        self.lw = None
        self.rd = []

    def w(self):
        out = [self.lw] + self.rd
        self.rd = []
        return out

    def note_w(self, tok):
        self.lw = tok

    def r(self):
        return [self.lw]

    def note_r(self, tok):
        self.rd.append(tok)


def build_nc(tap=None):
    nc = bass.Bass("TRN2", target_bir_lowering=False)

    def din(name, shp, dt=F32):
        return nc.dram_tensor(name, shp, dt, kind="ExternalInput").ap()

    x_d = din("x", [TOK, D])
    xT_d = din("xT", [D, TOK])
    Wq_d = din("Wq", [D, C]); Wk_d = din("Wk", [D, C])
    Wv_d = din("Wv", [D, C]); Wg_d = din("Wg", [D, C])
    Wab_d = din("Wab", [D, 2])
    Wo_d = din("Wo", [C, D])
    bq_d = din("bq", [C, 1]); bk_d = din("bk", [C, 1])
    bv_d = din("bv", [C, 1]); bg_d = din("bg", [C, 1])
    bab_d = din("bab", [2, 1])
    bo_b_d = din("bo_b", [128, D])
    b2_b_d = din("b2_b", [128, D])
    gam_b_d = din("gam_b", [128, D])
    bet_b_d = din("bet_b", [128, D])
    b1T_d = din("b1T", [128, NH])
    W1T_d = din("W1T", [D, HID])
    W2T_d = din("W2T", [HID, D])
    idf_d = din("idf", [128, 128])
    idb_d = din("idb", [128, 128], BF16)
    maskW_d = din("maskW", [128, 128])
    maskU_d = din("maskU", [128, 128])
    smask_d = din("smask", [128, 6])
    out_d = nc.dram_tensor("out", [TOK, D], F32, kind="ExternalOutput").ap()
    dbg_d = nc.dram_tensor("dbg", [128, 4096], F32, kind="ExternalOutput").ap()

    cc_in = nc.dram_tensor("cc_in", [128, 256], F32).ap()
    cc_out = nc.dram_tensor("cc_out", [512, 256], F32).ap()
    Hb = nc.dram_tensor("Hb", [HID, TOK], BF16).ap()

    from contextlib import ExitStack
    es = ExitStack()
    SB = lambda n, s, d=F32: es.enter_context(nc.sbuf_tensor(n, s, d))
    PS = lambda n, s, d=F32: es.enter_context(nc.psum_tensor(n, s, d))

    with es:
        FA = SB("FA", [128, 12864], F32)

        def fa(off, ln):
            return FA[:, off:off + ln]
        CO = fa(0, 768)
        RW = fa(768, 2048)
        ONES = fa(2816, 128)
        IDF = fa(2944, 128)
        MKW = fa(3072, 128)
        MKU = fa(3200, 128)
        SMK = fa(3328, 6)
        BCOL = fa(3336, 8)
        B1T = fa(3352, 32)
        S1 = fa(3392, 2048)
        WMS = S1
        F1 = fa(5440, 2048)
        RCV = fa(5440, 768)
        CCP = fa(6208, 256)
        ZI = fa(6464, 256)
        GPT = fa(6720, 128)
        HPT = fa(6848, 128)
        F2 = fa(7488, 2048)
        OUT2 = fa(3392, 4096)
        TR1_ = fa(9536, 128)
        TR2_ = fa(9664, 128)
        YTL = fa(9792, 2048)
        W1S = YTL
        W2S = fa(11840, 1024)
        OSTG = fa(11840, 1024)

        BA = SB("BA", [128, 79104], BF16)

        def ba(off, ln):
            return BA[:, off:off + ln]
        R1 = ba(0, NT * D)
        R2 = ba(16384, ND * TOK)
        AT1o = 32768
        qT = ba(AT1o + 0, 2048)
        kT = ba(AT1o + 2048, 2048)
        vT = ba(AT1o + 4096, 2048)
        gT = ba(AT1o + 6144, 2048)
        UCB = ba(AT1o + 8192, 4096)
        PTBs = ba(AT1o + 12288, 2048)
        ULT = ba(AT1o + 14336, 2048)
        GTB = ba(AT1o + 16384, 2048)
        HCB = ba(AT1o + 18432, 2048)
        HTB = ba(AT1o + 20480, 2048)
        KNB = ba(AT1o + 22528, 2048)
        VNB = ba(AT1o + 24576, 2048)
        KPB = ba(AT1o + 26624, 2048)
        GUP = ba(AT1o + 28672, 2048)
        QTT = ba(AT1o + 30720, 2048)
        W2Q = ba(AT1o, NH * D)
        BF2o = 65536
        WQ = ba(BF2o + 0, 1024)
        WK = ba(BF2o + 1024, 1024)
        WV = ba(BF2o + 2048, 1024)
        WG = ba(BF2o + 3072, 1024)
        WO = ba(BF2o + 4096, 1024)
        WAB = ba(BF2o + 5120, 16)
        GB = ba(BF2o + 0, 2048)
        XQN = ba(BF2o + 2048, 2048)
        HSL = ba(BF2o + 0, 2 * TOK)
        W1Q = ba(BF2o + 4096, 2048)
        BF3o = 71680
        WCH = ba(BF3o + 0, 128)
        XP = ba(BF3o + 128, 256)
        YS = ba(BF3o + 384, 896)
        TPP = ba(BF3o + 1280, 512)
        MPT = ba(BF3o + 1792, 256)
        UFT = ba(BF3o + 2048, 256)
        ZBt = ba(BF3o + 2304, 256)
        ZTt = ba(BF3o + 2560, 256)
        GA = ba(BF3o + 2816, 256)
        GAT = ba(BF3o + 3072, 256)
        HA = ba(BF3o + 3328, 256)
        IDB = ba(BF3o + 3584, 128)
        ONEB = ba(BF3o + 3712, 128)
        GMB = ba(BF3o + 3840, 1024)
        BTB = ba(BF3o + 4864, 1024)
        BOB = ba(BF3o + 5888, 1024)
        B2B = BOB
        SQS = ba(BF2o + 4096, 1024)

        IA = SB("IA", [128, 512], I32)
        ZI32 = IA[:, 0:512]
        ZH32 = IA[:, 0:512]

        pb = [Bank(PS(f"pp{i}", [128, 512], F32)) for i in range(6)]
        btf = Bank(PS("ptf", [128, 512], F32))
        btb = Bank(PS("ptb", [128, 1024], BF16))

        em = Emitter()
        for s in ("dS", "dW0", "dW1", "dX", "dM0", "dM1", "dM2", "dR0", "dR1", "dR2", "dR3", "dG", "dA", "v", "s", "t", "g", "cc", "out"):
            em.add_sem(s, es.enter_context(nc.semaphore(s)))

        def col(i, w=16):
            return CO[:, i * 16: i * 16 + w]

        def do_tap(name, region, width, waits):
            if tap == name:
                em.on("gpsimd", (lambda region=region, width=width: (lambda g: g.dma_start(
                    out=dbg_d[:, 0:width], in_=region)))(),
                    wait=list(waits), inc="dG", amt=16)

        (SUM1, SQ1, MU1, R1C, MR1, V1A, V1B,
         SUM2, SQ2, MU2, R2C, NM2,
         SUM3, SQ3, MU3, RHO, NMR, SXC, ISX,
         S2C, IS2, SUM5, SQ5, MU5, R5C, NM5,
         LCOL, BCOLc, GCOL, BGC, LLB, EKP, GLR,
         WM1, WM2, SW1C, ISW1, SW2C, ISW2,
         TMPA, TMPB, TMPC, WMT) = range(43)
        NCS = 43

        x_r = x_d.rearrange("(t p) d -> p t d", p=128)
        tok_x = em.on("gpsimd", lambda g: g.dma_start(
            out=R1[:, :].rearrange("p (t d) -> p t d", t=NT), in_=x_r), inc="dG", amt=16)
        xT_r = xT_d.rearrange("(k p) t -> p k t", p=128)
        tok_xT = em.on("gpsimd", lambda g: g.dma_start(
            out=R2[:, :].rearrange("p (k t) -> p k t", k=ND), in_=xT_r), inc="dG", amt=16)

        def ldw(dst, src, m):
            src_r = src.rearrange("(k p) c -> p k c", p=128)
            return lambda g: g.dma_start(
                out=dst[:, :].rearrange("p (k c) -> p k c", k=ND), in_=src_r)
        tok_wq = em.on("gpsimd", ldw(WQ, Wq_d, C), inc="dG", amt=16)
        tok_wk = em.on("gpsimd", ldw(WK, Wk_d, C), inc="dG", amt=16)
        tok_wv = em.on("gpsimd", ldw(WV, Wv_d, C), inc="dG", amt=16)
        tok_wg = em.on("gpsimd", ldw(WG, Wg_d, C), inc="dG", amt=16)
        tok_wab = em.on("gpsimd", ldw(WAB, Wab_d, 2), inc="dG", amt=16)
        tok_wo = em.on("gpsimd", lambda g: g.dma_start(out=WO[:, :], in_=Wo_d), inc="dG", amt=16)

        csts = []
        for dst, src in [(IDF, idf_d), (MKW, maskW_d), (MKU, maskU_d),
                         (B1T, b1T_d)]:
            csts.append(em.on("sync", (lambda dst=dst, src=src: (
                lambda sy: sy.dma_start(out=dst[:, :], in_=src)))(), inc="dS", amt=16))
        CIDF, CMKW, CMKU, CB1T = csts
        CBOB = em.on("gpsimd", lambda g: g.dma_start(out=BOB[:, :], in_=bo_b_d), inc="dG", amt=16)
        CGMB = em.on("gpsimd", lambda g: g.dma_start(out=GMB[:, :], in_=gam_b_d), inc="dG", amt=16)
        CBTB = em.on("gpsimd", lambda g: g.dma_start(out=BTB[:, :], in_=bet_b_d), inc="dG", amt=16)
        tok_idb = em.on("sync", lambda sy: sy.dma_start(out=IDB[:, :], in_=idb_d), inc="dS", amt=16)
        tok_smk = em.on("sync", lambda sy: sy.dma_start(out=SMK[:, :], in_=smask_d), inc="dS", amt=16)
        for i, src in enumerate([bq_d, bk_d, bv_d, bg_d]):
            csts.append(em.on("sync", (lambda i=i, src=src: (
                lambda sy: sy.dma_start(out=BCOL[:, i:i + 1], in_=src)))(), inc="dS", amt=16))
        tok_bab = em.on("sync", lambda sy: sy.dma_start(out=BCOL[0:2, 4:5], in_=bab_d), inc="dS", amt=16)
        csts.append(tok_bab)
        tok_cst = csts[-1]

        tok_ones = em.on("vector", lambda v: v.memset(ONES[:, :], 1.0), inc="v")
        tok_epsc = em.on("vector", lambda v: v.memset(BCOL[:, 5:6], EPS), wait=[tok_bab], inc="v")
        EPSC = BCOL[:, 5:6]
        tok_oneb = em.on("vector", lambda v: v.memset(ONEB[:, 0:1], 1.0), inc="v")
        tok_negb = em.on("vector", lambda v: v.memset(ONEB[:, 1:2], -1.0), inc="v")

        rS1 = Reg()
        st_toks = []
        for t in range(NT):
            ts_ = em.on("vector", (lambda t=t: (lambda v: v.reduce_sum(
                col(SUM1)[:, t:t + 1], R1[:, t * D:(t + 1) * D], axis=AX.X)))(),
                wait=[tok_x], inc="v")
            tq = em.on("scalar", (lambda t=t: (lambda s: s.activation(
                S1[:, 0:D], R1[:, t * D:(t + 1) * D], AF.Square,
                accum_out=col(SQ1)[:, t:t + 1])))(), wait=[tok_x] + rS1.w(), inc="s")
            rS1.note_w(tq)
            st_toks += [ts_, tq]
        t_mu1 = em.on("vector", lambda v: v.tensor_scalar(
            col(MU1), col(SUM1), 1.0 / D, None, OP.mult), wait=st_toks, inc="v")
        t_e2 = em.on("vector", lambda v: v.tensor_scalar(
            col(V1A), col(SQ1), 1.0 / D, None, OP.mult), wait=st_toks, inc="v")
        t_m2 = em.on("vector", lambda v: v.tensor_mul(col(V1B), col(MU1), col(MU1)),
                     wait=[t_mu1], inc="v")
        t_v1 = em.on("vector", lambda v: v.tensor_sub(col(V1A), col(V1A), col(V1B)),
                     wait=[t_e2, t_m2], inc="v")
        t_sd = em.on("scalar", lambda s: s.activation(col(V1B), col(V1A), AF.Sqrt, bias=EPSC),
                     wait=[t_v1, tok_epsc], inc="s")
        t_r1 = em.on("vector", lambda v: v.reciprocal(col(R1C), col(V1B)), wait=[t_sd], inc="v")
        t_mr1 = em.on("vector", lambda v: v.tensor_mul(col(MR1), col(MU1), col(R1C)),
                      wait=[t_r1], inc="v")

        rROW0 = Reg()
        rRW16 = Reg()

        def bcast_from_col(col_ap, dstreg, dst_ap, pre_toks):
            wb = btf.w()
            tt = em.on("tensor", (lambda col_ap=col_ap: (lambda t: t.transpose(
                btf.ap[0:16, 0:128], col_ap, IDF[:, :])))(),
                wait=wb + list(pre_toks) + [CIDF], inc="t")
            btf.note_w(tt)
            tv = em.on("vector", lambda v: v.tensor_copy(RW[32:48, 384:512], btf.ap[0:16, 0:128]),
                       wait=btf.r() + rRW16.w(), inc="v")
            btf.note_r(tv)
            rRW16.note_w(tv)
            td = em.on("gpsimd", lambda g: g.dma_start(out=RW[0:1, 0:2048], in_=RW[32:48, 384:512]),
                       wait=rRW16.r() + rROW0.w(), inc="dG", amt=16)
            rRW16.note_r(td)
            rROW0.note_w(td)
            return bcast_from_row(RW[0:1, 0:2048], rROW0, dstreg, dst_ap, [td])

        def bcast_from_row(row_ap, rowreg, dstreg, dst_ap, pre_toks):
            last = None
            wdst = dstreg.w()
            for nchk in range(4):
                wb = pb[5].w()
                tm = em.on("tensor", (lambda nchk=nchk, row_ap=row_ap: (lambda t: t.matmul(
                    pb[5].ap[:, 0:512], ONES[0:1, :], row_ap[0:1, nchk * 512:(nchk + 1) * 512],
                    start=True, stop=True)))(),
                    wait=wb + list(pre_toks) + [tok_ones], inc="t")
                pb[5].note_w(tm)
                te = em.on("vector", (lambda nchk=nchk, dst_ap=dst_ap: (lambda v: v.tensor_copy(
                    dst_ap[:, nchk * 512:(nchk + 1) * 512], pb[5].ap[:, 0:512])))(),
                    wait=pb[5].r() + (wdst if nchk == 0 else []), inc="v")
                pb[5].note_r(te)
                if rowreg is not None:
                    rowreg.note_r(te)
                last = te
            dstreg.note_w(last)
            return last

        rF1 = Reg()
        rF2 = Reg()
        tok_r1b = bcast_from_col(col(R1C), rF1, F1, [t_r1])
        tok_mr1b = bcast_from_col(col(MR1), rF2, F2, [t_mr1])

        def colsum(Wt, slot, M, wtok):
            wb = btf.w()
            mm = []
            for k in range(ND):
                tm = em.on("tensor", (lambda k=k, Wt=Wt, M=M: (lambda t: t.matmul(
                    btf.ap[0:M, 2:3], Wt[:, k * M:(k + 1) * M], ONEB[:, 1:2],
                    start=(k == 0), stop=(k == ND - 1))))(),
                    wait=(wb if k == 0 else []) + [wtok, tok_negb] + (mm[-1:] if mm else []),
                    inc="t")
                mm.append(tm)
            btf.note_w(mm[-1])
            te = em.on("vector", (lambda M=M, slot=slot: (lambda v: v.tensor_copy(
                col(slot)[0:M, 0:1], btf.ap[0:M, 2:3])))(), wait=btf.r(), inc="v")
            btf.note_r(te)
            return te
        tok_csq = colsum(WQ, NCS + 0, C, tok_wq)
        tok_csk = colsum(WK, NCS + 1, C, tok_wk)
        tok_csv = colsum(WV, NCS + 2, C, tok_wv)
        tok_csg = colsum(WG, NCS + 3, C, tok_wg)
        tok_csab = colsum(WAB, NCS + 4, 2, tok_wab)

        rYTL = Reg()

        def project(Wt, M, cstok, cs_slot, bias_idx, act, dst_ap, wtok, dstreg, xw=()):
            wy = rYTL.w()
            ev = []
            pair_mms = {}
            for pair in range(2):
                banks = [pb[2 * pair], pb[2 * pair + 1]]
                wbs = [b.w() for b in banks]
                mms = [[], []]
                for k in range(ND):
                    for j in range(2):
                        nchk = 2 * pair + j
                        tm = em.on("tensor", (lambda k=k, nchk=nchk, bank=banks[j], Wt=Wt, M=M: (lambda t: t.matmul(
                            bank.ap[0:M, 0:512], Wt[:, k * M:(k + 1) * M],
                            R2[:, k * TOK + nchk * 512: k * TOK + (nchk + 1) * 512],
                            start=(k == 0), stop=(k == ND - 1))))(),
                            wait=(wbs[j] if k == 0 else []) + [wtok, tok_xT],
                            inc="t")
                        mms[j].append(tm)
                for j in range(2):
                    banks[j].note_w(mms[j][-1])
                    pair_mms[2 * pair + j] = mms[j]
            for nchk in range(4):
                bank = pb[nchk]
                te1 = em.on("vector", (lambda nchk=nchk, bank=bank, M=M: (lambda v: v.tensor_mul(
                    YTL[0:M, nchk * 512:(nchk + 1) * 512], bank.ap[0:M, 0:512],
                    F1[0:M, nchk * 512:(nchk + 1) * 512])))(),
                    wait=bank.r() + [tok_r1b] + (wy if nchk == 0 else []), inc="v")
                bank.note_r(te1)
                rF1.note_r(te1)
                te2 = em.on("vector", (lambda nchk=nchk, M=M, cs_slot=cs_slot: (lambda v: v.scalar_tensor_tensor(
                    YTL[0:M, nchk * 512:(nchk + 1) * 512],
                    F2[0:M, nchk * 512:(nchk + 1) * 512], col(cs_slot)[0:M, 0:1],
                    YTL[0:M, nchk * 512:(nchk + 1) * 512], OP.mult, OP.add)))(),
                    wait=[te1, tok_mr1b, cstok], inc="v")
                rF2.note_r(te2)
                ev.append(te2)
            tse = em.on("scalar", (lambda M=M, bias_idx=bias_idx, act=act, dst_ap=dst_ap: (lambda s: s.activation(
                dst_ap, YTL[0:M, 0:TOK], act, bias=BCOL[0:M, bias_idx:bias_idx + 1])))(),
                wait=ev + list(xw) + [tok_cst], inc="s")
            rYTL.note_w(tse)
            rYTL.note_r(tse)
            if dstreg is not None:
                dstreg.note_w(tse)
            return tse

        tok_qT = project(WQ, C, tok_csq, NCS + 0, 0, AF.Identity, qT[:, 0:TOK], tok_wq, None)
        tok_kTr = project(WK, C, tok_csk, NCS + 1, 1, AF.Identity, S1[:, 0:TOK], tok_wk, rS1)
        tok_vT = project(WV, C, tok_csv, NCS + 2, 2, AF.Identity, vT[:, 0:TOK], tok_wv, None)
        tok_gT = project(WG, C, tok_csg, NCS + 3, 3, AF.Sigmoid, gT[:, 0:TOK], tok_wg, None)
        tok_ab = project(WAB, 2, tok_csab, NCS + 4, 4, AF.Sigmoid, RW[0:2, 0:TOK],
                         tok_wab, None, xw=rROW0.w())
        rROW0.note_w(tok_ab)

        t_k2 = em.on("scalar", lambda s: s.activation(F2[:, 0:TOK], S1[:, 0:TOK], AF.Square),
                     wait=rF2.w() + rS1.r() + [tok_kTr, tok_ab], inc="s")
        rF2.note_w(t_k2)
        rS1.note_r(t_k2)
        kn_last = None
        for nchk in range(4):
            wb = btf.w()
            tm = em.on("tensor", (lambda nchk=nchk: (lambda t: t.matmul(
                btf.ap[0:1, 0:512], ONES[:, 0:1], F2[:, nchk * 512:(nchk + 1) * 512],
                start=True, stop=True)))(), wait=wb + rF2.r() + [tok_ones], inc="t")
            btf.note_w(tm)
            rF2.note_r(tm)
            te = em.on("vector", (lambda nchk=nchk: (lambda v: v.tensor_copy(
                YTL[0:1, nchk * 512:(nchk + 1) * 512], btf.ap[0:1, 0:512])))(),
                wait=btf.r() + (rYTL.w() if nchk == 0 else []), inc="v")
            btf.note_r(te)
            kn_last = te
        rYTL.note_w(kn_last)
        t_ks = em.on("scalar", lambda s: s.activation(YTL[0:1, 0:TOK], YTL[0:1, 0:TOK], AF.Sqrt),
                     wait=rYTL.w(), inc="s")
        rYTL.note_w(t_ks)
        t_kc = em.on("vector", lambda v: v.tensor_scalar(
            YTL[0:1, 0:TOK], YTL[0:1, 0:TOK], 1e-12, None, OP.max), wait=rYTL.w(), inc="v")
        rYTL.note_w(t_kc)
        t_kr = em.on("vector", lambda v: v.reciprocal(YTL[0:1, 0:TOK], YTL[0:1, 0:TOK]),
                     wait=rYTL.w(), inc="v")
        rYTL.note_w(t_kr)
        tok_rkb = bcast_from_row(YTL[0:1, 0:TOK], rYTL, rF2, F2, rF2.w() + [t_kr])
        tok_kT = em.on("vector", lambda v: v.tensor_mul(kT[:, 0:TOK], S1[:, 0:TOK], F2[:, 0:TOK]),
                       wait=[tok_rkb] + rS1.r(), inc="v")
        rS1.note_r(tok_kT)
        rF2.note_r(tok_kT)
        w1_r = W1T_d.rearrange("(k p) h -> p k h", p=128)
        w2_r = W2T_d.rearrange("(k p) dd -> p k dd", p=128)
        w1_slices = [w1_r[:, k, hh * 1024:(hh + 1) * 1024] for k in range(ND) for hh in range(4)]
        w2_slices = [w2_r[:, i, :] for i in range(NH)]
        rWMb = [Reg(), Reg(), Reg()]
        wm_bufs = [F2[:, 0:1024], F2[:, 1024:2048], W2S[:, 0:1024]]
        wmall = []
        wm_parts = {WM1: [], WM2: []}
        wm_sub = {WM1: (SUM1, SQ1), WM2: (MU1, R1C)}
        wm_work = [(a, WM1) for a in w1_slices] + [(a, WM2) for a in w2_slices]
        wm_j = [0]

        def emit_wmean(n):
            for _ in range(n):
                j = wm_j[0]
                if j >= len(wm_work):
                    return
                wm_j[0] += 1
                src_ap, slot = wm_work[j]
                bb_ = j % 3
                td = em.on("sync", (lambda src_ap=src_ap, bb_=bb_: (lambda sy: sy.dma_start(
                    out=wm_bufs[bb_], in_=src_ap)))(),
                    wait=rWMb[bb_].w() + ([tok_kT] if j <= 1 else []),
                    inc="dM%d" % bb_, amt=16)
                rWMb[bb_].note_w(td)
                i2 = len(wm_parts[slot])
                tslot = wm_sub[slot][0] if i2 < 16 else wm_sub[slot][1]
                ta = em.on("scalar", (lambda bb_=bb_, tslot=tslot, i2=i2: (lambda s: s.activation(
                    wm_bufs[bb_], wm_bufs[bb_],
                    AF.Abs, accum_out=col(tslot)[:, i2 % 16:i2 % 16 + 1])))(),
                    wait=rWMb[bb_].w(), inc="s")
                rWMb[bb_].note_w(ta)
                wm_parts[slot].append(ta)
                if len(wm_parts[slot]) == 32:
                    tm_ = em.on("vector", (lambda slot=slot: (lambda v: v.tensor_add(
                        col(slot), col(wm_sub[slot][0]), col(wm_sub[slot][1]))))(),
                        wait=wm_parts[slot], inc="v")
                    wm_parts[slot] = [tm_]
                    wmall.append(tm_)
        emit_wmean(len(wm_work))
        wm1_toks = [wmall[0]]
        wm2_toks = [wmall[1]]
        rW2S_seed = rWMb[2].lw

        t_la = em.on("scalar", lambda s: s.activation(RW[0:1, 0:TOK], RW[0:1, 0:TOK], AF.Ln),
                     wait=[tok_ab] + rROW0.w(), inc="s")
        rROW0.note_w(t_la)
        rL16 = [Reg(), Reg()]
        rB16 = Reg()
        t_l16 = em.on("gpsimd", lambda g: g.dma_start(out=RW[32:48, 0:128], in_=RW[0:1, 0:TOK]),
                      wait=rROW0.r() + rL16[0].w(), inc="dG", amt=16)
        rROW0.note_r(t_l16)
        rL16[0].note_w(t_l16)
        t_b16 = em.on("gpsimd", lambda g: g.dma_start(out=RW[32:48, 256:384], in_=RW[1:2, 0:TOK]),
                      wait=[tok_ab] + rB16.w(), inc="dG", amt=16)
        rB16.note_w(t_b16)
        srco, dsto = 0, 1
        for lvl in range(7):
            sft = 1 << lvl
            w = rL16[srco].r() + rL16[dsto].w()
            t_c1 = em.on("vector", (lambda srco=srco, dsto=dsto, sft=sft: (lambda v: v.tensor_copy(
                RW[32:48, dsto * 128:dsto * 128 + sft], RW[32:48, srco * 128:srco * 128 + sft])))(),
                wait=w, inc="v")
            t_c2 = em.on("vector", (lambda srco=srco, dsto=dsto, sft=sft: (lambda v: v.tensor_add(
                RW[32:48, dsto * 128 + sft:dsto * 128 + 128],
                RW[32:48, srco * 128 + sft:srco * 128 + 128],
                RW[32:48, srco * 128:srco * 128 + 128 - sft])))(), wait=w, inc="v")
            rL16[srco].note_r(t_c2)
            rL16[dsto].note_w(t_c2)
            rL16[dsto].note_w(t_c1)
            rL16[dsto].note_w(t_c2)
            srco, dsto = dsto, srco
        l16o = srco * 128
        tok_l16 = rL16[srco].lw

        wb = btf.w()
        t_lc = em.on("tensor", (lambda l16o=l16o: (lambda t: t.transpose(
            btf.ap[:, 0:16], RW[32:48, l16o:l16o + 128], IDF[32:48, 32:48])))(),
            wait=wb + rL16[srco].r() + [CIDF], inc="t")
        btf.note_w(t_lc)
        rL16[srco].note_r(t_lc)
        t_lcv = em.on("vector", lambda v: v.tensor_copy(col(LCOL), btf.ap[:, 0:16]),
                      wait=btf.r(), inc="v")
        btf.note_r(t_lcv)
        wb = btf.w()
        t_bc_ = em.on("tensor", lambda t: t.transpose(
            btf.ap[:, 16:32], RW[32:48, 256:384], IDF[32:48, 32:48]),
            wait=wb + rB16.r() + [CIDF], inc="t")
        btf.note_w(t_bc_)
        rB16.note_r(t_bc_)
        t_bcv = em.on("vector", lambda v: v.tensor_copy(col(BCOLc), btf.ap[:, 16:32]),
                      wait=btf.r(), inc="v")
        btf.note_r(t_bcv)
        t_gc = em.on("scalar", lambda s: s.activation(col(GCOL), col(LCOL), AF.Exp),
                     wait=[t_lcv], inc="s")
        t_bg = em.on("vector", lambda v: v.tensor_mul(col(BGC), col(BCOLc), col(GCOL)),
                     wait=[t_bcv, t_gc], inc="v")
        rRW4 = Reg()
        t_ll = em.on("gpsimd", (lambda l16o=l16o: (lambda g: g.dma_start(
            out=col(V1A)[0:1, 0:16], in_=RW[32:48, l16o + 127:l16o + 128])))(),
            wait=rL16[srco].r() + rRW4.w(), inc="dG", amt=16)
        rL16[srco].note_r(t_ll)
        rRW4.note_w(t_ll)
        wb = btf.w()
        t_llb = em.on("tensor", lambda t: t.matmul(
            btf.ap[:, 32:48], ONES[0:1, :], col(V1A)[0:1, 0:16], start=True, stop=True),
            wait=wb + rRW4.r() + [tok_ones], inc="t")
        btf.note_w(t_llb)
        rRW4.note_r(t_llb)
        t_llbv = em.on("vector", lambda v: v.tensor_copy(col(LLB), btf.ap[:, 32:48]),
                       wait=btf.r(), inc="v")
        btf.note_r(t_llbv)
        t_ek1 = em.on("vector", lambda v: v.tensor_sub(col(TMPC), col(LLB), col(LCOL)),
                      wait=[t_llbv, t_lcv], inc="v")
        t_ekp = em.on("scalar", lambda s: s.activation(col(EKP), col(TMPC), AF.Exp),
                      wait=[t_ek1], inc="s")
        t_glr = em.on("scalar", lambda s: s.activation(col(GLR), col(LLB), AF.Exp),
                      wait=[t_llbv], inc="s")

        t_lrow = em.on("gpsimd", (lambda l16o=l16o: (lambda g: g.dma_start(
            out=RW[0:1, 0:TOK], in_=RW[32:48, l16o:l16o + 128])))(),
            wait=rL16[srco].r() + rROW0.w(), inc="dG", amt=16)
        rL16[srco].note_r(t_lrow)
        rROW0.note_w(t_lrow)
        tok_ljb = bcast_from_row(RW[0:1, 0:TOK], rROW0, rS1, S1, rS1.w() + [t_lrow])
        tok_gb = em.on("scalar", lambda s: s.activation(GB[:, 0:TOK], S1[:, 0:TOK], AF.Exp),
                       wait=rS1.r(), inc="s")
        rS1.note_r(tok_gb)

        rWCH = Reg(); rXP = [Reg(), Reg()]; rYS = [Reg() for _ in range(7)]
        rTPP = [Reg(), Reg()]; rTR1 = Reg(); rTR2 = Reg()
        tgu_toks = [None] * NT
        tqt_toks = [None] * NT
        hcb_toks = [None] * NT
        htb_toks = [None] * NT
        gtb_toks = [None] * NT
        knb_toks = [None] * NT
        ucb_toks = [None] * NT
        ptb_toks = [None] * NT
        ult_toks = [None] * NT
        ga_tok = []
        gacc_cur = 0
        for c in range(NT):
            cC = slice(c * C, (c + 1) * C)
            wb = btb.w()
            tt1 = em.on("tensor", (lambda c=c: (lambda t: t.transpose(
                btb.ap[:, 0:128], kT[:, c * C:(c + 1) * C], IDB[:, :])))(),
                wait=wb + [tok_kT, tok_idb], inc="t")
            tt2 = em.on("tensor", (lambda c=c: (lambda t: t.transpose(
                btb.ap[:, 128:256], vT[:, c * C:(c + 1) * C], IDB[:, :])))(),
                wait=[tok_vT], inc="t")
            btb.note_w(tt1); btb.note_w(tt2)
            rb = btb.r()
            tv1 = em.on("vector", (lambda c=c: (lambda v: v.tensor_copy(
                KNB[:, c * C:(c + 1) * C], btb.ap[:, 0:128])))(), wait=rb, inc="v")
            tv2 = em.on("vector", (lambda c=c: (lambda v: v.tensor_copy(
                VNB[:, c * C:(c + 1) * C], btb.ap[:, 128:256])))(), wait=rb, inc="v")
            btb.note_r(tv1); btb.note_r(tv2)
            knb_toks[c] = tv1
            tkp = em.on("vector", (lambda c=c: (lambda v: v.tensor_scalar(
                KPB[:, c * C:(c + 1) * C], KNB[:, c * C:(c + 1) * C],
                col(EKP)[:, c:c + 1], None, OP.mult)))(), wait=[tv1, t_ekp], inc="v")
            wb = pb[2].w()
            tkk = em.on("tensor", (lambda c=c: (lambda t: t.matmul(
                pb[2].ap[:, 0:128], kT[:, c * C:(c + 1) * C], kT[:, c * C:(c + 1) * C],
                start=True, stop=True)))(), wait=wb + [tok_kT], inc="t")
            pb[2].note_w(tkk)
            td1 = em.on("vector", (lambda c=c: (lambda v: v.tensor_scalar(
                TR1_[:, :], S1[:, c * C:(c + 1) * C], col(LCOL)[:, c:c + 1], None,
                OP.subtract)))(), wait=rTR1.w() + [tok_ljb, t_lcv], inc="v")
            rS1.note_r(td1)
            rTR1.note_w(td1)
            td2 = em.on("vector", lambda v: v.tensor_add(TR2_[:, :], TR1_[:, :], MKW[:, :]),
                        wait=rTR2.w() + rTR1.r() + [CMKW], inc="v")
            rTR1.note_r(td2)
            rTR2.note_w(td2)
            tgs = em.on("scalar", lambda s: s.activation(XP[:, 0:128], TR2_[:, :], AF.Exp, scale=-1.0),
                        wait=rTR2.r() + rXP[0].w(), inc="s")
            rTR2.note_r(tgs)
            rXP[0].note_w(tgs)
            tw = em.on("vector", (lambda c=c: (lambda v: v.scalar_tensor_tensor(
                WCH[:, :], pb[2].ap[:, 0:128], col(BCOLc)[:, c:c + 1], XP[:, 0:128],
                OP.mult, OP.mult)))(), wait=pb[2].r() + rXP[0].r() + rWCH.w() + [t_bcv], inc="v")
            pb[2].note_r(tw)
            rXP[0].note_r(tw)
            rWCH.note_w(tw)
            tdu = em.on("vector", lambda v: v.tensor_add(TR2_[:, :], TR1_[:, :], MKU[:, :]),
                        wait=rTR2.w() + rTR1.r() + [CMKU], inc="v")
            rTR1.note_r(tdu)
            rTR2.note_w(tdu)
            tgu = em.on("scalar", (lambda c=c: (lambda s: s.activation(
                GUP[:, c * C:(c + 1) * C], TR2_[:, :], AF.Exp)))(), wait=rTR2.r(), inc="s")
            rTR2.note_r(tgu)
            tgu_toks[c] = tgu
            tqt = em.on("vector", (lambda c=c: (lambda v: v.tensor_mul(
                QTT[:, c * C:(c + 1) * C], qT[:, c * C:(c + 1) * C],
                GB[:, c * C:(c + 1) * C])))(), wait=[tok_qT, tok_gb], inc="v")
            tqt_toks[c] = tqt
            wb = btb.w()
            ty0 = em.on("tensor", lambda t: t.transpose(btb.ap[:, 256:384], WCH[:, :], IDB[:, :]),
                        wait=wb + rWCH.r() + [tok_idb], inc="t")
            btb.note_w(ty0)
            rWCH.note_r(ty0)
            ty0v = em.on("vector", lambda v: v.tensor_copy(YS[:, 0:128], btb.ap[:, 256:384]),
                         wait=btb.r() + rYS[0].w(), inc="v")
            btb.note_r(ty0v)
            rYS[0].note_w(ty0v)
            w0 = rTPP[0].w()
            tr1 = em.on("vector", (lambda c=c: (lambda v: v.tensor_scalar(
                TPP[:, 0:128], VNB[:, c * C:(c + 1) * C], col(BCOLc)[:, c:c + 1],
                None, OP.mult)))(), wait=w0 + [tv2, t_bcv], inc="v")
            tr2 = em.on("vector", (lambda c=c: (lambda v: v.tensor_scalar(
                TPP[:, 128:256], KNB[:, c * C:(c + 1) * C], col(BGC)[:, c:c + 1],
                None, OP.mult)))(), wait=w0 + [tv1, t_bg], inc="v")
            rTPP[0].note_w(tr1)
            rTPP[0].note_w(tr2)
            if c == 0:
                do_tap("wch0", WCH[:, :], 128, [tw])
                do_tap("ys00", YS[:, 0:128], 128, [ty0v])
                do_tap("tpp0", TPP[:, 0:256], 256, [tr1, tr2])
                do_tap("kk0", pb[2].ap[:, 0:128], 128, [tkk]) if False else None
            xcur_reg, xcur_ap = rWCH, WCH[:, :]
            ycur_reg, ycur_ap = rYS[0], YS[:, 0:128]
            for kk in range(1, 7):
                if kk <= 5:
                    xo = kk % 2
                    wb = pb[3].w()
                    tmx = em.on("tensor", (lambda a=ycur_ap, b=xcur_ap: (lambda t: t.matmul(
                        pb[3].ap[:, 0:128], a, b, start=True, stop=True)))(),
                        wait=wb + ycur_reg.r() + xcur_reg.r(), inc="t")
                    pb[3].note_w(tmx)
                    ycur_reg.note_r(tmx)
                    xcur_reg.note_r(tmx)
                    tvx = em.on("vector", (lambda xo=xo: (lambda v: v.tensor_copy(
                        XP[:, xo * 128:(xo + 1) * 128], pb[3].ap[:, 0:128])))(),
                        wait=pb[3].r() + rXP[xo].w(), inc="v")
                    pb[3].note_r(tvx)
                    rXP[xo].note_w(tvx)
                wb = pb[4].w()
                tmy = em.on("tensor", (lambda a=xcur_ap, b=ycur_ap: (lambda t: t.matmul(
                    pb[4].ap[:, 0:128], a, b, start=True, stop=True)))(),
                    wait=wb + ycur_reg.r() + xcur_reg.r(), inc="t")
                pb[4].note_w(tmy)
                ycur_reg.note_r(tmy)
                xcur_reg.note_r(tmy)
                tvy = em.on("vector", (lambda kk=kk: (lambda v: v.tensor_copy(
                    YS[:, kk * 128:(kk + 1) * 128], pb[4].ap[:, 0:128])))(),
                    wait=pb[4].r() + rYS[kk].w(), inc="v")
                pb[4].note_r(tvy)
                rYS[kk].note_w(tvy)
                if kk <= 5:
                    xcur_reg, xcur_ap = rXP[xo], XP[:, xo * 128:(xo + 1) * 128]
                ycur_reg, ycur_ap = rYS[kk], YS[:, kk * 128:(kk + 1) * 128]
            if c == 0:
                do_tap("ys10", YS[:, 128:256], 128, [])
                do_tap("ysall", YS[:, 0:896], 896, [])
            for i, kk in enumerate(range(6, -1, -1)):
                cur, nxt = i % 2, 1 - (i % 2)
                wb = pb[3].w()
                tm = em.on("tensor", (lambda cur=cur, kk=kk: (lambda t: t.matmul(
                    pb[3].ap[:, 0:256], YS[:, kk * 128:(kk + 1) * 128],
                    TPP[:, cur * 256:(cur + 1) * 256], start=True, stop=True)))(),
                    wait=wb + rYS[kk].r() + rTPP[cur].r(), inc="t")
                pb[3].note_w(tm)
                rYS[kk].note_r(tm)
                rTPP[cur].note_r(tm)
                if kk > 0:
                    tv = em.on("vector", (lambda cur=cur, nxt=nxt: (lambda v: v.tensor_add(
                        TPP[:, nxt * 256:(nxt + 1) * 256], TPP[:, cur * 256:(cur + 1) * 256],
                        pb[3].ap[:, 0:256])))(),
                        wait=pb[3].r() + rTPP[nxt].w() + rTPP[cur].r(), inc="v")
                    rTPP[cur].note_r(tv)
                    rTPP[nxt].note_w(tv)
                else:
                    tv = em.on("vector", (lambda c=c, cur=cur: (lambda v: v.tensor_sub(
                        UCB[:, c * 256:(c + 1) * 256], TPP[:, cur * 256:(cur + 1) * 256],
                        pb[3].ap[:, 0:256])))(), wait=pb[3].r() + rTPP[cur].r(), inc="v")
                    rTPP[cur].note_r(tv)
                pb[3].note_r(tv)
            tok_ucb = tv
            ucb_toks[c] = tok_ucb
            wb = btb.w()
            tpt = em.on("tensor", (lambda c=c: (lambda t: t.transpose(
                btb.ap[:, 384:512], UCB[:, c * 256 + 128:c * 256 + 256], IDB[:, :])))(),
                wait=wb + [tok_ucb], inc="t")
            tut = em.on("tensor", (lambda c=c: (lambda t: t.transpose(
                btb.ap[:, 512:640], UCB[:, c * 256:c * 256 + 128], IDB[:, :])))(),
                wait=[tok_ucb], inc="t")
            btb.note_w(tpt); btb.note_w(tut)
            rb = btb.r()
            tptv = em.on("vector", (lambda c=c: (lambda v: v.tensor_copy(
                PTBs[:, c * C:(c + 1) * C], btb.ap[:, 384:512])))(), wait=rb, inc="v")
            tutv = em.on("vector", (lambda c=c: (lambda v: v.tensor_copy(
                ULT[:, c * C:(c + 1) * C], btb.ap[:, 512:640])))(), wait=rb, inc="v")
            btb.note_r(tptv); btb.note_r(tutv)
            ptb_toks[c] = tptv
            ult_toks[c] = tutv
            wb = pb[2].w()
            thm = em.on("tensor", (lambda c=c: (lambda t: t.matmul(
                pb[2].ap[:, 0:128], KPB[:, c * C:(c + 1) * C],
                UCB[:, c * 256:c * 256 + 128], start=True, stop=True)))(),
                wait=wb + [tkp, tok_ucb], inc="t")
            thtm = em.on("tensor", (lambda c=c: (lambda t: t.matmul(
                pb[2].ap[:, 128:256], UCB[:, c * 256:c * 256 + 128],
                KPB[:, c * C:(c + 1) * C], start=True, stop=True)))(), inc="t")
            tgm = em.on("tensor", (lambda c=c: (lambda t: t.matmul(
                pb[2].ap[:, 256:384], UCB[:, c * 256 + 128:c * 256 + 256],
                KPB[:, c * C:(c + 1) * C], start=True, stop=True)))(), inc="t")
            pb[2].note_w(thm); pb[2].note_w(thtm); pb[2].note_w(tgm)
            rb = pb[2].r()
            thv = em.on("vector", (lambda c=c: (lambda v: v.tensor_copy(
                HCB[:, c * C:(c + 1) * C], pb[2].ap[:, 0:128])))(), wait=rb, inc="v")
            thtv = em.on("vector", (lambda c=c: (lambda v: v.tensor_copy(
                HTB[:, c * C:(c + 1) * C], pb[2].ap[:, 128:256])))(), inc="v")
            tgv = em.on("vector", (lambda c=c: (lambda v: v.scalar_tensor_tensor(
                GTB[:, c * C:(c + 1) * C], IDF[:, :], col(GLR)[:, c:c + 1],
                pb[2].ap[:, 256:384], OP.mult, OP.subtract)))(),
                wait=[t_glr, CIDF], inc="v")
            pb[2].note_r(thv); pb[2].note_r(thtv); pb[2].note_r(tgv)
            hcb_toks[c] = thv
            htb_toks[c] = thtv
            gtb_toks[c] = tgv
            if c == 0:
                wb = btb.w()
                tga = em.on("tensor", lambda t: t.transpose(btb.ap[:, 640:768], GTB[:, 0:C], IDB[:, :]),
                            wait=wb + [tgv], inc="t")
                btb.note_w(tga)
                tgav = em.on("vector", lambda v: v.tensor_copy(GA[:, 0:C], btb.ap[:, 640:768]),
                             wait=btb.r(), inc="v")
                btb.note_r(tgav)
                tgatv = em.on("vector", lambda v: v.tensor_copy(GAT[:, 0:C], GTB[:, 0:C]),
                              wait=[tgv], inc="v")
                thav = em.on("vector", lambda v: v.tensor_copy(HA[:, 0:C], HCB[:, 0:C]),
                             wait=[thv], inc="v")
                ga_tok = [tgav, tgatv, thav]
                gacc_cur = 0
            else:
                prv, nxt = gacc_cur, 1 - gacc_cur
                wb = pb[2].w()
                tm1 = em.on("tensor", (lambda c=c, prv=prv: (lambda t: t.matmul(
                    pb[2].ap[:, 384:512], GTB[:, c * C:(c + 1) * C],
                    GA[:, prv * C:(prv + 1) * C], start=True, stop=True)))(),
                    wait=wb + ga_tok + [tgv], inc="t")
                pb[2].note_w(tm1)
                tv1_ = em.on("vector", (lambda nxt=nxt: (lambda v: v.tensor_copy(
                    GA[:, nxt * C:(nxt + 1) * C], pb[2].ap[:, 384:512])))(),
                    wait=pb[2].r(), inc="v")
                pb[2].note_r(tv1_)
                wb = pb[2].w()
                tm2 = em.on("tensor", (lambda c=c, prv=prv: (lambda t: t.matmul(
                    pb[2].ap[:, 384:512], GA[:, prv * C:(prv + 1) * C],
                    GTB[:, c * C:(c + 1) * C], start=True, stop=True)))(),
                    wait=wb + ga_tok + [tgv], inc="t")
                pb[2].note_w(tm2)
                tv2_ = em.on("vector", (lambda nxt=nxt: (lambda v: v.tensor_copy(
                    GAT[:, nxt * C:(nxt + 1) * C], pb[2].ap[:, 384:512])))(),
                    wait=pb[2].r(), inc="v")
                pb[2].note_r(tv2_)
                wb = pb[2].w()
                tm3 = em.on("tensor", (lambda c=c, prv=prv: (lambda t: t.matmul(
                    pb[2].ap[:, 384:512], GTB[:, c * C:(c + 1) * C],
                    HA[:, prv * C:(prv + 1) * C], start=True, stop=True)))(),
                    wait=wb + ga_tok + [tgv], inc="t")
                pb[2].note_w(tm3)
                tv3_ = em.on("vector", (lambda c=c, nxt=nxt: (lambda v: v.tensor_add(
                    HA[:, nxt * C:(nxt + 1) * C], HCB[:, c * C:(c + 1) * C],
                    pb[2].ap[:, 384:512])))(), wait=pb[2].r() + [thv], inc="v")
                pb[2].note_r(tv3_)
                ga_tok = [tv1_, tv2_, tv3_]
                gacc_cur = nxt

        def wmean_finish(wmslot, swslot, iswslot, nelem, wtok):
            t1 = em.on("vector", (lambda wmslot=wmslot: (lambda v: v.reduce_sum(
                col(WMT)[:, 0:1], col(wmslot), axis=AX.X)))(), wait=[wtok], inc="v")
            wb = btf.w()
            tm = em.on("tensor", lambda t: t.matmul(
                btf.ap[0:1, 0:1], ONES[:, 0:1], col(WMT)[:, 0:1], start=True, stop=True),
                wait=wb + [t1, tok_ones], inc="t")
            btf.note_w(tm)
            t2 = em.on("vector", (lambda nelem=nelem, wmslot=wmslot: (lambda v: v.tensor_scalar(
                col(wmslot)[0:1, 1:2], btf.ap[0:1, 0:1], 1.0 / nelem, EPS, OP.mult, OP.max)))(),
                wait=btf.r(), inc="v")
            btf.note_r(t2)
            wb = btf.w()
            tb = em.on("tensor", (lambda wmslot=wmslot: (lambda t: t.matmul(
                btf.ap[:, 1:2], ONES[0:1, :], col(wmslot)[0:1, 1:2], start=True, stop=True)))(),
                wait=wb + [t2], inc="t")
            btf.note_w(tb)
            t3 = em.on("vector", (lambda iswslot=iswslot: (lambda v: v.tensor_copy(
                col(iswslot)[:, 0:1], btf.ap[:, 1:2])))(), wait=btf.r(), inc="v")
            btf.note_r(t3)
            t4 = em.on("vector", (lambda swslot=swslot, iswslot=iswslot: (lambda v: v.reciprocal(
                col(swslot)[:, 0:1], col(iswslot)[:, 0:1])))(), wait=[t3], inc="v")
            return t3, t4
        tok_isw1, tok_sw1 = wmean_finish(WM1, SW1C, ISW1, D * HID, wm1_toks[-1])
        tok_isw2, tok_sw2 = wmean_finish(WM2, SW2C, ISW2, D * HID, wm2_toks[-1])

        tp1 = em.on("vector", (lambda gc=gacc_cur: (lambda v: v.tensor_copy(
            CCP[:, 0:128], GAT[:, gc * C:(gc + 1) * C])))(), wait=ga_tok, inc="v")
        tp2 = em.on("vector", (lambda gc=gacc_cur: (lambda v: v.tensor_copy(
            CCP[:, 128:256], HA[:, gc * C:(gc + 1) * C])))(), wait=ga_tok, inc="v")
        tcc1 = em.on("gpsimd", lambda g: g.dma_start(out=cc_in, in_=CCP[:, :]),
                     wait=[tp1, tp2], inc="dG", amt=16)
        tcc2 = em.on("gpsimd", lambda g: g.collective_compute(
            "AllGather", OP.bypass, replica_groups=[[0, 1, 2, 3], [4, 5, 6, 7]],
            ins=[cc_in], outs=[cc_out]), wait=[tcc1], inc="cc")
        tcc3 = em.on("gpsimd", lambda g: g.dma_start(
            out=RCV[:, :].rearrange("p (s w) -> p s w", s=3),
            in_=cc_out.rearrange("(s p) w -> p s w", p=128)[:, 0:3, :]),
            wait=[tcc2], inc="dG", amt=16)

        tz0 = em.on("vector", lambda v: v.memset(ZI[:, 0:C], 0.0), inc="v")
        zi_cur = 0
        tz_tok = [tz0, tcc3, tok_smk]
        rGPT = Reg(); rHPT = Reg()
        for p in range(3):
            tb1 = em.on("vector", (lambda p=p: (lambda v: v.tensor_scalar(
                GPT[:, :], RCV[:, p * 256:p * 256 + 128], SMK[:, p:p + 1], None,
                OP.mult)))(), wait=tz_tok + rGPT.w(), inc="v")
            rGPT.note_w(tb1)
            tb2 = em.on("vector", (lambda p=p: (lambda v: v.scalar_tensor_tensor(
                GPT[:, :], IDF[:, :], SMK[:, 3 + p:4 + p], GPT[:, :],
                OP.mult, OP.add)))(), wait=rGPT.w() + [CIDF], inc="v")
            rGPT.note_w(tb2)
            tb3 = em.on("vector", (lambda p=p: (lambda v: v.tensor_scalar(
                HPT[:, :], RCV[:, p * 256 + 128:p * 256 + 256], SMK[:, p:p + 1],
                None, OP.mult)))(), wait=tz_tok + rHPT.w(), inc="v")
            rHPT.note_w(tb3)
            nxt = 1 - zi_cur
            wb = btf.w()
            tzm = em.on("tensor", (lambda zc=zi_cur: (lambda t: t.matmul(
                btf.ap[:, 128:256], GPT[:, :], ZI[:, zc * C:(zc + 1) * C],
                start=True, stop=True)))(), wait=wb + rGPT.r() + rHPT.r() + tz_tok, inc="t")
            btf.note_w(tzm)
            rGPT.note_r(tzm)
            tza = em.on("vector", (lambda nxt=nxt: (lambda v: v.tensor_add(
                ZI[:, nxt * C:(nxt + 1) * C], HPT[:, :], btf.ap[:, 128:256])))(),
                wait=btf.r() + rHPT.r(), inc="v")
            btf.note_r(tza)
            rHPT.note_r(tza)
            tz_tok = [tza]
            zi_cur = nxt
        tzb = em.on("vector", (lambda zc=zi_cur: (lambda v: v.tensor_copy(
            ZBt[:, 0:C], ZI[:, zc * C:(zc + 1) * C])))(), wait=tz_tok, inc="v")
        wb = btf.w()
        tzt = em.on("tensor", (lambda zc=zi_cur: (lambda t: t.transpose(
            btf.ap[:, 256:384], ZI[:, zc * C:(zc + 1) * C], IDF[:, :])))(),
            wait=wb + tz_tok + [CIDF], inc="t")
        btf.note_w(tzt)
        tztv = em.on("vector", lambda v: v.tensor_copy(ZTt[:, 0:C], btf.ap[:, 256:384]),
                     wait=btf.r(), inc="v")
        btf.note_r(tztv)

        do_tap("zine", ZBt[:, 0:128], 128, [tzb, tztv])
        do_tap("gtbe", GTB[:, 0:2048], 2048, gtb_toks)
        do_tap("hcbe", HCB[:, 0:2048], 2048, hcb_toks)
        do_tap("gupe", GUP[:, 0:2048], 2048, tgu_toks)
        do_tap("qte", qT[:, 0:2048], 2048, [tok_qT])
        do_tap("kte", kT[:, 0:2048], 2048, [tok_kT])
        do_tap("lje", S1[:, 0:2048], 2048, [tok_ljb])
        do_tap("ucbe", UCB[:, 0:4096], 4096, ucb_toks[0:8])
        rZ = [Reg(), Reg()]
        rZT = [Reg(), Reg()]
        rZ[0].note_w(tzb)
        rZT[0].note_w(tztv)
        rUFT = [Reg(), Reg()]
        rMPT = [Reg(), Reg()]
        x2_toks = [None] * NT
        for c in range(NT):
            zc = c % 2
            zn = (c + 1) % 2
            uo = c % 2
            if c < NT - 1:
                wb = pb[2].w()
                tzm1 = em.on("tensor", (lambda c=c, zc=zc: (lambda t: t.matmul(
                    pb[2].ap[:, 0:128], GTB[:, c * C:(c + 1) * C], ZBt[:, zc * C:(zc + 1) * C],
                    start=True, stop=True)))(),
                    wait=wb + rZ[zc].r() + [gtb_toks[c]], inc="t")
                tzm2 = em.on("tensor", (lambda c=c, zc=zc: (lambda t: t.matmul(
                    pb[2].ap[:, 128:256], ZBt[:, zc * C:(zc + 1) * C], GTB[:, c * C:(c + 1) * C],
                    start=True, stop=True)))(), inc="t")
                pb[2].note_w(tzm1); pb[2].note_w(tzm2)
                rZ[zc].note_r(tzm1); rZ[zc].note_r(tzm2)
                rb = pb[2].r()
                tza1 = em.on("vector", (lambda c=c, zn=zn: (lambda v: v.tensor_add(
                    ZBt[:, zn * C:(zn + 1) * C], HCB[:, c * C:(c + 1) * C],
                    pb[2].ap[:, 0:128])))(), wait=rb + rZ[zn].w() + [hcb_toks[c]], inc="v")
                tza2 = em.on("vector", (lambda c=c, zn=zn: (lambda v: v.tensor_add(
                    ZTt[:, zn * C:(zn + 1) * C], HTB[:, c * C:(c + 1) * C],
                    pb[2].ap[:, 128:256])))(), wait=rZT[zn].w() + [htb_toks[c]], inc="v")
                pb[2].note_r(tza1); pb[2].note_r(tza2)
                rZ[zn] = Reg(); rZ[zn].note_w(tza1)
                rZT[zn] = Reg(); rZT[zn].note_w(tza2)
            wb = pb[3].w()
            tm1 = em.on("tensor", (lambda c=c, zc=zc: (lambda t: t.matmul(
                pb[3].ap[:, 0:128], ZBt[:, zc * C:(zc + 1) * C], PTBs[:, c * C:(c + 1) * C],
                start=True, stop=True)))(),
                wait=wb + rZ[zc].r() + [ptb_toks[c]], inc="t")
            pb[3].note_w(tm1)
            rZ[zc].note_r(tm1)
            tvu = em.on("vector", (lambda c=c, uo=uo: (lambda v: v.tensor_sub(
                UFT[:, uo * 128:(uo + 1) * 128], ULT[:, c * C:(c + 1) * C],
                pb[3].ap[:, 0:128])))(),
                wait=pb[3].r() + rUFT[uo].w() + [ult_toks[c]], inc="v")
            pb[3].note_r(tvu)
            rUFT[uo] = Reg(); rUFT[uo].note_w(tvu)
            wb = pb[3].w()
            tm2 = em.on("tensor", (lambda c=c, uo=uo: (lambda t: t.matmul(
                pb[3].ap[:, 128:256], UFT[:, uo * 128:(uo + 1) * 128],
                qT[:, c * C:(c + 1) * C], start=True, stop=True)))(),
                wait=wb + rUFT[uo].r() + [tok_qT], inc="t")
            pb[3].note_w(tm2)
            rUFT[uo].note_r(tm2)
            tvm = em.on("vector", (lambda c=c, uo=uo: (lambda v: v.tensor_mul(
                MPT[:, uo * 128:(uo + 1) * 128], GUP[:, c * C:(c + 1) * C],
                pb[3].ap[:, 128:256])))(),
                wait=pb[3].r() + rMPT[uo].w() + [tgu_toks[c]], inc="v")
            pb[3].note_r(tvm)
            rMPT[uo] = Reg(); rMPT[uo].note_w(tvm)
            wb = pb[3].w()
            tm3 = em.on("tensor", (lambda c=c, zc=zc: (lambda t: t.matmul(
                pb[3].ap[:, 256:384], ZTt[:, zc * C:(zc + 1) * C], QTT[:, c * C:(c + 1) * C],
                start=True, stop=False)))(),
                wait=wb + rZT[zc].r() + [tqt_toks[c]], inc="t")
            tm4 = em.on("tensor", (lambda c=c, uo=uo: (lambda t: t.matmul(
                pb[3].ap[:, 256:384], KNB[:, c * C:(c + 1) * C], MPT[:, uo * 128:(uo + 1) * 128],
                start=False, stop=True)))(), wait=rMPT[uo].r() + [knb_toks[c]], inc="t")
            pb[3].note_w(tm3); pb[3].note_w(tm4)
            rZT[zc].note_r(tm3)
            rMPT[uo].note_r(tm4)
            tvo = em.on("vector", (lambda c=c, uo=uo: (lambda v: v.tensor_mul(
                UFT[:, uo * 128:(uo + 1) * 128], gT[:, c * C:(c + 1) * C],
                pb[3].ap[:, 256:384])))(),
                wait=pb[3].r() + rUFT[uo].w() + [tok_gT], inc="v")
            pb[3].note_r(tvo)
            rUFT[uo] = Reg(); rUFT[uo].note_w(tvo)
            yo = c % 2
            wy = rYTL.w() if c == 0 else []
            oy = []
            for nn in range(2):
                wb = pb[4].w()
                tm5 = em.on("tensor", (lambda nn=nn, uo=uo: (lambda t: t.matmul(
                    pb[4].ap[:, 0:512], UFT[:, uo * 128:(uo + 1) * 128],
                    WO[:, nn * 512:(nn + 1) * 512], start=True, stop=True)))(),
                    wait=wb + rUFT[uo].r() + [tok_wo], inc="t")
                pb[4].note_w(tm5)
                rUFT[uo].note_r(tm5)
                tv5 = em.on("vector", (lambda c=c, nn=nn, yo=yo: (lambda v: v.scalar_tensor_tensor(
                    YTL[:, yo * D + nn * 512: yo * D + (nn + 1) * 512],
                    pb[4].ap[:, 0:512], 1.0, BOB[:, nn * 512:(nn + 1) * 512],
                    OP.mult, OP.add,
                    accum_out=col(TMPA)[:, nn:nn + 1])))(),
                    wait=pb[4].r() + wy + [CBOB] +
                         ([x2_toks[c - 2]] if c >= 2 else []), inc="v")
                pb[4].note_r(tv5)
                oy.append(tv5)
            tsum = em.on("vector", (lambda c=c: (lambda v: v.tensor_add(
                col(SUM2)[:, c:c + 1], col(TMPA)[:, 0:1], col(TMPA)[:, 1:2])))(),
                wait=oy, inc="v")
            tsq = em.on("scalar", (lambda c=c, yo=yo: (lambda s: s.activation(
                S1[:, 0:1024], YTL[:, yo * D:(yo + 1) * D], AF.Square,
                accum_out=col(SQ2)[:, c:c + 1])))(), wait=oy + rS1.w(), inc="s")
            rS1.note_w(tsq)
            t1 = em.on("vector", (lambda c=c: (lambda v: v.tensor_scalar(
                col(MU2)[:, c:c + 1], col(SUM2)[:, c:c + 1], 1.0 / D, None, OP.mult)))(),
                wait=[tsum], inc="v")
            t2 = em.on("vector", (lambda c=c: (lambda v: v.tensor_scalar(
                col(TMPB)[:, 0:1], col(SQ2)[:, c:c + 1], 1.0 / D, None, OP.mult)))(),
                wait=[tsq], inc="v")
            t3 = em.on("vector", (lambda c=c: (lambda v: v.tensor_mul(
                col(TMPB)[:, 1:2], col(MU2)[:, c:c + 1], col(MU2)[:, c:c + 1])))(),
                wait=[t1], inc="v")
            t4 = em.on("vector", lambda v: v.tensor_sub(
                col(TMPB)[:, 0:1], col(TMPB)[:, 0:1], col(TMPB)[:, 1:2]),
                wait=[t2, t3], inc="v")
            t5 = em.on("scalar", lambda s: s.activation(
                col(TMPB)[:, 1:2], col(TMPB)[:, 0:1], AF.Sqrt, bias=EPSC),
                wait=[t4, tok_epsc], inc="s")
            t6 = em.on("vector", (lambda c=c: (lambda v: v.reciprocal(
                col(R2C)[:, c:c + 1], col(TMPB)[:, 1:2])))(), wait=[t5], inc="v")
            t7 = em.on("vector", (lambda c=c: (lambda v: v.scalar_tensor_tensor(
                col(NM2)[:, c:c + 1], col(MU2)[:, c:c + 1], -1.0, col(R2C)[:, c:c + 1],
                OP.mult, OP.mult)))(), wait=[t6], inc="v")
            t8 = em.on("vector", (lambda c=c, yo=yo: (lambda v: v.tensor_scalar(
                YTL[:, yo * D:(yo + 1) * D], YTL[:, yo * D:(yo + 1) * D],
                col(R2C)[:, c:c + 1], col(NM2)[:, c:c + 1], OP.mult, OP.add)))(),
                wait=[t7, tsq], inc="v")
            t9 = em.on("vector", (lambda c=c, yo=yo: (lambda v: v.scalar_tensor_tensor(
                R1[:, c * D:(c + 1) * D], YTL[:, yo * D:(yo + 1) * D], 1.0,
                R1[:, c * D:(c + 1) * D], OP.mult, OP.add,
                accum_out=col(SUM3)[:, c:c + 1])))(), wait=[t8, tok_x], inc="v")
            rYTL.note_w(t9)
            rYTL.note_r(t9)
            x2_toks[c] = t9

        rS1b = Reg()
        sq3_toks = []
        for c in range(NT):
            tq = em.on("scalar", (lambda c=c: (lambda s: s.activation(
                S1[:, 1024:2048], R1[:, c * D:(c + 1) * D], AF.Square,
                accum_out=col(SQ3)[:, c:c + 1])))(), wait=[x2_toks[c]] + rS1b.w(), inc="s")
            rS1b.note_w(tq)
            sq3_toks.append(tq)
        t1 = em.on("vector", lambda v: v.tensor_scalar(col(MU3), col(SUM3), 1.0 / D, None, OP.mult),
                   wait=x2_toks, inc="v")
        t2 = em.on("vector", lambda v: v.tensor_scalar(col(TMPA), col(SQ3), 1.0 / D, None, OP.mult),
                   wait=sq3_toks, inc="v")
        t3 = em.on("vector", lambda v: v.tensor_mul(col(TMPB), col(MU3), col(MU3)), wait=[t1], inc="v")
        t4 = em.on("vector", lambda v: v.tensor_sub(col(TMPA), col(TMPA), col(TMPB)),
                   wait=[t2, t3], inc="v")
        t5 = em.on("vector", lambda v: v.tensor_scalar(col(TMPB), col(TMPA), EPS, None, OP.add),
                   wait=[t4], inc="v")
        t6a = em.on("vector", lambda v: v.reciprocal(col(TMPC), col(TMPB)), wait=[t5], inc="v")
        t6 = em.on("vector", lambda v: v.tensor_mul(col(TMPC), col(TMPA), col(TMPC)),
                   wait=[t6a], inc="v")
        t8 = em.on("vector", lambda v: v.tensor_scalar(col(TMPC), col(TMPC), EPS, None, OP.add),
                   wait=[t6], inc="v")
        t9 = em.on("vector", lambda v: v.tensor_mul(col(TMPA), col(TMPB), col(TMPC)),
                   wait=[t8], inc="v")
        t10 = em.on("scalar", lambda s: s.activation(col(TMPB), col(TMPA), AF.Sqrt),
                    wait=[t9], inc="s")
        t11 = em.on("vector", lambda v: v.reciprocal(col(RHO), col(TMPB)), wait=[t10], inc="v")
        t12 = em.on("vector", lambda v: v.scalar_tensor_tensor(
            col(NMR), col(MU3), -1.0, col(RHO), OP.mult, OP.mult), wait=[t11], inc="v")
        rho_tok = t12

        tok_b2b = em.on("gpsimd", lambda g: g.dma_start(out=B2B[:, :], in_=b2_b_d),
                        wait=x2_toks, inc="dG", amt=16)
        rZI32 = Reg()
        rXQN = [Reg(), Reg()]
        rS1h = [rS1, rS1b]
        xq_last = None
        for c in range(NT):
            hh = 0
            ho = 0
            rh = rS1h[hh]
            ta = em.on("vector", (lambda c=c, ho=ho: (lambda v: v.tensor_scalar(
                S1[:, ho:ho + D], R1[:, c * D:(c + 1) * D], col(RHO)[:, c:c + 1],
                col(NMR)[:, c:c + 1], OP.mult, OP.add)))(),
                wait=rh.w() + [rho_tok, x2_toks[c]], inc="v")
            rh.note_w(ta)
            tb = em.on("vector", (lambda ho=ho: (lambda v: v.scalar_tensor_tensor(
                S1[:, ho:ho + D], S1[:, ho:ho + D], 1.0, GMB[:, :], OP.mult, OP.mult)))(),
                wait=rh.w() + [CGMB], inc="v")
            rh.note_w(tb)
            tc = em.on("vector", (lambda ho=ho: (lambda v: v.tensor_add(
                S1[:, ho:ho + D], S1[:, ho:ho + D], BTB[:, :])))(),
                wait=rh.w() + [CBTB], inc="v")
            rh.note_w(tc)
            tdm = em.on("vector", (lambda c=c, ho=ho, hh=hh: (lambda v: v.reduce_max(
                col(TMPA)[:, hh:hh + 1], S1[:, ho:ho + D], axis=AX.X,
                apply_absolute_value=True)))(), wait=rh.r(), inc="v")
            rh.note_r(tdm)
            tde = em.on("vector", (lambda hh=hh: (lambda v: v.tensor_scalar(
                col(TMPA)[:, 2 + hh:3 + hh], col(TMPA)[:, hh:hh + 1], EPS, None,
                OP.max)))(), wait=[tdm], inc="v")
            tdf = em.on("vector", (lambda hh=hh: (lambda v: v.reciprocal(
                col(TMPB)[:, hh:hh + 1], col(TMPA)[:, 2 + hh:3 + hh])))(),
                wait=[tde], inc="v")
            tdg = em.on("vector", (lambda c=c, hh=hh: (lambda v: v.tensor_scalar(
                col(SXC)[:, c:c + 1], col(TMPB)[:, hh:hh + 1], 127.0, None, OP.mult)))(),
                wait=[tdf], inc="v")
            tdh = em.on("vector", (lambda c=c, hh=hh: (lambda v: v.scalar_tensor_tensor(
                col(ISX)[:, c:c + 1], col(TMPA)[:, 2 + hh:3 + hh], 1.0 / 127.0,
                col(ISW1)[:, 0:1], OP.mult, OP.mult)))(), wait=[tde, tok_isw1], inc="v")
            xo = c % 2
            wxq = rXQN[xo].w()
            tj = None
            for hf in range(2):
                ti = em.on("vector", (lambda c=c, hf=hf, ho=ho: (lambda v: v.tensor_scalar(
                    ZI32[:, :], S1[:, ho + hf * 512:ho + (hf + 1) * 512],
                    col(SXC)[:, c:c + 1], None, OP.mult)))(),
                    wait=rZI32.w() + rh.r() + [tdg], inc="v")
                rZI32.note_w(ti)
                rh.note_r(ti)
                tj = em.on("vector", (lambda xo=xo, hf=hf: (lambda v: v.tensor_copy(
                    XQN[:, xo * D + hf * 512: xo * D + (hf + 1) * 512], ZI32[:, :])))(),
                    wait=(wxq if hf == 0 else []) + rZI32.r(), inc="v")
                rZI32.note_r(tj)
            rXQN[xo] = Reg(); rXQN[xo].note_w(tj)
            for half in range(2):
                wb = btb.w()
                tts = []
                for kk in range(4):
                    k = half * 4 + kk
                    tt = em.on("tensor", (lambda k=k, kk=kk, xo=xo: (lambda t: t.transpose(
                        btb.ap[:, kk * 128:(kk + 1) * 128],
                        XQN[:, xo * D + k * 128: xo * D + (k + 1) * 128], IDB[:, :])))(),
                        wait=(wb if kk == 0 else []) + rXQN[xo].r() + [tok_idb] +
                             (tts[-1:] if tts else []), inc="t")
                    tts.append(tt)
                    btb.note_w(tt)
                    rXQN[xo].note_r(tt)
                rb = btb.r()
                evs = []
                for kk in range(4):
                    k = half * 4 + kk
                    te = em.on("vector", (lambda c=c, k=k, kk=kk: (lambda v: v.tensor_copy(
                        R2[:, k * TOK + c * 128: k * TOK + (c + 1) * 128],
                        btb.ap[:, kk * 128:(kk + 1) * 128])))(),
                        wait=rb + ([tok_xT] if c == 0 else []), inc="v")
                    btb.note_r(te)
                    evs.append(te)
                xq_last = evs[-1]
        tok_xq = xq_last

        do_tap("x2e", R1[:, 0:4096], 4096, x2_toks)
        do_tap("xqe", R2[:, 0:4096], 4096, [tok_xq])
        do_tap("coe", CO[:, 0:768], 768, x2_toks + [rho_tok])
        tok_sxw1b = bcast_from_col(col(ISX), rF1, F1, rF1.w() + [tdh, tok_xq, tzb, tztv])

        tmx0 = em.on("vector", lambda v: v.memset(F2[:, 0:TOK], 0.0),
                     wait=rF2.w() + rWMb[0].w() + rWMb[1].w(), inc="v")
        rF2.note_w(tmx0)
        rRWM = Reg()
        tmn0 = em.on("vector", lambda v: v.memset(RW[:, 0:TOK], 0.0),
                     wait=rROW0.w() + rL16[0].w() + rL16[1].w() + rB16.w() + rRW16.w() + [tok_sxw1b],
                     inc="v")
        rRWM.note_w(tmn0)
        w1_slice_r = W1T_d.rearrange("(k p) (h j) -> p k h j", p=128, j=128)
        hb_r = Hb.rearrange("(h p) t -> h p t", p=128)
        rW1S = [Reg(), Reg()]
        rW1Q = [Reg(), Reg()]
        rHSL = [Reg(), Reg()]
        tmx_last = tmx0
        w2_slice2 = W2T_d.rearrange("(h p) dd -> h p dd", p=128)
        rW2Sr = Reg()
        rW2Sr.note_w(rW2S_seed)
        w2q_toks = [None] * NH
        def w1_load(h):
            pp = h % 2
            td = em.on("sync", (lambda h=h, pp=pp: (lambda sy: sy.dma_start(
                out=W1S[:, pp * D:(pp + 1) * D].rearrange("p (k j) -> p k j", k=ND),
                in_=w1_slice_r[:, :, h, :])))(),
                wait=rW1S[pp].w() + ([wm1_toks[-1], x2_toks[-1], tok_sxw1b]
                                     if h <= 1 else []),
                inc="dW0" if pp == 0 else "dW1", amt=16)
            rW1S[pp] = Reg(); rW1S[pp].note_w(td)

        def w1_quant(h):
            pp = h % 2
            tq1 = em.on("vector", (lambda pp=pp: (lambda v: v.tensor_scalar(
                W1Q[:, pp * D:(pp + 1) * D], W1S[:, pp * D:(pp + 1) * D],
                col(SW1C)[:, 0:1], 0.5, OP.mult, OP.is_gt)))(),
                wait=rW1Q[pp].w() + rW1S[pp].r() + [tok_sw1], inc="v")
            rW1S[pp].note_r(tq1)
            tq2 = em.on("vector", (lambda pp=pp: (lambda g: g.tensor_scalar(
                W1S[:, pp * D:(pp + 1) * D], W1S[:, pp * D:(pp + 1) * D],
                col(SW1C)[:, 0:1], -0.5, OP.mult, OP.is_lt)))(),
                wait=rW1S[pp].w() + [tq1, tok_sw1], inc="v")
            rW1S[pp].note_w(tq2)
            tq3 = em.on("vector", (lambda pp=pp: (lambda v: v.tensor_sub(
                W1Q[:, pp * D:(pp + 1) * D], W1Q[:, pp * D:(pp + 1) * D],
                W1S[:, pp * D:(pp + 1) * D])))(), wait=[tq1, tq2], inc="v")
            rW1S[pp].note_r(tq3)
            rW1Q[pp] = Reg(); rW1Q[pp].note_w(tq3)

        w1_load(0)
        w1_load(1)
        w1_quant(0)
        for h in range(NH):
            pp = h % 2
            hp = h % 2
            whs = rHSL[hp].w()
            ev = []
            for pair in range(2):
                banks2 = [pb[2 * pair], pb[2 * pair + 1]]
                wbs = [b.w() for b in banks2]
                mms = [[], []]
                for k in range(ND):
                    for j in range(2):
                        nchk = 2 * pair + j
                        tm = em.on("tensor", (lambda k=k, nchk=nchk, bank=banks2[j], pp=pp: (lambda t: t.matmul(
                            bank.ap[:, 0:512], W1Q[:, pp * D + k * 128: pp * D + (k + 1) * 128],
                            R2[:, k * TOK + nchk * 512: k * TOK + (nchk + 1) * 512],
                            start=(k == 0), stop=(k == ND - 1))))(),
                            wait=(wbs[j] if k == 0 else []) + rW1Q[pp].r() + [tok_xq],
                            inc="t")
                        mms[j].append(tm)
                for j in range(2):
                    banks2[j].note_w(mms[j][-1])
                    rW1Q[pp].note_r(mms[j][-1])
            if h + 2 < NH:
                w1_load(h + 2)
            if h + 1 < NH:
                w1_quant(h + 1)
            for nchk in range(4):
                bank = pb[nchk]
                tv = em.on("vector", (lambda nchk=nchk, bank=bank: (lambda v: v.tensor_mul(
                    S1[:, nchk * 512:(nchk + 1) * 512], bank.ap[:, 0:512],
                    F1[:, nchk * 512:(nchk + 1) * 512])))(),
                    wait=bank.r() + (rS1.w() + rS1b.w() if nchk == 0 else []) +
                         [tok_sxw1b], inc="v")
                bank.note_r(tv)
                rF1.note_r(tv)
                rS1.note_w(tv)
                ts_ = em.on("scalar", (lambda h=h, nchk=nchk, hp=hp: (lambda s: s.activation(
                    HSL[:, hp * TOK + nchk * 512: hp * TOK + (nchk + 1) * 512],
                    S1[:, nchk * 512:(nchk + 1) * 512], AF.Gelu,
                    bias=B1T[:, h:h + 1])))(),
                    wait=rS1.r() + (whs if nchk == 0 else []) + [CB1T], inc="s")
                rS1.note_r(ts_)
                ev.append(ts_)
            rHSL[hp] = Reg(); rHSL[hp].note_w(ev[-1])
            for e in ev:
                rHSL[hp].note_w(e)
            tmx = em.on("vector", (lambda hp=hp: (lambda g: g.tensor_tensor(
                F2[:, 0:TOK], F2[:, 0:TOK], HSL[:, hp * TOK:(hp + 1) * TOK],
                OP.max)))(), wait=ev + rF2.w(), inc="v")
            rF2.note_w(tmx)
            rHSL[hp].note_r(tmx)
            tmn = em.on("vector", (lambda hp=hp: (lambda g: g.scalar_tensor_tensor(
                RW[:, 0:TOK], HSL[:, hp * TOK:(hp + 1) * TOK], -1.0, RW[:, 0:TOK],
                OP.mult, OP.max)))(), wait=ev + rRWM.w(), inc="v")
            rRWM.note_w(tmn)
            rHSL[hp].note_r(tmn)
            tmx_last = tmx
            tmn_last = tmn
            tdh_ = em.on("scalar", (lambda h=h, hp=hp: (lambda s: s.dma_start(
                out=hb_r[h], in_=HSL[:, hp * TOK:(hp + 1) * TOK])))(),
                wait=ev, inc="dA", amt=16)
            rHSL[hp].note_r(tdh_)
            td2 = em.on("sync", (lambda h=h: (lambda sy: sy.dma_start(
                out=W2S[:, 0:D], in_=w2_slice2[h])))(),
                wait=rW2Sr.w() + ([wm2_toks[-1], x2_toks[-1], tok_xq]
                                  if h == 0 else []), inc="dX", amt=16)
            rW2Sr.note_w(td2)
            wq1 = em.on("vector", (lambda h=h: (lambda g: g.tensor_scalar(
                W2Q[:, h * D:(h + 1) * D], W2S[:, 0:D],
                col(SW2C)[:, 0:1], 0.5, OP.mult, OP.is_gt)))(),
                wait=rW2Sr.r() + [tok_sw2] + ([x2_toks[-1], tok_xq] if h == 0 else []),
                inc="v")
            rW2Sr.note_r(wq1)
            wq2 = em.on("vector", (lambda h=h: (lambda g: g.tensor_scalar(
                W2S[:, 0:D], W2S[:, 0:D],
                col(SW2C)[:, 0:1], -0.5, OP.mult, OP.is_lt)))(),
                wait=rW2Sr.w(), inc="v")
            rW2Sr.note_w(wq2)
            wq3 = em.on("vector", (lambda h=h: (lambda g: g.tensor_sub(
                W2Q[:, h * D:(h + 1) * D], W2Q[:, h * D:(h + 1) * D],
                W2S[:, 0:D])))(), wait=[wq1, wq2], inc="v")
            rW2Sr.note_r(wq3)
            w2q_toks[h] = wq3

        hb_done = ("dA", em.cnt["dA"])
        pass1_t = ("t", em.cnt["t"])

        s2r = []
        for c in range(NT):
            wb = btf.w()
            tt = em.on("tensor", (lambda c=c: (lambda t: t.transpose(
                btf.ap[:, 0:128], F2[:, c * 128:(c + 1) * 128], IDF[:, :])))(),
                wait=wb + [tmx_last, CIDF], inc="t")
            tt2 = em.on("tensor", (lambda c=c: (lambda t: t.transpose(
                btf.ap[:, 128:256], RW[:, c * 128:(c + 1) * 128], IDF[:, :])))(),
                wait=[tmn_last], inc="t")
            btf.note_w(tt); btf.note_w(tt2)
            rF2.note_r(tt)
            rRWM.note_r(tt2)
            rb = btf.r()
            tr = em.on("vector", (lambda c=c: (lambda v: v.reduce_max(
                col(TMPA)[:, c:c + 1], btf.ap[:, 0:128], axis=AX.X)))(),
                wait=rb, inc="v")
            tr2 = em.on("vector", (lambda c=c: (lambda v: v.reduce_max(
                col(TMPB)[:, c:c + 1], btf.ap[:, 128:256], axis=AX.X)))(),
                wait=rb, inc="v")
            btf.note_r(tr); btf.note_r(tr2)
            s2r += [tr, tr2]
        tmrg = em.on("vector", lambda v: v.tensor_max(col(TMPA), col(TMPA), col(TMPB)),
                     wait=s2r, inc="v")
        t1 = em.on("vector", lambda v: v.tensor_scalar(col(TMPB), col(TMPA), EPS, None, OP.max),
                   wait=[tmrg], inc="v")
        t2 = em.on("vector", lambda v: v.reciprocal(col(TMPC), col(TMPB)), wait=[t1], inc="v")
        tok_s2 = em.on("vector", lambda v: v.tensor_scalar(col(S2C), col(TMPC), 127.0, None, OP.mult),
                       wait=[t2], inc="v")
        tok_is2 = em.on("vector", lambda v: v.tensor_scalar(
            col(IS2), col(TMPB), col(ISW2)[:, 0:1], 1.0 / 127.0, OP.mult, OP.mult),
            wait=[t1, tok_isw2], inc="v")
        tok_s2b = bcast_from_col(col(S2C), rF2, F2, rF2.w() + rRWM.w() + [tok_s2])

        hb_rd = Hb.rearrange("(h p) (g t) -> p h g t", p=128, t=512)
        out_r = out_d.rearrange("(c p) d -> c p d", p=128)
        rR2 = Reg()
        rR2.note_w(tok_xq)
        rSQS = Reg()
        rZH = Reg()
        rOUT2 = [Reg() for _ in range(4)]
        rOSTG = Reg()
        out_toks = []
        prev_last_rd = None
        for g_ in range(4):
            sub_toks = []
            for sj in range(4):
                if prev_last_rd is None:
                    wsub = rR2.w() + [hb_done, pass1_t]
                else:
                    wsub = [prev_last_rd[sj * 8 + 7]]
                tds = em.on("sync", (lambda g_=g_, sj=sj: (lambda sy: sy.dma_start(
                    out=R2[:, sj * 8 * 512:(sj + 1) * 8 * 512].rearrange(
                        "p (h t) -> p h t", h=8),
                    in_=hb_rd[:, sj * 8:(sj + 1) * 8, g_, :])))(),
                    wait=wsub, inc="dR%d" % sj, amt=16)
                sub_toks.append(tds)
                rR2.note_w(tds)
            q_toks = []
            for h in range(NH):
                tq1 = em.on("vector", (lambda h=h, g_=g_: (lambda v: v.tensor_tensor(
                    ZH32[:, :], R2[:, h * 512:(h + 1) * 512],
                    F2[:, g_ * 512:(g_ + 1) * 512], OP.mult)))(),
                    wait=rZH.w() + [sub_toks[h // 8], tok_s2b], inc="v")
                rR2.note_r(tq1)
                rF2.note_r(tq1)
                rZH.note_w(tq1)
                tq2 = em.on("vector", (lambda h=h: (lambda v: v.tensor_copy(
                    R2[:, h * 512:(h + 1) * 512], ZH32[:, :])))(),
                    wait=rZH.r() + [rR2.lw], inc="v")
                rZH.note_r(tq2)
                rR2.note_w(tq2)
                q_toks.append(tq2)
            hq_tok = q_toks[-1]
            for tl in range(4):
                c = g_ * 4 + tl
                ev2 = []
                banks2 = [pb[(tl * 2) % 4], pb[(tl * 2 + 1) % 4]]
                wbs = [b.w() for b in banks2]
                mms = [[], []]
                for h in range(NH):
                    for dch in range(2):
                        tm = em.on("tensor", (lambda h=h, tl=tl, dch=dch, bank=banks2[dch]: (lambda t: t.matmul(
                            bank.ap[:, 0:512], R2[:, h * 512 + tl * 128: h * 512 + (tl + 1) * 128],
                            W2Q[:, h * D + dch * 512: h * D + (dch + 1) * 512],
                            start=(h == 0), stop=(h == NH - 1))))(),
                            wait=(wbs[dch] if h == 0 else []) + [q_toks[h], w2q_toks[h]],
                            inc="t")
                        mms[dch].append(tm)
                for dch in range(2):
                    banks2[dch].note_w(mms[dch][-1])
                    rR2.note_r(mms[dch][-1])
                if tl == 3:
                    prev_last_rd = mms[1]
                for dch in range(2):
                    bank = banks2[dch]
                    tv = em.on("vector", (lambda c=c, tl=tl, dch=dch, bank=bank: (lambda v: v.scalar_tensor_tensor(
                        OUT2[:, tl * D + dch * 512: tl * D + (dch + 1) * 512],
                        bank.ap[:, 0:512], col(IS2)[:, c:c + 1],
                        B2B[:, dch * 512:(dch + 1) * 512], OP.mult, OP.add,
                        accum_out=col(TMPA)[:, dch:dch + 1])))(),
                        wait=bank.r() + (rOUT2[tl].w() if dch == 0 else []) +
                             [tok_is2, tok_b2b], inc="v")
                    bank.note_r(tv)
                    ev2.append(tv)
                rOUT2[tl] = Reg(); rOUT2[tl].note_w(ev2[-1])
                for e in ev2:
                    rOUT2[tl].note_w(e)
                tsum = em.on("vector", (lambda c=c: (lambda v: v.tensor_add(
                    col(SUM5)[:, c:c + 1], col(TMPA)[:, 0:1], col(TMPA)[:, 1:2])))(),
                    wait=ev2, inc="v")
                tsq = em.on("scalar", (lambda c=c, tl=tl: (lambda s: s.activation(
                    SQS[:, :], OUT2[:, tl * D:(tl + 1) * D], AF.Square,
                    accum_out=col(SQ5)[:, c:c + 1])))(), wait=ev2 + rSQS.w(), inc="s")
                rSQS.note_w(tsq)
                rOUT2[tl].note_r(tsq)
                t1 = em.on("vector", (lambda c=c: (lambda v: v.tensor_scalar(
                    col(MU5)[:, c:c + 1], col(SUM5)[:, c:c + 1], 1.0 / D, None, OP.mult)))(),
                    wait=[tsum], inc="v")
                t2 = em.on("vector", (lambda c=c: (lambda v: v.tensor_scalar(
                    col(TMPB)[:, 0:1], col(SQ5)[:, c:c + 1], 1.0 / D, None, OP.mult)))(),
                    wait=[tsq], inc="v")
                t3 = em.on("vector", (lambda c=c: (lambda v: v.tensor_mul(
                    col(TMPB)[:, 1:2], col(MU5)[:, c:c + 1], col(MU5)[:, c:c + 1])))(),
                    wait=[t1], inc="v")
                t4 = em.on("vector", lambda v: v.tensor_sub(
                    col(TMPB)[:, 0:1], col(TMPB)[:, 0:1], col(TMPB)[:, 1:2]),
                    wait=[t2, t3], inc="v")
                t5 = em.on("scalar", lambda s: s.activation(
                    col(TMPB)[:, 1:2], col(TMPB)[:, 0:1], AF.Sqrt, bias=EPSC),
                    wait=[t4], inc="s")
                t6 = em.on("vector", (lambda c=c: (lambda v: v.reciprocal(
                    col(R5C)[:, c:c + 1], col(TMPB)[:, 1:2])))(), wait=[t5], inc="v")
                t7 = em.on("vector", (lambda c=c: (lambda v: v.scalar_tensor_tensor(
                    col(NM5)[:, c:c + 1], col(MU5)[:, c:c + 1], -1.0, col(R5C)[:, c:c + 1],
                    OP.mult, OP.mult)))(), wait=[t6], inc="v")
                t8 = em.on("vector", (lambda c=c, tl=tl: (lambda v: v.tensor_scalar(
                    OUT2[:, tl * D:(tl + 1) * D], OUT2[:, tl * D:(tl + 1) * D],
                    col(R5C)[:, c:c + 1], col(NM5)[:, c:c + 1], OP.mult, OP.add)))(),
                    wait=[t7, tsq], inc="v")
                rOUT2[tl].note_w(t8)
                t9 = em.on("vector", (lambda c=c, tl=tl: (lambda v: v.scalar_tensor_tensor(
                    OSTG[:, 0:D], OUT2[:, tl * D:(tl + 1) * D], 1.0,
                    R1[:, c * D:(c + 1) * D], OP.mult, OP.add)))(),
                    wait=rOSTG.w() + [t8, w2q_toks[-1]], inc="v")
                rOUT2[tl].note_r(t9)
                rOSTG.note_w(t9)
                to = em.on("sync", (lambda c=c: (lambda sy: sy.dma_start(
                    out=out_r[c], in_=OSTG[:, 0:D])))(),
                    wait=rOSTG.r(), inc="out", amt=16)
                rOSTG.note_r(to)
                out_toks.append(to)

        tap_points = {
            "x2": (R1[:, 0:4096], 4096),
            "qT": (qT[:, 0:2048], 2048),
            "kT": (kT[:, 0:2048], 2048),
            "vT": (vT[:, 0:2048], 2048),
            "gT": (gT[:, 0:2048], 2048),
            "lj": (S1[:, 0:2048], 2048),
            "gtb": (GTB[:, 0:2048], 2048),
            "hcb": (HCB[:, 0:2048], 2048),
            "ucb": (UCB[:, 0:4096], 4096),
            "zin": (ZBt[:, 0:128], 128),
            "xq": (R2[:, 0:4096], 4096),
            "w2q": (W2Q[:, 0:4096], 4096),
            "co": (CO[:, 0:768], 768),
            "ytl": (YTL[:, 0:2048], 2048),
            "f2": (F2[:, 0:2048], 2048),
            "gup": (GUP[:, 0:2048], 2048),
            "qtt": (QTT[:, 0:2048], 2048),
        }
        if tap is not None and tap in tap_points:
            region, width = tap_points[tap]
        else:
            region, width = CO[:, 0:768], 768
        em.on("gpsimd", (lambda region=region, width=width: (lambda g: g.dma_start(
            out=dbg_d[:, 0:width], in_=region)))(),
            wait=em.barrier(), inc="dG", amt=16)
        em.on("sync", None, wait=[("dG", em.cnt["dG"])], inc=None)
        em.on("sync", None, wait=[("out", em.cnt["out"])], inc=None)

        block = es.enter_context(nc.Block())

        @block.sync
        def _(sy):
            em.run("sync", sy)

        @block.gpsimd
        def _(g):
            em.run("gpsimd", g)

        @block.vector
        def _(v):
            em.run("vector", v)

        @block.scalar
        def _(s):
            em.run("scalar", s)

        @block.tensor
        def _(t):
            em.run("tensor", t)

    return nc


_NC = None
TAP = None
DBG_OUT = {}


def kernel(**inputs):
    global _NC
    f32 = lambda a: np.ascontiguousarray(np.asarray(a, dtype=np.float32))
    x = f32(inputs["L0_attn_x"])
    Wq = f32(inputs["L0_attn_Wq"]); bq = f32(inputs["L0_attn_bq"])
    Wk = f32(inputs["L0_attn_Wk"]); bk = f32(inputs["L0_attn_bk"])
    Wv = f32(inputs["L0_attn_Wv"]); bv = f32(inputs["L0_attn_bv"])
    Wa = f32(inputs["L0_attn_Wa"]); ba = f32(inputs["L0_attn_ba"])
    Wb = f32(inputs["L0_attn_Wb"]); bb = f32(inputs["L0_attn_bb"])
    Wg = f32(inputs["L0_attn_Wg"]); bg = f32(inputs["L0_attn_bg"])
    Wo = f32(inputs["L0_attn_Wo"]); bo = f32(inputs["L0_attn_bo"])
    w1 = f32(inputs["L0_ffn_w1"]); b1 = f32(inputs["L0_ffn_b1"])
    w2 = f32(inputs["L0_ffn_w2"]); b2 = f32(inputs["L0_ffn_b2"])
    gamma = f32(inputs["L0_ffn_gamma"]); beta = f32(inputs["L0_ffn_beta"])

    Wab = np.ascontiguousarray(np.concatenate([Wa, Wb], axis=1))
    bab = np.ascontiguousarray(np.stack([ba[0], bb[0]])[:, None])
    W1T = np.ascontiguousarray(w1.T)
    W2T = np.ascontiguousarray(w2.T)
    bcolf = lambda a: np.ascontiguousarray(a[:, None])
    brd = lambda a: np.ascontiguousarray(np.broadcast_to(a[None, :], (128, a.shape[0])))
    b1T = np.ascontiguousarray(b1.reshape(NH, 128).T)
    idf = np.eye(128, dtype=np.float32)
    idb = np.eye(128, dtype=ml_dtypes.bfloat16)
    ii = np.arange(128)
    maskW = np.where(ii[:, None] > ii[None, :], 0.0, POS).astype(np.float32)
    maskU = np.where(ii[None, :] >= ii[:, None], 0.0, NEG).astype(np.float32)

    global _NC
    if _NC is None:
        _NC = build_nc(TAP)
    nc = _NC

    in_maps = []
    for core in range(8):
        b, s = core // 4, core % 4
        xs = np.ascontiguousarray(x[b, s * TOK:(s + 1) * TOK, :])
        xTs = np.ascontiguousarray(xs.T)
        sm = np.zeros((128, 6), np.float32)
        for p in range(3):
            m = 1.0 if p < s else 0.0
            sm[:, p] = m
            sm[:, 3 + p] = 1.0 - m
        in_maps.append(dict(
            x=xs, xT=xTs, Wq=Wq, Wk=Wk, Wv=Wv, Wg=Wg, Wab=Wab, Wo=Wo,
            bq=bcolf(bq), bk=bcolf(bk), bv=bcolf(bv), bg=bcolf(bg), bab=bab,
            bo_b=brd(bo), b2_b=brd(b2), gam_b=brd(gamma), bet_b=brd(beta),
            b1T=b1T, W1T=W1T, W2T=W2T, idf=idf, idb=idb,
            maskW=maskW, maskU=maskU, smask=sm,
        ))
    res = run_bass_kernel_spmd(nc, in_maps, core_ids=list(range(8)))
    out = np.empty_like(x)
    for core in range(8):
        b, s = core // 4, core % 4
        out[b, s * TOK:(s + 1) * TOK, :] = res.results[core]["out"]
    DBG_OUT.clear()
    for core in range(8):
        DBG_OUT[core] = np.asarray(res.results[core]["dbg"])
    return out
